# version 11
# baseline (speedup 1.0000x reference)
from contextlib import ExitStack

import numpy as np
import ml_dtypes

import concourse.bass as bass
import concourse.mybir as mybir
from concourse.bass_utils import run_bass_kernel_spmd

F32 = mybir.dt.float32
BF16 = mybir.dt.bfloat16
AF = mybir.ActivationFunctionType
ALU = mybir.AluOpType
AX = mybir.AxisListType

N_CORES = 8


_DED = {"pv_g", "vh_ev", "s_ev", "up_ev", "dn_ev", "g_h", "h_lo", "h2_rd",
        "d_h", "d_hB", "d_wto", "d_wtoB", "d_ws", "d_wsB", "d_wt", "d_wtB",
        "fin_o", "qn_rx", "kn_rx", "g_s", "fin_m", "con_g",
        "ws_abs", "ws_add", "wt_abs", "wt_g", "wt_m", "w_tcp"}

_GROUPS = {
    "g_te": ["tr_mm", "ln_sx", "ln_x2mm", "bc_m", "bc_r", "q_mm", "k_mm",
             "qn_mm", "kn_mm", "c_bc", "sc_mm", "vh_mm", "pv_mm", "s_mm",
             "up_mm", "dn_mm", "rfb_m", "rhb_m", "w_smm", "w_tbc",
             "nb_a", "nb_b"],
    "g_sc": ["al_e", "al_a", "al_h", "al_lb", "ab_h", "ab_lb", "mb_hi", "jr_h", "tr_cp",
             "q_h", "k_h", "c_s", "c_cp", "e2d", "exp",
             "nr_a", "nr_b", "wo_h", "wv_h", "c_h", "h_hi", "h_sq",
             "rfb_c", "rhb_c", "ws_abs", "wt_abs", "w_tcp", "ln_sq",
             "wq_h", "wk_h", "tr_1", "tr_2",
             "r_m2", "r_sq", "r_w", "xn_hic", "b_v", "b_o", "bqs", "bks",
             "rh_2"],
    "g_ve": ["con_v", "con_r", "al_n", "al_h2", "al_r", "al_l2", "mb_f",
             "mb_lo", "ab_z", "ab_h2", "ab_l", "ab_l2", "r6n", "r6z", "jr_1", "jr_l",
             "tr_3", "b_vv", "b_ov", "bqsv", "bksv",
             "r_m", "r_e2", "r_var", "r_tz", "r_rs", "r_t", "r_u", "r_u1", "r_v8",
             "r_rf", "xn_t1", "xn_f", "xn_lo", "wq_l", "wk_l", "q_l",
             "k_l", "q_sq", "k_sq", "r6a", "qn_m", "kn_m", "c_m", "c_n", "nc_a", "nc_b", "at_a",
             "at_b", "pt_fix", "vh_1", "wv_sp", "wo_sp", "c_sp", "wt_g",
             "wt_m", "wsz", "ws_add", "w_th", "rh_1", "rh_3",
             "rh_4", "fin_a"],
    "g_gp": ["d_l6", "d_r6", "d_at", "s_w", "rh_d", "r_rsd",
             "r_rfd", "ab_d"],
    "g_sy": ["d_x", "d_al", "d_al2", "d_bv", "d_bo", "d_wv", "d_wq", "d_wqs", "d_wk",
             "d_wo", "d_af", "d_wtu", "d_wtd", "d_ha", "d_xs"],
}
_TAG2SEM = {t: g for g, ts in _GROUPS.items() for t in ts}


class Ctx:

    def __init__(self, nc):
        self.nc = nc
        self.sems = {}
        self.cnt = {}

    def phys(self, name):
        if name in _DED:
            return name
        return _TAG2SEM[name]

    def S(self, name):
        ph = self.phys(name)
        if ph not in self.sems:
            self.sems[ph] = self.nc.alloc_semaphore(ph)
            self.cnt[ph] = 0
        return self.sems[ph]

    def inc(self, instr, name, n=1):
        if name.endswith("0"):
            return instr
        instr.then_inc(self.S(name), n)
        self.cnt[self.phys(name)] += n
        return instr

    def dinc(self, instr, name):
        self.inc(instr, name, 16)
        eng = self.engines[instr.ins.engine]
        eng.wait_ge(self.S(name), self.cnt[self.phys(name)])
        return instr

    def wait(self, eng, name, val=None):
        ph = self.phys(name)
        v = self.cnt.get(ph, 0) if val is None else val
        if v > 0:
            eng.wait_ge(self.S(name), v)

    def val(self, name):
        return self.cnt.get(self.phys(name), 0)


def build_program(T=2048, D=1024, H=16, DFF=4096, stop_after=None):
    HD = 64
    NP = H // 2
    DC = D // 128
    KC = T // 128
    QB = T // 512
    HC = DFF // 128
    FC = DC
    TB = QB
    QCOL = T // 512
    KCG = KC // 2

    nc = bass.Bass("TRN2", target_bir_lowering=False)
    c = Ctx(nc)
    sy, te, ve, sc, gp = nc.sync, nc.tensor, nc.vector, nc.scalar, nc.gpsimd
    c.engines = {e.engine: e for e in (sy, te, ve, sc, gp)}

    x_in = nc.dram_tensor("x_in", [128, DC * T], F32, kind="ExternalInput").ap()
    wq_in = nc.dram_tensor("wq_in", [NP, 128, DC * 128], F32, kind="ExternalInput").ap()
    wk_in = nc.dram_tensor("wk_in", [NP, 128, DC * 128], F32, kind="ExternalInput").ap()
    wv_in = nc.dram_tensor("wv_in", [128, DC * D], F32, kind="ExternalInput").ap()
    bq_in = nc.dram_tensor("bq_in", [NP, 1, 128], F32, kind="ExternalInput").ap()
    bk_in = nc.dram_tensor("bk_in", [NP, 1, 128], F32, kind="ExternalInput").ap()
    bv_in = nc.dram_tensor("bv_in", [1, D], F32, kind="ExternalInput").ap()
    wo_in = nc.dram_tensor("wo_in", [128, DC * FC * 128], F32, kind="ExternalInput").ap()
    bo_in = nc.dram_tensor("bo_in", [1, D], F32, kind="ExternalInput").ap()
    wup_in = nc.dram_tensor("wup_in", [128, DC * HC * 128], F32, kind="ExternalInput").ap()
    wdn_in = nc.dram_tensor("wdn_in", [128, HC * FC * 128], F32, kind="ExternalInput").ap()
    alpha_in = nc.dram_tensor("alpha_in", [H, 1], F32, kind="ExternalInput").ap()

    attn_scr = nc.dram_tensor("attn_scr", [H, QB, 64, 512], F32, kind="Internal").ap()
    s_scr = nc.dram_tensor("s_scr", [FC, TB, 128, 512], F32, kind="Internal").ap()
    h_scr_hi = nc.dram_tensor("h_scr_hi", [TB, 128, HC * 512], BF16, kind="Internal").ap()
    h_scr_lo = nc.dram_tensor("h_scr_lo", [TB, 128, HC * 512], BF16, kind="Internal").ap()
    wup_t_scr = nc.dram_tensor("wup_t_scr", [128, DC * HC * 128], BF16, kind="Internal").ap()
    wdn_t_scr = nc.dram_tensor("wdn_t_scr", [128, HC * FC * 128], BF16, kind="Internal").ap()

    out_hbm = nc.dram_tensor("out_hbm", [128, FC * T], F32, kind="ExternalOutput").ap()

    ps = nc.alloc_psum_tensor("ps", [128, 4096], F32).ap()

    ones_row = nc.alloc_sbuf_tensor("ones_row", [128, 128], F32).ap()
    ones_bf = nc.alloc_sbuf_tensor("ones_bf", [128, 512], BF16).ap()
    lhsT6 = nc.alloc_sbuf_tensor("lhsT6", [128, T], BF16).ap()
    Mb = nc.alloc_sbuf_tensor("Mb", [128, T], BF16).ap()
    R_relu = nc.alloc_sbuf_tensor("R_relu", [128, 128], F32).ap()
    alpha_c = nc.alloc_sbuf_tensor("alpha_c", [16, 8], F32).ap()
    tiny_bf = nc.alloc_sbuf_tensor("tiny_bf", [16, 8], BF16).ap()
    arow = nc.alloc_sbuf_tensor("arow", [128, 96], F32).ap()
    a4 = nc.alloc_sbuf_tensor("a4", [128, 16], F32).ap()
    neg2a = nc.alloc_sbuf_tensor("neg2a", [128, 16], F32).ap()
    rowbuf = nc.alloc_sbuf_tensor("rowbuf", [128, T], F32).ap()
    thr = nc.alloc_sbuf_tensor("thr", [128, 2], F32).ap()
    smalls = nc.alloc_sbuf_tensor("smalls", [128, 16], F32).ap()
    negc = nc.alloc_sbuf_tensor("negc", [128, 1], F32).ap()
    bq_bf = nc.alloc_sbuf_tensor("bq_bf", [128, 256], BF16).ap()
    bk_bf = nc.alloc_sbuf_tensor("bk_bf", [128, 256], BF16).ap()
    e2diag = nc.alloc_sbuf_tensor("e2diag", [128, 256], F32).ap()

    def salloc(stack, name, shape, dtype):
        return stack.enter_context(nc.sbuf_tensor(name, list(shape), dtype)).ap()


    def _early_finish():
        c.wait(gp, "sc_mm", c.val("sc_mm"))
        c.dinc(gp.dma_start(out=out_hbm[0:1, 0:4], in_=ones_row[0:1, 0:4]), "fin_o")
        c.wait(gp, "fin_o")
    xn_stack = ExitStack()
    xn_hi = salloc(xn_stack, "xn_hi", [128, DC * T], BF16)
    xn_lo = salloc(xn_stack, "xn_lo", [128, DC * T], BF16)

    def split_row(src_f32_row0, hi_row0, lo_row0, tag):
        c.inc(sc.activation(hi_row0, src_f32_row0, AF.Copy), tag)
        c.wait(ve, tag)
        c.inc(ve.tensor_sub(lo_row0, src_f32_row0, hi_row0), tag + "v")

    c.inc(ve.memset(ones_row, 1.0), "con_v")
    c.inc(ve.memset(ones_bf, 1.0), "con_v")
    c.inc(gp.iota(R_relu, pattern=[[-1, 128]], base=0, channel_multiplier=1,
                  allow_small_or_imprecise_dtypes=True), "con_g")
    c.wait(ve, "con_g", 1)
    c.inc(ve.tensor_scalar_max(R_relu, R_relu, 0.0), "con_r")

    phA = ExitStack()
    x_res = salloc(phA, "x_res", [128, DC * T], F32)
    scrA = salloc(phA, "scrA", [128, T], F32)
    scrA2 = salloc(phA, "scrA2", [128, T], F32)
    iota16 = salloc(phA, "iota16", [16, T], F32)
    jrows = salloc(phA, "jrows", [128, T], BF16)
    jrows2 = salloc(phA, "jrows2", [128, T], BF16)
    jrows3 = salloc(phA, "jrows3", [128, T], BF16)

    c.dinc(sy.dma_start(out=x_res, in_=x_in), "d_x")
    c.dinc(sy.dma_start(out=alpha_c[0:H, 0:1], in_=alpha_in[0:H, 0:1]), "d_al")

    c.inc(gp.iota(iota16, pattern=[[1, T]], base=0, channel_multiplier=0,
                  allow_small_or_imprecise_dtypes=True), "con_g")

    c.wait(sc, "d_al")
    c.inc(sc.activation(alpha_c[0:H, 1:2], alpha_c[0:H, 0:1], AF.Exp), "al_e")
    c.wait(sc, "al_e")
    c.inc(sc.activation(alpha_c[0:H, 2:3], alpha_c[0:H, 1:2], AF.Ln, bias=1.0), "al_a")
    c.wait(ve, "al_a")
    c.inc(ve.tensor_scalar_mul(alpha_c[0:H, 3:4], alpha_c[0:H, 2:3], -1.0), "al_n")
    c.wait(sc, "al_n")
    c.inc(sc.activation(tiny_bf[0:H, 0:1], alpha_c[0:H, 2:3], AF.Copy), "al_h")
    c.wait(ve, "al_h")
    c.inc(ve.tensor_copy(alpha_c[0:H, 4:5], tiny_bf[0:H, 0:1]), "al_h2")
    c.wait(ve, "al_h2")
    c.inc(ve.tensor_sub(alpha_c[0:H, 5:6], alpha_c[0:H, 2:3], alpha_c[0:H, 4:5]), "al_r")
    c.wait(sc, "al_r")
    c.inc(sc.activation(tiny_bf[0:H, 1:2], alpha_c[0:H, 5:6], AF.Copy), "al_lb")
    c.wait(ve, "al_lb")
    c.inc(ve.tensor_copy(alpha_c[0:H, 6:7], tiny_bf[0:H, 1:2]), "al_l2")

    c.wait(ve, "con_g")
    c.wait(ve, "al_n")
    c.inc(ve.tensor_scalar_mul(scrA[0:H, 0:T], iota16[0:H, :], alpha_c[0:H, 3:4]), "mb_f")
    c.wait(sc, "mb_f")
    c.inc(sc.activation(Mb[0:H, :], scrA[0:H, 0:T], AF.Copy), "mb_hi")
    c.wait(ve, "mb_hi")
    c.inc(ve.tensor_sub(Mb[32:32 + H, :], scrA[0:H, 0:T], Mb[0:H, :]), "mb_lo")
    c.dinc(sy.dma_start(out=arow[0:1, 0:H],
                        in_=alpha_in.rearrange("h one -> one h")), "d_al2")
    c.wait(sc, "d_al2")
    c.inc(sc.activation(arow[0:1, 16:16 + H], arow[0:1, 0:H], AF.Exp), "tr_1")
    c.wait(sc, "tr_1")
    c.inc(sc.activation(arow[0:1, 32:32 + H], arow[0:1, 16:16 + H], AF.Ln, bias=1.0), "tr_2")
    c.wait(ve, "tr_2")
    c.inc(ve.tensor_scalar_mul(arow[0:1, 48:48 + H], arow[0:1, 32:32 + H], -2.0), "tr_3")
    c.wait(te, "tr_3")
    c.wait(te, "con_v")
    c.inc(te.matmul(ps[0:128, 0:H], ones_row[0:1, 0:128], arow[0:1, 48:48 + H],
                    start=True, stop=True), "tr_mm")
    c.wait(sc, "tr_mm")
    c.inc(sc.activation(neg2a[:, 0:H], ps[0:128, 0:H], AF.Copy), "tr_cp")


    c.inc(ve.memset(arow[0:1, 64:96], 0.0), "ab_z")
    c.wait(sc, "tr_2")
    c.wait(sc, "ab_z")
    c.inc(sc.activation(jrows[0:1, 0:H], arow[0:1, 32:32 + H], AF.Copy), "ab_h")
    c.wait(ve, "ab_h")
    c.inc(ve.tensor_copy(arow[0:1, 64:64 + H], jrows[0:1, 0:H]), "ab_h2")
    c.wait(ve, "ab_h2")
    c.inc(ve.tensor_sub(arow[0:1, 16:16 + H], arow[0:1, 32:32 + H], arow[0:1, 64:64 + H]), "ab_l")
    c.wait(sc, "ab_l")
    c.inc(sc.activation(jrows[0:1, 0:H], arow[0:1, 16:16 + H], AF.Copy), "ab_lb")
    c.wait(ve, "ab_lb")
    c.inc(ve.tensor_copy(arow[0:1, 80:80 + H], jrows[0:1, 0:H]), "ab_l2")
    c.wait(gp, "ab_l2")
    for r, src0 in ((0, 64), (1, 64), (2, 80), (3, 80)):
        c.dinc(gp.dma_start(out=a4[r:r + 1, 0:16], in_=arow[0:1, src0:src0 + 16]), "ab_d")
    c.dinc(gp.dma_start(out=a4[64:68, 0:16], in_=a4[0:4, 0:16]), "ab_d")

    c.wait(ve, "ab_l2")
    c.inc(ve.memset(jrows[0:1, :], 1.0), "jr_1")
    c.wait(sc, "con_g")
    c.wait(sc, "jr_1")
    c.inc(sc.activation(jrows2[0:1, :], iota16[0:1, :], AF.Copy), "jr_h")
    c.wait(ve, "jr_h")
    c.inc(ve.tensor_sub(jrows3[0:1, :], iota16[0:1, :], jrows2[0:1, :]), "jr_l")
    c.wait(gp, "jr_1")
    c.wait(gp, "jr_l")
    for r in (4, 5, 12, 13):
        c.dinc(gp.dma_start(out=lhsT6[r:r + 1, :], in_=jrows[0:1, :]), "d_l6")
    for r in (0, 2, 8, 10):
        c.dinc(gp.dma_start(out=lhsT6[r:r + 1, :], in_=jrows2[0:1, :]), "d_l6")
    for r in (1, 3, 9, 11):
        c.dinc(gp.dma_start(out=lhsT6[r:r + 1, :], in_=jrows3[0:1, :]), "d_l6")
    c.dinc(gp.dma_start(out=lhsT6[64:70, :], in_=lhsT6[8:14, :]), "d_l6")

    c.wait(te, "d_x")
    c.wait(te, "tr_cp")
    for dc in range(DC):
        for j in range(QCOL):
            c.inc(te.matmul(ps[0:1, j * 512:(j + 1) * 512], ones_row[:, 0:1],
                            x_res[:, dc * T + j * 512: dc * T + (j + 1) * 512],
                            start=(dc == 0), stop=(dc == DC - 1)),
                  "ln_sx" if dc == DC - 1 else "ln_mm0")
    for dc in range(DC):
        buf = scrA if dc % 2 == 0 else scrA2
        c.wait(sc, "d_x")
        c.wait(sc, "mb_lo")
        if dc >= 2:
            c.wait(sc, "ln_x2mm")
        c.inc(sc.activation(buf[:, 0:T], x_res[:, dc * T:(dc + 1) * T], AF.Square), "ln_sq")
        c.wait(te, "ln_sq")
        for j in range(QCOL):
            c.inc(te.matmul(ps[0:1, 2048 + j * 512: 2048 + (j + 1) * 512], ones_row[:, 0:1],
                            buf[:, j * 512:(j + 1) * 512],
                            start=(dc == 0), stop=(dc == DC - 1)), "ln_x2mm")
    c.wait(ve, "ln_sx")
    c.wait(ve, "ln_x2mm")
    c.inc(ve.tensor_scalar_mul(rowbuf[0:1, 0:T], ps[0:1, 0:T], 1.0 / D), "r_m")
    c.inc(ve.tensor_scalar_mul(scrA2[0:1, 0:T], ps[0:1, 2048:2048 + T], 1.0 / D), "r_e2")
    c.wait(sc, "r_m")
    c.wait(sc, "ln_sq")
    c.inc(sc.activation(scrA[0:1, 0:T], rowbuf[0:1, 0:T], AF.Square), "r_m2")
    c.wait(ve, "r_m2")
    c.wait(ve, "r_e2")
    c.inc(ve.tensor_sub(scrA2[0:1, 0:T], scrA2[0:1, 0:T], scrA[0:1, 0:T]), "r_var")
    c.wait(ve, "r_var")
    c.wait(ve, "jr_h")
    c.wait(ve, "ab_l2")
    c.inc(ve.tensor_scalar_add(iota16[0:1, 0:T], scrA2[0:1, 0:T], 1e-5), "r_tz")
    c.wait(sc, "r_tz")
    c.inc(sc.activation(scrA[0:1, 0:T], iota16[0:1, 0:T], AF.Sqrt), "r_sq")
    c.wait(ve, "r_sq")
    c.inc(ve.reciprocal(iota16[0:1, 0:T], scrA[0:1, 0:T]), "r_rs")

    c.wait(te, "r_m2")
    c.wait(te, "r_e2")
    for j in range(QCOL):
        c.inc(te.matmul(ps[0:128, j * 512:(j + 1) * 512], ones_row[0:1, 0:128],
                        rowbuf[0:1, j * 512:(j + 1) * 512], start=True, stop=True), "bc_m")
    c.wait(te, "r_rs")
    for j in range(QCOL):
        c.inc(te.matmul(ps[0:128, 2048 + j * 512:2048 + (j + 1) * 512], ones_row[0:1, 0:128],
                        iota16[0:1, j * 512:(j + 1) * 512], start=True, stop=True), "bc_r")

    c.wait(ve, "r_rs")
    c.inc(ve.tensor_scalar_add(scrA[0:1, 0:T], scrA2[0:1, 0:T], 1e-5), "r_t")
    c.wait(ve, "r_t")
    c.wait(ve, "bc_r", c.val("bc_r"))
    c.inc(ve.reciprocal(iota16[0:1, 0:T], scrA[0:1, 0:T]), "r_u1")
    c.wait(ve, "r_u1")
    c.inc(ve.tensor_mul(scrA[0:1, 0:T], scrA2[0:1, 0:T], iota16[0:1, 0:T]), "r_u")
    c.wait(sc, "r_u")
    c.inc(sc.activation(scrA2[0:1, 0:T], scrA[0:1, 0:T], AF.Sqrt), "r_w")
    c.wait(ve, "r_w")
    c.inc(ve.tensor_scalar_add(scrA[0:1, 0:T], scrA2[0:1, 0:T], 1e-8), "r_v8")
    c.wait(ve, "r_v8")
    c.inc(ve.reciprocal(scrA2[0:1, 0:T], scrA[0:1, 0:T]), "r_rf")
    c.wait(gp, "r_rf")
    c.wait(gp, "bc_m", c.val("bc_m"))
    c.dinc(gp.dma_start(out=rowbuf[0:1, 0:T], in_=scrA2[0:1, 0:T]), "r_rfd")

    c.wait(ve, "bc_m")
    c.wait(ve, "bc_r")
    c.wait(ve, "r_rf")
    c.wait(ve, "r_rfd", c.val("r_rfd"))
    for dc in range(DC):
        if dc >= 1:
            c.wait(ve, "xn_lo")
        c.inc(ve.tensor_sub(scrA[:, 0:T], x_res[:, dc * T:(dc + 1) * T], ps[0:128, 0:T]), "xn_t1")
        c.wait(ve, "xn_t1")
        c.inc(ve.tensor_mul(scrA2[:, 0:T], scrA[:, 0:T], ps[0:128, 2048:2048 + T]), "xn_f")
        c.wait(sc, "xn_f")
        c.inc(sc.activation(xn_hi[:, dc * T:(dc + 1) * T], scrA2[:, 0:T], AF.Copy), "xn_hic")
        c.wait(ve, "xn_hic")
        c.inc(ve.tensor_sub(xn_lo[:, dc * T:(dc + 1) * T], scrA2[:, 0:T],
                            xn_hi[:, dc * T:(dc + 1) * T]), "xn_lo")

    c.wait(sy, "xn_lo")
    if stop_after == "A":
        _early_finish()
        return nc
    phA.close()

    phB = ExitStack()
    v_hat = salloc(phB, "v_hat", [128, KC * H * 65], F32)
    pt0 = salloc(phB, "pt0", [128, 2048], F32)
    phB0 = ExitStack()
    wv_hi = salloc(phB0, "wv_hi", [128, DC * D], BF16)
    wv_lo = salloc(phB0, "wv_lo", [128, DC * D], BF16)
    bv_bf = salloc(phB0, "bv_bf", [128, 2 * D], BF16)

    vrr = v_hat.rearrange("p (kc h e) -> p kc h e", kc=KC, h=H)

    c.dinc(sy.dma_start(out=v_hat[0:1, 0:D], in_=bv_in), "d_bv")
    c.wait(sc, "d_bv")
    split_row(v_hat[0:1, 0:D], bv_bf[0:1, 0:D], bv_bf[0:1, D:2 * D], "b_v")

    for dc in range(DC):
        c.wait(sy, "wv_sp")
        half = (dc % 2) * D
        c.dinc(sy.dma_start(out=pt0[:, half:half + D], in_=wv_in[:, dc * D:(dc + 1) * D]), "d_wv")
        c.wait(sc, "d_wv")
        c.inc(sc.activation(wv_hi[:, dc * D:(dc + 1) * D], pt0[:, half:half + D], AF.Copy), "wv_h")
        c.wait(ve, "wv_h")
        c.inc(ve.tensor_sub(wv_lo[:, dc * D:(dc + 1) * D], pt0[:, half:half + D],
                            wv_hi[:, dc * D:(dc + 1) * D]), "wv_sp")

    if stop_after == "B0a":
        _early_finish()
        return nc
    c.inc(ve.memset(vrr[:, :, :, 64:65], 1.0), "vh_1")
    if stop_after == "B0b":
        _early_finish()
        return nc

    NVH = D // 512
    c.wait(te, "wv_sp")
    c.wait(te, "b_v")
    vh_bank = 0
    for kc in range(KC):
        for vh in range(NVH):
            b = vh_bank % 2
            vh_bank += 1
            c.wait(te, "vh_ev", max(0, c.val("vh_ev") - 1))
            c.inc(te.matmul(ps[0:128, b * 512:(b + 1) * 512], ones_bf[0:1, 0:128],
                            bv_bf[0:1, vh * 512:(vh + 1) * 512], start=True, stop=False), "vh_mm0")
            c.inc(te.matmul(ps[0:128, b * 512:(b + 1) * 512], ones_bf[0:1, 0:128],
                            bv_bf[0:1, D + vh * 512: D + (vh + 1) * 512], start=False, stop=False), "vh_mm0")
            for dc in range(DC):
                xh = xn_hi[:, dc * T + kc * 128: dc * T + kc * 128 + 128]
                xl = xn_lo[:, dc * T + kc * 128: dc * T + kc * 128 + 128]
                wh = wv_hi[:, dc * D + vh * 512: dc * D + (vh + 1) * 512]
                wl = wv_lo[:, dc * D + vh * 512: dc * D + (vh + 1) * 512]
                last = dc == DC - 1
                c.inc(te.matmul(ps[0:128, b * 512:(b + 1) * 512], xh, wh, start=False, stop=False), "vh_mm0")
                c.inc(te.matmul(ps[0:128, b * 512:(b + 1) * 512], xh, wl, start=False, stop=False), "vh_mm0")
                c.inc(te.matmul(ps[0:128, b * 512:(b + 1) * 512], xl, wh, start=False,
                                stop=last), "vh_mm" if last else "vh_mm0")
            if stop_after == "B0c":
                continue
            c.wait(ve, "vh_mm")
            c.wait(ve, "vh_1")
            c.wait(ve, "b_vv", c.val("b_vv"))
            nheads = 512 // HD
            dst = vrr[:, kc, vh * nheads:(vh + 1) * nheads, 0:64]
            src = ps[0:128, b * 512:(b + 1) * 512].rearrange("p (h e) -> p h e", e=64)
            c.inc(ve.tensor_copy(dst, src), "vh_ev")

    c.wait(te, "d_l6")
    phB0.close()


    pt1 = salloc(phB, "pt1", [128, 2048], F32)
    q_hi = salloc(phB, "q_hi", [128, T], BF16)
    q_lo = salloc(phB, "q_lo", [128, T], BF16)
    k_hi = salloc(phB, "k_hi", [128, T], BF16)
    k_lo = salloc(phB, "k_lo", [128, T], BF16)
    rhs6m = salloc(phB, "rhs6m", [128, T], BF16)
    rhs6p = salloc(phB, "rhs6p", [128, T], BF16)
    c.wait(ve, "vh_mm", c.val("vh_mm"))
    c.wait(ve, "vh_ev", c.val("vh_ev"))
    c.inc(ve.memset(rhs6m, 0.0), "r6z")
    c.inc(ve.memset(rhs6p, 0.0), "r6z")
    wqk_hi = salloc(phB, "wqk_hi", [128, DC * 128], BF16)
    wqk_lo = salloc(phB, "wqk_lo", [128, DC * 128], BF16)
    wstage = salloc(phB, "wstage", [128, DC * 128 + 256], F32)
    if stop_after in ("B0", "B0c"):
        _early_finish()
        return nc

    phW = ExitStack()
    WP = 512
    wstg = salloc(phW, "wstg", [128, 2 * WP], F32)
    wabs = salloc(phW, "wabs", [128, WP], F32)
    wsgn = salloc(phW, "wsgn", [128, WP], BF16)
    wmask = salloc(phW, "wmask", [128, WP], BF16)
    wtout = salloc(phW, "wtout", [128, WP], BF16)
    wsum = salloc(phW, "wsum", [128, 8], F32)
    NPC_UP = (DC * HC * 128) // WP
    NPC_DN = (HC * FC * 128) // WP
    w_state = {"stats_done": False, "thr_done": False}

    def w_units():
        c.wait(sy, "vh_mm", c.val("vh_mm"))
        c.wait(sy, "wv_sp", c.val("wv_sp"))
        c.wait(ve, "vh_mm", c.val("vh_mm"))
        c.inc(ve.memset(wsum[:, 0:2], 0.0), "wsz")
        c.wait(sc, "wsz")
        yield
        for (win, sumcol, npieces) in ((wup_in, 0, NPC_UP), (wdn_in, 1, NPC_DN)):
            for i in range(npieces):
                half = (i % 2) * WP
                dtag = "d_ws" if i % 2 == 0 else "d_wsB"
                c.wait(sy, "ws_abs", max(0, c.val("ws_abs") - 1))
                c.dinc(sy.dma_start(out=wstg[:, half:half + WP],
                                    in_=win[:, i * WP:(i + 1) * WP]), dtag)
                yield
                c.wait(sc, dtag)
                c.wait(sc, "ws_add", c.val("ws_add"))
                c.inc(sc.activation(wabs, wstg[:, half:half + WP], AF.Abs,
                                    accum_out=wsum[:, 4 + (i % 2):5 + (i % 2)]), "ws_abs")
                c.wait(ve, "ws_abs")
                c.inc(ve.tensor_add(wsum[:, sumcol:sumcol + 1], wsum[:, sumcol:sumcol + 1],
                                    wsum[:, 4 + (i % 2):5 + (i % 2)]), "ws_add")
                yield
        w_state["stats_done"] = True
        yield
        c.wait(sy, "ws_abs", c.val("ws_abs"))
        for (win, wtscr, thcol, npieces) in ((wup_in, wup_t_scr, 0, NPC_UP),
                                             (wdn_in, wdn_t_scr, 1, NPC_DN)):
            for i in range(npieces):
                half = (i % 2) * WP
                dtag = "d_wt" if i % 2 == 0 else "d_wtB"
                c.wait(sy, "wt_abs", max(0, c.val("wt_abs") - 2))
                c.dinc(sy.dma_start(out=wstg[:, half:half + WP],
                                    in_=win[:, i * WP:(i + 1) * WP]), dtag)
                yield
                c.wait(sc, dtag)
                c.wait(sc, "wt_g", c.val("wt_g"))
                c.wait(sc, "wt_m", c.val("wt_m"))
                c.inc(sc.activation(wabs, wstg[:, half:half + WP], AF.Abs), "wt_abs")
                c.inc(sc.activation(wsgn, wstg[:, half:half + WP], AF.Sign), "wt_abs")
                yield
                c.wait(ve, "wt_abs")
                c.wait(ve, "w_tcp", 1)
                c.inc(ve.tensor_single_scalar(wmask, wabs, thr[:, thcol:thcol + 1],
                                              ALU.is_gt), "wt_g")
                c.wait(ve, "wt_g")
                otag = "d_wto" if i % 2 == 0 else "d_wtoB"
                c.wait(ve, "d_wto", c.val("d_wto"))
                c.wait(ve, "d_wtoB", c.val("d_wtoB"))
                c.inc(ve.tensor_mul(wtout[:, 0:WP], wsgn, wmask), "wt_m")
                c.wait(gp, "wt_m")
                c.dinc(gp.dma_start(out=wtscr[:, i * WP:(i + 1) * WP],
                                    in_=wtout[:, 0:WP]), otag)
                yield

    wgen = w_units()
    _W_DONE = object()

    def drain_w(n):
        for _ in range(n):
            if w_state["stats_done"] and not w_state["thr_done"]:
                return
            if next(wgen, _W_DONE) is _W_DONE:
                return

    def qk_psum(bias_t, bias_tag, tag):
        c.wait(te, bias_tag)
        for j in range(QCOL):
            c.inc(te.matmul(ps[0:128, j * 512:(j + 1) * 512], bias_t[0:1, 0:128],
                            ones_bf[0:1, 0:512], start=True, stop=False), tag + "0")
            c.inc(te.matmul(ps[0:128, j * 512:(j + 1) * 512], bias_t[0:1, 128:256],
                            ones_bf[0:1, 0:512], start=False, stop=False), tag + "0")
            for dc in range(DC):
                xh = xn_hi[:, dc * T + j * 512: dc * T + (j + 1) * 512]
                xl = xn_lo[:, dc * T + j * 512: dc * T + (j + 1) * 512]
                wh = wqk_hi[:, dc * 128:(dc + 1) * 128]
                wl = wqk_lo[:, dc * 128:(dc + 1) * 128]
                last = dc == DC - 1
                c.inc(te.matmul(ps[0:128, j * 512:(j + 1) * 512], wh, xh,
                                start=False, stop=False), tag + "0")
                c.inc(te.matmul(ps[0:128, j * 512:(j + 1) * 512], wl, xh,
                                start=False, stop=False), tag + "0")
                c.inc(te.matmul(ps[0:128, j * 512:(j + 1) * 512], wh, xl,
                                start=False, stop=last), tag if last else tag + "0")

    c.dinc(sy.dma_start(out=wstage[:, 0:DC * 128], in_=wq_in[0]), "d_wqs")
    c.dinc(sy.dma_start(out=wstage[0:1, DC * 128:DC * 128 + 128], in_=bq_in[0]), "d_wqs")
    c.dinc(sy.dma_start(out=wstage[0:1, DC * 128 + 128:DC * 128 + 256], in_=bk_in[0]), "d_wqs")
    c.wait(sc, "d_wqs")
    c.wait(sc, "vh_mm", c.val("vh_mm"))
    c.inc(sc.activation(wqk_hi, wstage[:, 0:DC * 128], AF.Copy), "wq_h")
    c.wait(ve, "wq_h")
    c.wait(ve, "vh_mm", c.val("vh_mm"))
    c.inc(ve.tensor_sub(wqk_lo, wstage[:, 0:DC * 128], wqk_hi), "wq_l")
    split_row(wstage[0:1, DC * 128:DC * 128 + 128], bq_bf[0:1, 0:128], bq_bf[0:1, 128:256], "bqs")
    split_row(wstage[0:1, DC * 128 + 128:DC * 128 + 256], bk_bf[0:1, 0:128], bk_bf[0:1, 128:256], "bks")

    for p in range(NP):
        hA, hB = 2 * p, 2 * p + 1
        if p == NP // 2:
            while not w_state["stats_done"]:
                next(wgen)
            c.wait(te, "exp", c.val("exp"))
            c.wait(te, "at_b", c.val("at_b"))
            c.wait(te, "nc_b", c.val("nc_b"))
            c.wait(te, "ws_add")
            c.inc(te.matmul(ps[0:1, 0:2], ones_row[:, 0:1], wsum[:, 0:2],
                            start=True, stop=True), "w_smm")
            c.wait(ve, "w_smm")
            c.inc(ve.tensor_scalar_mul(wsum[0:1, 2:4], ps[0:1, 0:2], 0.5 / (D * DFF)), "w_th")
            c.wait(te, "w_th")
            c.inc(te.matmul(ps[0:128, 512:514], ones_row[0:1, 0:128], wsum[0:1, 2:4],
                            start=True, stop=True), "w_tbc")
            c.wait(sc, "w_tbc")
            c.inc(sc.activation(thr[:, 0:2], ps[0:128, 512:514], AF.Copy), "w_tcp")
            c.wait(te, "w_tcp")
            w_state["thr_done"] = True

        c.wait(gp, "sc_mm", c.val("sc_mm"))
        c.wait(gp, "vh_ev", c.val("vh_ev"))
        c.wait(gp, "mb_lo", c.val("mb_lo"))
        c.wait(gp, "ab_d", c.val("ab_d"))
        c.wait(gp, "r6n", c.val("r6n"))
        c.wait(gp, "r6z", c.val("r6z"))
        for (dstrow, srcrow) in ((4, 0 + hA), (5, 32 + hA), (12, 0 + hB), (13, 32 + hB)):
            c.dinc(gp.dma_start(out=rhs6m[dstrow:dstrow + 1, :],
                                in_=Mb[srcrow:srcrow + 1, :]), "d_r6")
        c.dinc(gp.dma_start(out=rhs6m[68:70, :], in_=rhs6m[12:14, :]), "d_r6")
        c.wait(ve, "d_r6", c.val("d_r6"))
        c.wait(ve, "sc_mm", c.val("sc_mm"))
        c.inc(ve.tensor_scalar(rhs6m[0:4, :], rhs6m[0:4, :], 0.0, a4[0:4, hA:hA + 1],
                               ALU.mult, ALU.add), "r6a")
        c.inc(ve.tensor_scalar(rhs6m[64:68, :], rhs6m[64:68, :], 0.0, a4[64:68, hB:hB + 1],
                               ALU.mult, ALU.add), "r6a")
        if stop_after == f"P{p}w3":
            break
        c.wait(ve, "r6a")
        c.inc(ve.tensor_scalar_mul(rhs6p[0:6, :], rhs6m[0:6, :], -1.0), "r6n")
        c.inc(ve.tensor_scalar_mul(rhs6p[64:70, :], rhs6m[64:70, :], -1.0), "r6n")

        if stop_after == f"P{p}w":
            break
        c.wait(sc, "pt_fix", c.val("pt_fix"))
        c.wait(sc, "tr_cp")
        c.wait(sc, "con_r")
        c.inc(sc.activation(e2diag[:, 0:128], R_relu, AF.Exp, scale=neg2a[:, hA:hA + 1]), "e2d")
        c.inc(sc.activation(e2diag[:, 128:256], R_relu, AF.Exp, scale=neg2a[:, hB:hB + 1]), "e2d")
        drain_w(4)

        c.wait(te, "vh_ev")
        c.wait(te, "exp", c.val("exp"))
        c.wait(te, "wq_l")
        qk_psum(bq_bf, "bqs", "q_mm")
        if stop_after == f"P{p}qm":
            break
        c.wait(sc, "q_mm")
        c.wait(sc, "sc_mm", c.val("sc_mm"))
        c.inc(sc.activation(q_hi, ps[0:128, 0:T], AF.Copy, scale=0.125), "q_h")
        c.wait(ve, "q_h")
        c.inc(ve.scalar_tensor_tensor(q_lo, ps[0:128, 0:T], 0.125, q_hi,
                                      ALU.mult, ALU.subtract), "q_l")
        c.wait(ve, "pv_g", c.val("pv_g"))
        c.wait(ve, "d_at", c.val("d_at"))
        if stop_after == f"P{p}qs":
            break
        c.inc(ve.tensor_mul(pt0[:, 0:T], q_hi, q_hi), "q_sq")
        if stop_after == f"P{p}n1":
            break

        c.wait(sy, "pv_g", c.val("pv_g"))
        c.wait(sy, "at_b", c.val("at_b"))
        c.wait(sy, "nc_b", c.val("nc_b"))
        c.wait(sy, "d_at", c.val("d_at"))
        if p == 0:
            c.wait(sy, "vh_mm", c.val("vh_mm"))
            c.wait(sy, "vh_ev", c.val("vh_ev"))
        c.dinc(sy.dma_start(out=pt1[:, 0:DC * 128], in_=wk_in[p]), "d_wk")
        c.wait(sc, "d_wk")
        c.wait(sc, "q_mm", c.val("q_mm"))
        c.inc(sc.activation(wqk_hi, pt1[:, 0:DC * 128], AF.Copy), "wk_h")
        c.wait(ve, "wk_h")
        c.inc(ve.tensor_sub(wqk_lo, pt1[:, 0:DC * 128], wqk_hi), "wk_l")
        c.wait(te, "q_l")
        c.wait(te, "wk_l")
        qk_psum(bk_bf, "bks", "k_mm")
        c.wait(sc, "k_mm")
        c.inc(sc.activation(k_hi, ps[0:128, 0:T], AF.Copy), "k_h")
        c.wait(ve, "k_h")
        c.inc(ve.tensor_sub(k_lo, ps[0:128, 0:T], k_hi), "k_l")

        if p + 1 < NP:
            c.wait(sy, "wq_l", c.val("wq_l"))
            c.wait(sy, "bqsv", c.val("bqsv"))
            c.wait(sy, "bksv", c.val("bksv"))
            c.dinc(sy.dma_start(out=wstage[:, 0:DC * 128], in_=wq_in[p + 1]), "d_wqs")
            c.dinc(sy.dma_start(out=wstage[0:1, DC * 128:DC * 128 + 128],
                                in_=bq_in[p + 1]), "d_wqs")
            c.dinc(sy.dma_start(out=wstage[0:1, DC * 128 + 128:DC * 128 + 256],
                                in_=bk_in[p + 1]), "d_wqs")
            c.wait(sc, "d_wqs")
            c.wait(sc, "k_mm")
            c.inc(sc.activation(wqk_hi, wstage[:, 0:DC * 128], AF.Copy), "wq_h")
            c.wait(ve, "wq_h")
            c.inc(ve.tensor_sub(wqk_lo, wstage[:, 0:DC * 128], wqk_hi), "wq_l")
            split_row(wstage[0:1, DC * 128:DC * 128 + 128],
                      bq_bf[0:1, 0:128], bq_bf[0:1, 128:256], "bqs")
            split_row(wstage[0:1, DC * 128 + 128:DC * 128 + 256],
                      bk_bf[0:1, 0:128], bk_bf[0:1, 128:256], "bks")
        if stop_after == f"P{p}w1":
            break

        for j in range(QCOL):
            c.wait(te, "q_sq")
            c.wait(te, "at_b", c.val("at_b"))
            c.wait(te, "qn_rx", max(0, c.val("qn_rx") - 1))
            c.inc(te.matmul(ps[0:1, 2048 + (j % 2) * 512: 2048 + (j % 2) * 512 + 512],
                            ones_row[:, 0:1], pt0[:, j * 512:(j + 1) * 512],
                            start=True, stop=True), "qn_mm")
            if stop_after == f"P{p}n2":
                break
            c.wait(ve, "qn_mm")
            c.inc(ve.reduce_max(smalls[0:1, j:j + 1],
                                ps[0:1, 2048 + (j % 2) * 512: 2048 + (j % 2) * 512 + 512],
                                axis=AX.X), "qn_rx")
        if stop_after in (f"P{p}n2", f"P{p}n3"):
            break
        c.wait(ve, "qn_rx")
        c.inc(ve.reduce_max(smalls[0:1, 8:9], smalls[0:1, 0:QCOL], axis=AX.X), "qn_m")
        if stop_after == f"P{p}qn":
            break

        c.wait(ve, "qn_mm", c.val("qn_mm"))
        c.inc(ve.tensor_mul(pt0[:, 0:T], k_hi, k_hi), "k_sq")
        c.wait(ve, "qn_m")
        for j in range(QCOL):
            c.wait(te, "k_sq")
            c.wait(te, "qn_rx", c.val("qn_rx"))
            c.wait(te, "kn_rx", max(0, c.val("kn_rx") - 1))
            c.inc(te.matmul(ps[0:1, 2048 + (j % 2) * 512: 2048 + (j % 2) * 512 + 512],
                            ones_row[:, 0:1], pt0[:, j * 512:(j + 1) * 512],
                            start=True, stop=True), "kn_mm")
            c.wait(ve, "kn_mm")
            c.inc(ve.reduce_max(smalls[0:1, j:j + 1],
                                ps[0:1, 2048 + (j % 2) * 512: 2048 + (j % 2) * 512 + 512],
                                axis=AX.X), "kn_rx")
        c.wait(ve, "kn_rx")
        c.inc(ve.reduce_max(smalls[0:1, 9:10], smalls[0:1, 0:QCOL], axis=AX.X), "kn_m")
        drain_w(4)
        if stop_after == f"P{p}k":
            break

        c.wait(sc, "qn_m")
        c.wait(sc, "kn_m")
        c.inc(sc.activation(smalls[0:1, 10:11], smalls[0:1, 8:9], AF.Sqrt), "c_s")
        c.inc(sc.activation(smalls[0:1, 11:12], smalls[0:1, 9:10], AF.Sqrt), "c_s")
        c.wait(ve, "c_s")
        c.inc(ve.tensor_mul(smalls[0:1, 12:13], smalls[0:1, 10:11], smalls[0:1, 11:12]), "c_m")
        c.wait(ve, "c_m")
        c.inc(ve.tensor_scalar_mul(smalls[0:1, 13:14], smalls[0:1, 12:13], -1.02), "c_n")
        c.wait(te, "c_n")
        c.wait(te, "kn_rx", c.val("kn_rx"))
        c.inc(te.matmul(ps[0:128, 2048:2049], ones_row[0:1, 0:128], smalls[0:1, 13:14],
                        start=True, stop=True), "c_bc")
        c.wait(sc, "c_bc")
        c.wait(sc, "exp", c.val("exp"))
        c.inc(sc.activation(negc, ps[0:128, 2048:2049], AF.Copy), "c_cp")

        if stop_after == f"P{p}qk":
            break
        c.wait(te, "k_l")
        c.wait(te, "k_sq", c.val("k_sq"))
        c.wait(te, "c_cp", c.val("c_cp"))
        c.wait(te, "d_r6")
        c.wait(te, "e2d", c.val("e2d"))

        def score_mms(h, qb, kc, b, out_list):
            base = 0 if h == hA else 64
            pb = ps[0:128, b * 512:(b + 1) * 512]
            kh = k_hi[base:base + 64, kc * 128:(kc + 1) * 128]
            kl = k_lo[base:base + 64, kc * 128:(kc + 1) * 128]
            qh = q_hi[base:base + 64, qb * 512:(qb + 1) * 512]
            ql = q_lo[base:base + 64, qb * 512:(qb + 1) * 512]
            out_list.append(lambda: c.inc(te.matmul(pb, kh, qh, start=True, stop=False), "sc_mm0"))
            out_list.append(lambda: c.inc(te.matmul(pb, kh, ql, start=False, stop=False), "sc_mm0"))
            out_list.append(lambda: c.inc(te.matmul(pb, kl, qh, start=False, stop=False), "sc_mm0"))
            lhs = lhsT6[base:base + 6, kc * 128:(kc + 1) * 128]
            r6m = rhs6m[base:base + 6, :]
            r6p = rhs6p[base:base + 6, :]
            q0, q1 = qb * 512, (qb + 1) * 512
            kstart, kend = kc * 128, kc * 128 + 128
            segs = []
            if q0 < min(kstart, q1):
                segs.append((q0, min(kstart, q1), r6p))
            dlo, dhi = max(q0, kstart), min(q1, kend)
            if dlo < dhi:
                segs.append((dlo, dhi, r6m))
            if max(q0, kend) < q1:
                segs.append((max(q0, kend), q1, r6m))
            for idx, (a0, a1, rsrc) in enumerate(segs):
                last = idx == len(segs) - 1
                out_list.append(lambda a0=a0, a1=a1, rsrc=rsrc, last=last:
                                c.inc(te.matmul(ps[0:128, b * 512 + (a0 - q0): b * 512 + (a1 - q0)],
                                                lhs, rsrc[:, a0:a1], start=False, stop=last),
                                      "sc_mm" if last else "sc_mm0"))

        def emit_pv(ptb, kc0, kc1):
            c.wait(te, "exp", c.val("exp"))
            c.wait(te, "pt_fix", c.val("pt_fix"))
            if kc0 == 0:
                c.wait(te, "at_b", c.val("at_b"))
            npv = 0
            for (h, kc, pcol) in ((hA, kc0, 0), (hA, kc1, 512), (hB, kc0, 1024), (hB, kc1, 1536)):
                b = 4 if h == hA else 5
                npv += 1
                c.inc(te.matmul(ps[0:65, b * 512:(b + 1) * 512],
                                vrr[:, kc, h, 0:65], ptb[:, pcol:pcol + 512],
                                start=(kc == 0), stop=(kc == KC - 1)),
                      "pv_g" if npv == 4 else "pv_mm0")

        def emit_norm(qb):
            c.wait(sc, "pv_g", c.val("pv_g"))
            c.wait(sc, "nb_a", c.val("nb_a"))
            c.wait(sc, "nb_b", c.val("nb_b"))
            c.inc(sc.activation(pt1[64:65, 0:512],
                                ps[64:65, 4 * 512:4 * 512 + 512], AF.Copy), "nr_a")
            c.inc(sc.activation(pt1[64:65, 512:1024],
                                ps[64:65, 5 * 512:5 * 512 + 512], AF.Copy), "nr_b")
            c.wait(te, "nr_a")
            c.wait(te, "nr_b")
            c.wait(te, "nc_b", c.val("nc_b"))
            c.inc(te.matmul(ps[0:64, 6 * 512:7 * 512], ones_row[64:65, 0:64],
                            pt1[64:65, 0:512], start=True, stop=True), "nb_a")
            c.inc(te.matmul(ps[0:64, 7 * 512:8 * 512], ones_row[64:65, 0:64],
                            pt1[64:65, 512:1024], start=True, stop=True), "nb_b")
            c.wait(ve, "nb_a")
            c.wait(ve, "nb_b")
            c.wait(ve, "at_b", c.val("at_b"))
            c.inc(ve.reciprocal(pt1[0:64, 0:512], ps[0:64, 6 * 512:7 * 512]), "nc_a")
            c.inc(ve.reciprocal(pt1[0:64, 512:1024], ps[0:64, 7 * 512:8 * 512]), "nc_b")
            c.wait(ve, "nc_a")
            c.wait(ve, "nc_b")
            c.inc(ve.tensor_mul(pt0[0:64, 0:512], ps[0:64, 4 * 512:5 * 512],
                                pt1[0:64, 0:512]), "at_a")
            c.inc(ve.tensor_mul(pt0[0:64, 512:1024], ps[0:64, 5 * 512:6 * 512],
                                pt1[0:64, 512:1024]), "at_b")
            c.wait(gp, "at_b")
            c.dinc(gp.dma_start(out=attn_scr[hA, qb], in_=pt0[0:64, 0:512]), "d_at")
            c.dinc(gp.dma_start(out=attn_scr[hB, qb], in_=pt0[0:64, 512:1024]), "d_at")
            drain_w(2)

        pending_norm = None
        for qb in range(QB):
            pending = None
            for kcg in range(KCG):
                kc0, kc1 = 2 * kcg, 2 * kcg + 1
                ptb = pt0 if kcg % 2 == 0 else pt1
                c.wait(te, "exp", c.val("exp"))
                mmsA, mmsB = [], []
                score_mms(hA, qb, kc0, 0, mmsA)
                score_mms(hA, qb, kc1, 1, mmsA)
                score_mms(hB, qb, kc0, 2, mmsB)
                score_mms(hB, qb, kc1, 3, mmsB)
                for i in range(max(len(mmsA), len(mmsB))):
                    if i < len(mmsA):
                        mmsA[i]()
                    if i < len(mmsB):
                        mmsB[i]()
                if kcg == 0 and pending_norm is not None:
                    emit_norm(pending_norm)
                    pending_norm = None
                if pending is not None:
                    emit_pv(*pending)
                c.wait(sc, "sc_mm")
                c.wait(sc, "c_cp")
                c.wait(sc, "pv_g", max(0, c.val("pv_g") - 1))
                c.wait(sc, "at_b", c.val("at_b"))
                c.wait(sc, "nc_b", c.val("nc_b"))
                c.wait(sc, "d_at", c.val("d_at"))
                c.inc(sc.activation(ptb, ps[0:128, 0:2048], AF.Exp, bias=negc[:, 0:1]), "exp")
                for (h, col, pcol) in ((hA, 0, 0), (hA, 1, 512), (hB, 0, 1024), (hB, 1, 1536)):
                    kc = 2 * kcg + col
                    if kc * 128 >= qb * 512 and (kc + 1) * 128 <= (qb + 1) * 512:
                        off = pcol + (kc * 128 - qb * 512)
                        c.wait(ve, "exp")
                        c.inc(ve.tensor_mul(ptb[:, off:off + 128], ptb[:, off:off + 128],
                                            e2diag[:, (0 if h == hA else 128):(128 if h == hA else 256)]),
                              "pt_fix")
                pending = (ptb, kc0, kc1)
                drain_w(2 if p < NP // 2 else 3)
            emit_pv(*pending)
            pending_norm = qb
            if stop_after == f"P{p}q{qb}":
                break
        if pending_norm is not None:
            emit_norm(pending_norm)
            pending_norm = None

        if stop_after is not None and stop_after.startswith(f"P{p}"):
            break

    if stop_after == "B" or (stop_after or "").startswith("P"):
        _early_finish()
        return nc
    c.wait(sy, "d_at", c.val("d_at"))
    c.wait(sy, "pv_g", c.val("pv_g"))
    c.wait(sy, "at_b", c.val("at_b"))
    c.wait(sy, "nc_b", c.val("nc_b"))
    c.wait(sy, "exp", c.val("exp"))
    c.wait(te, "exp", c.val("exp"))
    c.wait(te, "at_b", c.val("at_b"))
    c.wait(te, "nc_b", c.val("nc_b"))
    while next(wgen, _W_DONE) is not _W_DONE:
        pass
    c.wait(sy, "d_wto")
    c.wait(sy, "d_wtoB")
    c.wait(sy, "wt_m", c.val("wt_m"))
    phW.close()
    phB.close()

    phC = ExitStack()
    wo_hi = salloc(phC, "wo_hi", [128, DC * FC * 128], BF16)
    wo_lo = salloc(phC, "wo_lo", [128, DC * FC * 128], BF16)
    wo_st = salloc(phC, "wo_st", [128, 2048], F32)
    bo_bf = salloc(phC, "bo_bf", [128, 2 * D], BF16)

    c.dinc(sy.dma_start(out=wo_st[0:1, 0:D], in_=bo_in), "d_bo")
    c.wait(sc, "d_bo")
    split_row(wo_st[0:1, 0:D], bo_bf[0:1, 0:D], bo_bf[0:1, D:2 * D], "b_o")
    attn_f = salloc(phC, "attn_f", [128, DC * 512], F32)
    attn_h2 = salloc(phC, "attn_h2", [128, DC * 512], BF16)
    attn_l2 = salloc(phC, "attn_l2", [128, DC * 512], BF16)
    s_qb = salloc(phC, "s_qb", [128, FC * 512], F32)

    NWO = (DC * FC * 128) // 2048
    for i in range(NWO):
        c.wait(sy, "wo_sp", c.val("wo_sp"))
        c.wait(sy, "b_ov", c.val("b_ov"))
        c.dinc(sy.dma_start(out=wo_st, in_=wo_in[:, i * 2048:(i + 1) * 2048]), "d_wo")
        c.wait(sc, "d_wo")
        c.inc(sc.activation(wo_hi[:, i * 2048:(i + 1) * 2048], wo_st, AF.Copy), "wo_h")
        c.wait(ve, "wo_h")
        c.inc(ve.tensor_sub(wo_lo[:, i * 2048:(i + 1) * 2048], wo_st,
                            wo_hi[:, i * 2048:(i + 1) * 2048]), "wo_sp")

    for qb in range(QB):
        c.wait(sy, "c_sp", c.val("c_sp"))
        for dc in range(DC):
            c.dinc(sy.dma_start(out=attn_f[0:64, dc * 512:(dc + 1) * 512],
                                in_=attn_scr[2 * dc, qb]), "d_af")
            c.dinc(sy.dma_start(out=attn_f[64:128, dc * 512:(dc + 1) * 512],
                                in_=attn_scr[2 * dc + 1, qb]), "d_af")
        c.wait(sc, "d_af")
        c.wait(sc, "s_mm", c.val("s_mm"))
        c.inc(sc.activation(attn_h2, attn_f, AF.Copy), "c_h")
        c.wait(ve, "c_h")
        c.inc(ve.tensor_sub(attn_l2, attn_f, attn_h2), "c_sp")
        c.wait(ve, "s_w", c.val("s_w"))
        c.wait(te, "c_sp")
        c.wait(te, "wo_sp")
        c.wait(te, "b_o")
        for fc in range(FC):
            b = fc % 2
            c.wait(te, "s_ev", max(0, c.val("s_ev") - 1))
            c.inc(te.matmul(ps[0:128, b * 512:(b + 1) * 512], bo_bf[0:1, fc * 128:(fc + 1) * 128],
                            ones_bf[0:1, 0:512], start=True, stop=False), "s_mm0")
            c.inc(te.matmul(ps[0:128, b * 512:(b + 1) * 512], bo_bf[0:1, D + fc * 128: D + (fc + 1) * 128],
                            ones_bf[0:1, 0:512], start=False, stop=False), "s_mm0")
            for dc in range(DC):
                w_hi = wo_hi[:, (dc * FC + fc) * 128:(dc * FC + fc) * 128 + 128]
                w_lo = wo_lo[:, (dc * FC + fc) * 128:(dc * FC + fc) * 128 + 128]
                ah = attn_h2[:, dc * 512:(dc + 1) * 512]
                al = attn_l2[:, dc * 512:(dc + 1) * 512]
                last = dc == DC - 1
                c.inc(te.matmul(ps[0:128, b * 512:(b + 1) * 512], w_hi, ah,
                                start=False, stop=False), "s_mm0")
                c.inc(te.matmul(ps[0:128, b * 512:(b + 1) * 512], w_lo, ah,
                                start=False, stop=False), "s_mm0")
                c.inc(te.matmul(ps[0:128, b * 512:(b + 1) * 512], w_hi, al,
                                start=False, stop=last), "s_mm" if last else "s_mm0")
            c.wait(ve, "s_mm")
            c.inc(ve.tensor_copy(s_qb[:, fc * 512:(fc + 1) * 512],
                                 ps[0:128, b * 512:(b + 1) * 512]), "s_ev")
        c.wait(gp, "s_ev")
        for fc in range(FC):
            c.dinc(gp.dma_start(out=s_scr[fc, qb], in_=s_qb[:, fc * 512:(fc + 1) * 512]), "s_w")

    c.wait(sy, "s_w")
    if stop_after == "C":
        _early_finish()
        return nc
    phC.close()

    phD1 = ExitStack()
    wt_up = salloc(phD1, "wt_up", [128, DC * HC * 128], BF16)
    hpre = salloc(phD1, "hpre", [128, 2 * 512], F32)
    hful = salloc(phD1, "hful", [128, 2 * 512], F32)
    h2b = salloc(phD1, "h2b", [128, 2 * 512], F32)
    hhi = salloc(phD1, "hhi", [128, 2 * 512], BF16)
    hlo = salloc(phD1, "hlo", [128, 2 * 512], BF16)
    rfb = salloc(phD1, "rfb", [128, 512], F32)
    rowd = salloc(phD1, "rowd", [128, 1024], F32)

    c.dinc(sy.dma_start(out=wt_up, in_=wup_t_scr), "d_wtu")
    c.wait(te, "s_ev", c.val("s_ev"))

    for tb in range(TB):
        c.wait(te, "rfb_c", c.val("rfb_c"))
        c.wait(te, "r_rfd")
        c.inc(te.matmul(ps[0:128, 6 * 512:7 * 512], ones_row[0:1, 0:128],
                        rowbuf[0:1, tb * 512:(tb + 1) * 512], start=True, stop=True), "rfb_m")
        c.wait(sc, "rfb_m")
        c.wait(sc, "up_ev", c.val("up_ev"))
        c.inc(sc.activation(rfb, ps[0:128, 6 * 512:7 * 512], AF.Copy), "rfb_c")
        c.wait(te, "d_wtu")
        c.wait(te, "rfb_c")
        for hc in range(HC):
            b = hc % 2
            c.wait(te, "up_ev", max(0, c.val("up_ev") - 1))
            for dc in range(DC):
                wt = wt_up[:, (dc * HC + hc) * 128:(dc * HC + hc) * 128 + 128]
                xh = xn_hi[:, dc * T + tb * 512: dc * T + (tb + 1) * 512]
                xl = xn_lo[:, dc * T + tb * 512: dc * T + (tb + 1) * 512]
                c.inc(te.matmul(ps[0:128, b * 512:(b + 1) * 512], wt, xh,
                                start=(dc == 0), stop=False), "up_mm0")
                c.inc(te.matmul(ps[0:128, b * 512:(b + 1) * 512], wt, xl,
                                start=False, stop=(dc == DC - 1)),
                      "up_mm" if dc == DC - 1 else "up_mm0")
            c.wait(ve, "up_mm")
            c.wait(ve, "g_h", max(0, c.val("g_h") - 1))
            c.inc(ve.tensor_mul(hpre[:, b * 512:(b + 1) * 512],
                                ps[0:128, b * 512:(b + 1) * 512], rfb), "up_ev")
            c.wait(sc, "up_ev")
            c.wait(sc, "h_lo", max(0, c.val("h_lo") - 1))
            c.inc(sc.activation(hful[:, b * 512:(b + 1) * 512],
                                hpre[:, b * 512:(b + 1) * 512], AF.Gelu), "g_h")
            c.wait(sc, "g_h")
            c.wait(sc, "d_h" if b == 0 else "d_hB")
            c.inc(sc.activation(hhi[:, b * 512:(b + 1) * 512],
                                hful[:, b * 512:(b + 1) * 512], AF.Copy), "h_hi")
            c.wait(ve, "h_hi")
            c.inc(ve.tensor_sub(hlo[:, b * 512:(b + 1) * 512],
                                hful[:, b * 512:(b + 1) * 512],
                                hhi[:, b * 512:(b + 1) * 512]), "h_lo")
            c.wait(sc, "h2_rd", max(0, c.val("h2_rd") - 1))
            c.inc(sc.activation(h2b[:, b * 512:(b + 1) * 512],
                                hful[:, b * 512:(b + 1) * 512], AF.Square), "h_sq")
            c.wait(te, "h_sq")
            c.inc(te.matmul(ps[0:1, 7 * 512:7 * 512 + 512], ones_row[:, 0:1],
                            h2b[:, b * 512:(b + 1) * 512],
                            start=(hc == 0), stop=(hc == HC - 1)), "h2_rd")
            c.wait(gp, "h_lo")
            htag = "d_h" if b == 0 else "d_hB"
            c.dinc(gp.dma_start(out=h_scr_hi[tb][:, hc * 512:(hc + 1) * 512],
                                in_=hhi[:, b * 512:(b + 1) * 512]), htag)
            c.dinc(gp.dma_start(out=h_scr_lo[tb][:, hc * 512:(hc + 1) * 512],
                                in_=hlo[:, b * 512:(b + 1) * 512]), htag)
        c.wait(ve, "h2_rd")
        c.wait(ve, "rh_d", c.val("rh_d"))
        c.inc(ve.tensor_scalar_mul(rowd[0:1, 0:512], ps[0:1, 7 * 512:7 * 512 + 512],
                                   1.0 / DFF), "rh_1")
        c.wait(sc, "rh_1")
        c.inc(sc.activation(rowd[0:1, 512:1024], rowd[0:1, 0:512], AF.Sqrt), "rh_2")
        c.wait(ve, "rh_2")
        c.inc(ve.tensor_scalar_add(rowd[0:1, 0:512], rowd[0:1, 512:1024], 1e-8), "rh_3")
        c.wait(ve, "rh_3")
        c.inc(ve.reciprocal(rowd[0:1, 512:1024], rowd[0:1, 0:512]), "rh_4")
        c.wait(gp, "rh_4")
        c.dinc(gp.dma_start(out=rowbuf[32:33, tb * 512:(tb + 1) * 512],
                            in_=rowd[0:1, 512:1024]), "rh_d")

    c.wait(sy, "d_h")
    c.wait(sy, "d_hB")
    c.wait(sy, "rh_d")
    if stop_after == "D1":
        _early_finish()
        return nc
    phD1.close()
    xn_stack.close()

    phD2 = ExitStack()
    wt_dn = salloc(phD2, "wt_dn", [128, HC * FC * 128], BF16)
    h_ahi = salloc(phD2, "h_ahi", [128, HC * 512], BF16)
    h_alo = salloc(phD2, "h_alo", [128, HC * 512], BF16)
    gbuf = salloc(phD2, "gbuf", [128, 2 * 512], F32)
    xsbuf = salloc(phD2, "xsbuf", [128, 4 * 512], F32)
    rhb = salloc(phD2, "rhb", [128, 512], F32)

    c.dinc(sy.dma_start(out=wt_dn, in_=wdn_t_scr), "d_wtd")
    c.wait(te, "up_ev", c.val("up_ev"))
    c.wait(te, "rfb_c", c.val("rfb_c"))
    c.wait(te, "rh_1", c.val("rh_1"))

    for tb in range(TB):
        c.wait(sy, "dn_mm", c.val("dn_mm"))
        c.dinc(sy.dma_start(out=h_ahi, in_=h_scr_hi[tb]), "d_ha")
        c.dinc(sy.dma_start(out=h_alo, in_=h_scr_lo[tb]), "d_ha")
        c.wait(te, "rhb_c", c.val("rhb_c"))
        c.wait(te, "rh_d")
        c.inc(te.matmul(ps[0:128, 6 * 512:7 * 512], ones_row[32:33, 0:128],
                        rowbuf[32:33, tb * 512:(tb + 1) * 512], start=True, stop=True), "rhb_m")
        c.wait(sc, "rhb_m")
        c.wait(sc, "dn_ev", c.val("dn_ev"))
        c.inc(sc.activation(rhb, ps[0:128, 6 * 512:7 * 512], AF.Copy), "rhb_c")
        c.wait(te, "d_wtd")
        c.wait(te, "d_ha")
        c.wait(te, "rhb_c")
        for fc in range(FC):
            b = fc % 2
            c.wait(te, "dn_ev", max(0, c.val("dn_ev") - 1))
            for hc in range(HC):
                wt = wt_dn[:, (hc * FC + fc) * 128:(hc * FC + fc) * 128 + 128]
                hh = h_ahi[:, hc * 512:(hc + 1) * 512]
                hl = h_alo[:, hc * 512:(hc + 1) * 512]
                c.inc(te.matmul(ps[0:128, b * 512:(b + 1) * 512], wt, hh,
                                start=(hc == 0), stop=False), "dn_mm0")
                c.inc(te.matmul(ps[0:128, b * 512:(b + 1) * 512], wt, hl,
                                start=False, stop=(hc == HC - 1)),
                      "dn_mm" if hc == HC - 1 else "dn_mm0")
            xoff = 0 if fc % 2 == 0 else 1024
            c.wait(sy, "fin_o", max(0, c.val("fin_o") - 16))
            c.dinc(sy.dma_start(out=xsbuf[:, xoff:xoff + 512],
                                in_=x_in[:, fc * T + tb * 512: fc * T + (tb + 1) * 512]), "d_xs")
            c.dinc(sy.dma_start(out=xsbuf[:, xoff + 512:xoff + 1024],
                                in_=s_scr[fc, tb]), "d_xs")
            c.wait(ve, "dn_mm")
            c.wait(ve, "g_s", max(0, c.val("g_s") - 1))
            c.wait(ve, "fin_o", max(0, c.val("fin_o") - 16))
            c.inc(ve.tensor_mul(gbuf[:, b * 512:(b + 1) * 512],
                                ps[0:128, b * 512:(b + 1) * 512], rhb), "dn_ev")
            c.wait(sc, "dn_ev")
            c.wait(sc, "fin_m", max(0, c.val("fin_m") - 1))
            c.inc(sc.activation(gbuf[:, b * 512:(b + 1) * 512],
                                gbuf[:, b * 512:(b + 1) * 512], AF.Silu), "g_s")
            c.wait(ve, "g_s")
            c.wait(ve, "d_xs")
            c.inc(ve.tensor_mul(gbuf[:, b * 512:(b + 1) * 512],
                                gbuf[:, b * 512:(b + 1) * 512],
                                xsbuf[:, xoff + 512:xoff + 1024]), "fin_m")
            c.wait(ve, "fin_m")
            c.inc(ve.tensor_add(gbuf[:, b * 512:(b + 1) * 512],
                                gbuf[:, b * 512:(b + 1) * 512],
                                xsbuf[:, xoff:xoff + 512]), "fin_a")
            c.wait(gp, "fin_a")
            c.dinc(gp.dma_start(out=out_hbm[:, fc * T + tb * 512: fc * T + (tb + 1) * 512],
                                in_=gbuf[:, b * 512:(b + 1) * 512]), "fin_o")

    c.wait(gp, "fin_o")
    phD2.close()
    return nc


def _prep_core(xb, qkv_w, qkv_b, out_w, out_b, alpha, w_up, w_down,
               T=2048, D=1024, H=16, DFF=4096):
    DC, HC, FC, NP = D // 128, DFF // 128, D // 128, H // 2

    x_prep = np.ascontiguousarray(
        xb.T.reshape(DC, 128, T).transpose(1, 0, 2).reshape(128, DC * T))
    wq = np.empty((NP, 128, DC * 128), np.float32)
    wk = np.empty((NP, 128, DC * 128), np.float32)
    for p in range(NP):
        Wq = qkv_w[128 * p:128 * p + 128, :]
        Wk = qkv_w[D + 128 * p: D + 128 * p + 128, :]
        wq[p] = Wq.reshape(128, DC, 128).transpose(2, 1, 0).reshape(128, DC * 128)
        wk[p] = Wk.reshape(128, DC, 128).transpose(2, 1, 0).reshape(128, DC * 128)
    Wv = qkv_w[2 * D:3 * D, :]
    wv = Wv.reshape(D, DC, 128).transpose(2, 1, 0).reshape(128, DC * D)
    bq = np.ascontiguousarray(qkv_b[0:D].reshape(NP, 1, 128))
    bk = np.ascontiguousarray(qkv_b[D:2 * D].reshape(NP, 1, 128))
    bv = np.ascontiguousarray(qkv_b[2 * D:3 * D].reshape(1, D))
    wo = out_w.reshape(FC, 128, DC, 128).transpose(3, 2, 0, 1).reshape(128, DC * FC * 128)
    bo = np.ascontiguousarray(out_b.reshape(1, D))
    wup = w_up.reshape(HC, 128, DC, 128).transpose(3, 2, 0, 1).reshape(128, DC * HC * 128)
    wdn = w_down.reshape(FC, 128, HC, 128).transpose(3, 2, 0, 1).reshape(128, HC * FC * 128)
    al = np.ascontiguousarray(alpha.reshape(H, 1))
    return {
        "x_in": x_prep, "wq_in": np.ascontiguousarray(wq), "wk_in": np.ascontiguousarray(wk),
        "wv_in": np.ascontiguousarray(wv), "bq_in": bq, "bk_in": bk, "bv_in": bv,
        "wo_in": np.ascontiguousarray(wo), "bo_in": bo,
        "wup_in": np.ascontiguousarray(wup), "wdn_in": np.ascontiguousarray(wdn),
        "alpha_in": al,
    }


def kernel(L0_attn_x, L0_attn_qkv_w, L0_attn_qkv_b, L0_attn_out_w, L0_attn_out_b,
           L0_attn_decay_alpha, L0_ffn_w_up, L0_ffn_w_down):
    T, D, H, DFF = 2048, 1024, 16, 4096
    x = np.ascontiguousarray(np.asarray(L0_attn_x, dtype=np.float32))
    qkv_w = np.ascontiguousarray(np.asarray(L0_attn_qkv_w, dtype=np.float32))
    qkv_b = np.asarray(L0_attn_qkv_b, dtype=np.float32)
    out_w = np.ascontiguousarray(np.asarray(L0_attn_out_w, dtype=np.float32))
    out_b = np.asarray(L0_attn_out_b, dtype=np.float32)
    alpha = np.asarray(L0_attn_decay_alpha, dtype=np.float32)
    w_up = np.ascontiguousarray(np.asarray(L0_ffn_w_up, dtype=np.float32))
    w_down = np.ascontiguousarray(np.asarray(L0_ffn_w_down, dtype=np.float32))

    nc = build_program(T=T, D=D, H=H, DFF=DFF)
    in_maps = [
        _prep_core(x[b], qkv_w, qkv_b, out_w, out_b, alpha, w_up, w_down,
                   T=T, D=D, H=H, DFF=DFF)
        for b in range(N_CORES)
    ]
    res = run_bass_kernel_spmd(nc, in_maps, core_ids=list(range(N_CORES)))
    FC = D // 128
    out = np.empty((N_CORES, T, D), np.float32)
    for b in range(N_CORES):
        o = np.asarray(res.results[b]["out_hbm"])
        out[b] = o.reshape(128, FC, T).transpose(1, 0, 2).reshape(D, T).T
    return out


# version 12
# speedup vs baseline: 1.0004x; 1.0004x over previous
from contextlib import ExitStack

import numpy as np
import ml_dtypes

import concourse.bass as bass
import concourse.mybir as mybir
from concourse.bass_utils import run_bass_kernel_spmd

F32 = mybir.dt.float32
BF16 = mybir.dt.bfloat16
AF = mybir.ActivationFunctionType
ALU = mybir.AluOpType
AX = mybir.AxisListType

N_CORES = 8


_DED = {"pv_g", "vh_ev", "s_ev", "up_ev", "dn_ev", "g_h", "h_lo", "h2_rd",
        "d_h", "d_hB", "d_wto", "d_wtoB", "d_ws", "d_wsB", "d_wt", "d_wtB",
        "fin_o", "qn_rx", "kn_rx", "g_s", "fin_m", "con_g",
        "ws_abs", "ws_add", "wt_abs", "wt_g", "wt_m", "w_tcp"}

_GROUPS = {
    "g_te": ["tr_mm", "ln_sx", "ln_x2mm", "bc_m", "bc_r", "q_mm", "k_mm",
             "qn_mm", "kn_mm", "c_bc", "sc_mm", "vh_mm", "pv_mm", "s_mm",
             "up_mm", "dn_mm", "rfb_m", "rhb_m", "w_smm", "w_tbc",
             "nb_a", "nb_b"],
    "g_sc": ["al_e", "al_a", "al_h", "al_lb", "ab_h", "ab_lb", "mb_hi", "jr_h", "tr_cp",
             "q_h", "k_h", "c_s", "c_cp", "e2d", "exp",
             "nr_a", "nr_b", "wo_h", "wv_h", "c_h", "h_hi", "h_sq",
             "rfb_c", "rhb_c", "ws_abs", "wt_abs", "w_tcp", "ln_sq",
             "wq_h", "wk_h", "tr_1", "tr_2",
             "r_m2", "r_sq", "r_w", "xn_hic", "b_v", "b_o", "bqs", "bks",
             "rh_2"],
    "g_ve": ["con_v", "con_r", "al_n", "al_h2", "al_r", "al_l2", "mb_f",
             "mb_lo", "ab_z", "ab_h2", "ab_l", "ab_l2", "r6n", "r6z", "jr_1", "jr_l",
             "tr_3", "b_vv", "b_ov", "bqsv", "bksv",
             "r_m", "r_e2", "r_var", "r_tz", "r_rs", "r_t", "r_u", "r_u1", "r_v8",
             "r_rf", "xn_t1", "xn_f", "xn_lo", "wq_l", "wk_l", "q_l",
             "k_l", "q_sq", "k_sq", "r6a", "qn_m", "kn_m", "c_m", "c_n", "nc_a", "nc_b", "at_a",
             "at_b", "pt_fix", "vh_1", "wv_sp", "wo_sp", "c_sp", "wt_g",
             "wt_m", "wsz", "ws_add", "w_th", "rh_1", "rh_3",
             "rh_4", "fin_a"],
    "g_gp": ["d_l6", "d_r6", "d_at", "s_w", "rh_d", "r_rsd",
             "r_rfd", "ab_d"],
    "g_sy": ["d_x", "d_al", "d_al2", "d_bv", "d_bo", "d_wv", "d_wq", "d_wqs", "d_wk",
             "d_wo", "d_af", "d_wtu", "d_wtd", "d_ha", "d_xs"],
}
_TAG2SEM = {t: g for g, ts in _GROUPS.items() for t in ts}


class Ctx:

    def __init__(self, nc):
        self.nc = nc
        self.sems = {}
        self.cnt = {}

    def phys(self, name):
        if name in _DED:
            return name
        return _TAG2SEM[name]

    def S(self, name):
        ph = self.phys(name)
        if ph not in self.sems:
            self.sems[ph] = self.nc.alloc_semaphore(ph)
            self.cnt[ph] = 0
        return self.sems[ph]

    def inc(self, instr, name, n=1):
        if name.endswith("0"):
            return instr
        instr.then_inc(self.S(name), n)
        self.cnt[self.phys(name)] += n
        return instr

    def dinc(self, instr, name):
        self.inc(instr, name, 16)
        eng = self.engines[instr.ins.engine]
        eng.wait_ge(self.S(name), self.cnt[self.phys(name)])
        return instr

    def wait(self, eng, name, val=None):
        ph = self.phys(name)
        v = self.cnt.get(ph, 0) if val is None else val
        if v > 0:
            eng.wait_ge(self.S(name), v)

    def val(self, name):
        return self.cnt.get(self.phys(name), 0)


def build_program(T=2048, D=1024, H=16, DFF=4096, stop_after=None):
    HD = 64
    NP = H // 2
    DC = D // 128
    KC = T // 128
    QB = T // 512
    HC = DFF // 128
    FC = DC
    TB = QB
    QCOL = T // 512
    KCG = KC // 2

    nc = bass.Bass("TRN2", target_bir_lowering=False)
    c = Ctx(nc)
    sy, te, ve, sc, gp = nc.sync, nc.tensor, nc.vector, nc.scalar, nc.gpsimd
    c.engines = {e.engine: e for e in (sy, te, ve, sc, gp)}

    x_in = nc.dram_tensor("x_in", [128, DC * T], F32, kind="ExternalInput").ap()
    wq_in = nc.dram_tensor("wq_in", [NP, 128, DC * 128], F32, kind="ExternalInput").ap()
    wk_in = nc.dram_tensor("wk_in", [NP, 128, DC * 128], F32, kind="ExternalInput").ap()
    wv_in = nc.dram_tensor("wv_in", [128, DC * D], F32, kind="ExternalInput").ap()
    bq_in = nc.dram_tensor("bq_in", [NP, 1, 128], F32, kind="ExternalInput").ap()
    bk_in = nc.dram_tensor("bk_in", [NP, 1, 128], F32, kind="ExternalInput").ap()
    bv_in = nc.dram_tensor("bv_in", [1, D], F32, kind="ExternalInput").ap()
    wo_in = nc.dram_tensor("wo_in", [128, DC * FC * 128], F32, kind="ExternalInput").ap()
    bo_in = nc.dram_tensor("bo_in", [1, D], F32, kind="ExternalInput").ap()
    wup_in = nc.dram_tensor("wup_in", [128, DC * HC * 128], F32, kind="ExternalInput").ap()
    wdn_in = nc.dram_tensor("wdn_in", [128, HC * FC * 128], F32, kind="ExternalInput").ap()
    alpha_in = nc.dram_tensor("alpha_in", [H, 1], F32, kind="ExternalInput").ap()

    attn_scr = nc.dram_tensor("attn_scr", [H, QB, 64, 512], F32, kind="Internal").ap()
    s_scr = nc.dram_tensor("s_scr", [FC, TB, 128, 512], F32, kind="Internal").ap()
    h_scr_hi = nc.dram_tensor("h_scr_hi", [TB, 128, HC * 512], BF16, kind="Internal").ap()
    h_scr_lo = nc.dram_tensor("h_scr_lo", [TB, 128, HC * 512], BF16, kind="Internal").ap()
    wup_t_scr = nc.dram_tensor("wup_t_scr", [128, DC * HC * 128], BF16, kind="Internal").ap()
    wdn_t_scr = nc.dram_tensor("wdn_t_scr", [128, HC * FC * 128], BF16, kind="Internal").ap()

    out_hbm = nc.dram_tensor("out_hbm", [128, FC * T], F32, kind="ExternalOutput").ap()

    ps = nc.alloc_psum_tensor("ps", [128, 4096], F32).ap()

    ones_row = nc.alloc_sbuf_tensor("ones_row", [128, 128], F32).ap()
    ones_bf = nc.alloc_sbuf_tensor("ones_bf", [128, 512], BF16).ap()
    lhsT6 = nc.alloc_sbuf_tensor("lhsT6", [128, T], BF16).ap()
    Mb = nc.alloc_sbuf_tensor("Mb", [128, T], BF16).ap()
    R_relu = nc.alloc_sbuf_tensor("R_relu", [128, 128], F32).ap()
    alpha_c = nc.alloc_sbuf_tensor("alpha_c", [16, 8], F32).ap()
    tiny_bf = nc.alloc_sbuf_tensor("tiny_bf", [16, 8], BF16).ap()
    arow = nc.alloc_sbuf_tensor("arow", [128, 96], F32).ap()
    a4 = nc.alloc_sbuf_tensor("a4", [128, 16], F32).ap()
    neg2a = nc.alloc_sbuf_tensor("neg2a", [128, 16], F32).ap()
    rowbuf = nc.alloc_sbuf_tensor("rowbuf", [128, T], F32).ap()
    thr = nc.alloc_sbuf_tensor("thr", [128, 2], F32).ap()
    smalls = nc.alloc_sbuf_tensor("smalls", [128, 16], F32).ap()
    negc = nc.alloc_sbuf_tensor("negc", [128, 1], F32).ap()
    bq_bf = nc.alloc_sbuf_tensor("bq_bf", [128, 256], BF16).ap()
    bk_bf = nc.alloc_sbuf_tensor("bk_bf", [128, 256], BF16).ap()
    e2diag = nc.alloc_sbuf_tensor("e2diag", [128, 256], F32).ap()

    def salloc(stack, name, shape, dtype):
        return stack.enter_context(nc.sbuf_tensor(name, list(shape), dtype)).ap()


    def _early_finish():
        c.wait(gp, "sc_mm", c.val("sc_mm"))
        c.dinc(gp.dma_start(out=out_hbm[0:1, 0:4], in_=ones_row[0:1, 0:4]), "fin_o")
        c.wait(gp, "fin_o")
    xn_stack = ExitStack()
    xn_hi = salloc(xn_stack, "xn_hi", [128, DC * T], BF16)
    xn_lo = salloc(xn_stack, "xn_lo", [128, DC * T], BF16)

    def split_row(src_f32_row0, hi_row0, lo_row0, tag):
        c.inc(sc.activation(hi_row0, src_f32_row0, AF.Copy), tag)
        c.wait(ve, tag)
        c.inc(ve.tensor_sub(lo_row0, src_f32_row0, hi_row0), tag + "v")

    c.inc(ve.memset(ones_row, 1.0), "con_v")
    c.inc(ve.memset(ones_bf, 1.0), "con_v")
    c.inc(gp.iota(R_relu, pattern=[[-1, 128]], base=0, channel_multiplier=1,
                  allow_small_or_imprecise_dtypes=True), "con_g")
    c.wait(ve, "con_g", 1)
    c.inc(ve.tensor_scalar_max(R_relu, R_relu, 0.0), "con_r")

    phA = ExitStack()
    x_res = salloc(phA, "x_res", [128, DC * T], F32)
    scrA = salloc(phA, "scrA", [128, T], F32)
    scrA2 = salloc(phA, "scrA2", [128, T], F32)
    iota16 = salloc(phA, "iota16", [16, T], F32)
    jrows = salloc(phA, "jrows", [128, T], BF16)
    jrows2 = salloc(phA, "jrows2", [128, T], BF16)
    jrows3 = salloc(phA, "jrows3", [128, T], BF16)

    c.dinc(sy.dma_start(out=x_res, in_=x_in), "d_x")
    c.dinc(sy.dma_start(out=alpha_c[0:H, 0:1], in_=alpha_in[0:H, 0:1]), "d_al")

    c.inc(gp.iota(iota16, pattern=[[1, T]], base=0, channel_multiplier=0,
                  allow_small_or_imprecise_dtypes=True), "con_g")

    c.wait(sc, "d_al")
    c.inc(sc.activation(alpha_c[0:H, 1:2], alpha_c[0:H, 0:1], AF.Exp), "al_e")
    c.wait(sc, "al_e")
    c.inc(sc.activation(alpha_c[0:H, 2:3], alpha_c[0:H, 1:2], AF.Ln, bias=1.0), "al_a")
    c.wait(ve, "al_a")
    c.inc(ve.tensor_scalar_mul(alpha_c[0:H, 3:4], alpha_c[0:H, 2:3], -1.0), "al_n")
    c.wait(sc, "al_n")
    c.inc(sc.activation(tiny_bf[0:H, 0:1], alpha_c[0:H, 2:3], AF.Copy), "al_h")
    c.wait(ve, "al_h")
    c.inc(ve.tensor_copy(alpha_c[0:H, 4:5], tiny_bf[0:H, 0:1]), "al_h2")
    c.wait(ve, "al_h2")
    c.inc(ve.tensor_sub(alpha_c[0:H, 5:6], alpha_c[0:H, 2:3], alpha_c[0:H, 4:5]), "al_r")
    c.wait(sc, "al_r")
    c.inc(sc.activation(tiny_bf[0:H, 1:2], alpha_c[0:H, 5:6], AF.Copy), "al_lb")
    c.wait(ve, "al_lb")
    c.inc(ve.tensor_copy(alpha_c[0:H, 6:7], tiny_bf[0:H, 1:2]), "al_l2")

    c.wait(ve, "con_g")
    c.wait(ve, "al_n")
    c.inc(ve.tensor_scalar_mul(scrA[0:H, 0:T], iota16[0:H, :], alpha_c[0:H, 3:4]), "mb_f")
    c.wait(sc, "mb_f")
    c.inc(sc.activation(Mb[0:H, :], scrA[0:H, 0:T], AF.Copy), "mb_hi")
    c.wait(ve, "mb_hi")
    c.inc(ve.tensor_sub(Mb[32:32 + H, :], scrA[0:H, 0:T], Mb[0:H, :]), "mb_lo")
    c.dinc(sy.dma_start(out=arow[0:1, 0:H],
                        in_=alpha_in.rearrange("h one -> one h")), "d_al2")
    c.wait(sc, "d_al2")
    c.inc(sc.activation(arow[0:1, 16:16 + H], arow[0:1, 0:H], AF.Exp), "tr_1")
    c.wait(sc, "tr_1")
    c.inc(sc.activation(arow[0:1, 32:32 + H], arow[0:1, 16:16 + H], AF.Ln, bias=1.0), "tr_2")
    c.wait(ve, "tr_2")
    c.inc(ve.tensor_scalar_mul(arow[0:1, 48:48 + H], arow[0:1, 32:32 + H], -2.0), "tr_3")
    c.wait(te, "tr_3")
    c.wait(te, "con_v")
    c.inc(te.matmul(ps[0:128, 0:H], ones_row[0:1, 0:128], arow[0:1, 48:48 + H],
                    start=True, stop=True), "tr_mm")
    c.wait(sc, "tr_mm")
    c.inc(sc.activation(neg2a[:, 0:H], ps[0:128, 0:H], AF.Copy), "tr_cp")


    c.inc(ve.memset(arow[0:1, 64:96], 0.0), "ab_z")
    c.wait(sc, "tr_2")
    c.wait(sc, "ab_z")
    c.inc(sc.activation(jrows[0:1, 0:H], arow[0:1, 32:32 + H], AF.Copy), "ab_h")
    c.wait(ve, "ab_h")
    c.inc(ve.tensor_copy(arow[0:1, 64:64 + H], jrows[0:1, 0:H]), "ab_h2")
    c.wait(ve, "ab_h2")
    c.inc(ve.tensor_sub(arow[0:1, 16:16 + H], arow[0:1, 32:32 + H], arow[0:1, 64:64 + H]), "ab_l")
    c.wait(sc, "ab_l")
    c.inc(sc.activation(jrows[0:1, 0:H], arow[0:1, 16:16 + H], AF.Copy), "ab_lb")
    c.wait(ve, "ab_lb")
    c.inc(ve.tensor_copy(arow[0:1, 80:80 + H], jrows[0:1, 0:H]), "ab_l2")
    c.wait(gp, "ab_l2")
    for r, src0 in ((0, 64), (1, 64), (2, 80), (3, 80)):
        c.dinc(gp.dma_start(out=a4[r:r + 1, 0:16], in_=arow[0:1, src0:src0 + 16]), "ab_d")
    c.dinc(gp.dma_start(out=a4[64:68, 0:16], in_=a4[0:4, 0:16]), "ab_d")

    c.wait(ve, "ab_l2")
    c.inc(ve.memset(jrows[0:1, :], 1.0), "jr_1")
    c.wait(sc, "con_g")
    c.wait(sc, "jr_1")
    c.inc(sc.activation(jrows2[0:1, :], iota16[0:1, :], AF.Copy), "jr_h")
    c.wait(ve, "jr_h")
    c.inc(ve.tensor_sub(jrows3[0:1, :], iota16[0:1, :], jrows2[0:1, :]), "jr_l")
    c.wait(gp, "jr_1")
    c.wait(gp, "jr_l")
    for r in (4, 5, 12, 13):
        c.dinc(gp.dma_start(out=lhsT6[r:r + 1, :], in_=jrows[0:1, :]), "d_l6")
    for r in (0, 2, 8, 10):
        c.dinc(gp.dma_start(out=lhsT6[r:r + 1, :], in_=jrows2[0:1, :]), "d_l6")
    for r in (1, 3, 9, 11):
        c.dinc(gp.dma_start(out=lhsT6[r:r + 1, :], in_=jrows3[0:1, :]), "d_l6")
    c.dinc(gp.dma_start(out=lhsT6[64:70, :], in_=lhsT6[8:14, :]), "d_l6")

    c.wait(te, "d_x")
    c.wait(te, "tr_cp")
    for dc in range(DC):
        for j in range(QCOL):
            c.inc(te.matmul(ps[0:1, j * 512:(j + 1) * 512], ones_row[:, 0:1],
                            x_res[:, dc * T + j * 512: dc * T + (j + 1) * 512],
                            start=(dc == 0), stop=(dc == DC - 1)),
                  "ln_sx" if dc == DC - 1 else "ln_mm0")
    for dc in range(DC):
        buf = scrA if dc % 2 == 0 else scrA2
        c.wait(sc, "d_x")
        c.wait(sc, "mb_lo")
        if dc >= 2:
            c.wait(sc, "ln_x2mm")
        c.inc(sc.activation(buf[:, 0:T], x_res[:, dc * T:(dc + 1) * T], AF.Square), "ln_sq")
        c.wait(te, "ln_sq")
        for j in range(QCOL):
            c.inc(te.matmul(ps[0:1, 2048 + j * 512: 2048 + (j + 1) * 512], ones_row[:, 0:1],
                            buf[:, j * 512:(j + 1) * 512],
                            start=(dc == 0), stop=(dc == DC - 1)), "ln_x2mm")
    c.wait(ve, "ln_sx")
    c.wait(ve, "ln_x2mm")
    c.inc(ve.tensor_scalar_mul(rowbuf[0:1, 0:T], ps[0:1, 0:T], 1.0 / D), "r_m")
    c.inc(ve.tensor_scalar_mul(scrA2[0:1, 0:T], ps[0:1, 2048:2048 + T], 1.0 / D), "r_e2")
    c.wait(sc, "r_m")
    c.wait(sc, "ln_sq")
    c.inc(sc.activation(scrA[0:1, 0:T], rowbuf[0:1, 0:T], AF.Square), "r_m2")
    c.wait(ve, "r_m2")
    c.wait(ve, "r_e2")
    c.inc(ve.tensor_sub(scrA2[0:1, 0:T], scrA2[0:1, 0:T], scrA[0:1, 0:T]), "r_var")
    c.wait(ve, "r_var")
    c.wait(ve, "jr_h")
    c.wait(ve, "ab_l2")
    c.inc(ve.tensor_scalar_add(iota16[0:1, 0:T], scrA2[0:1, 0:T], 1e-5), "r_tz")
    c.wait(sc, "r_tz")
    c.inc(sc.activation(scrA[0:1, 0:T], iota16[0:1, 0:T], AF.Sqrt), "r_sq")
    c.wait(ve, "r_sq")
    c.inc(ve.reciprocal(iota16[0:1, 0:T], scrA[0:1, 0:T]), "r_rs")

    c.wait(te, "r_m2")
    c.wait(te, "r_e2")
    for j in range(QCOL):
        c.inc(te.matmul(ps[0:128, j * 512:(j + 1) * 512], ones_row[0:1, 0:128],
                        rowbuf[0:1, j * 512:(j + 1) * 512], start=True, stop=True), "bc_m")
    c.wait(te, "r_rs")
    for j in range(QCOL):
        c.inc(te.matmul(ps[0:128, 2048 + j * 512:2048 + (j + 1) * 512], ones_row[0:1, 0:128],
                        iota16[0:1, j * 512:(j + 1) * 512], start=True, stop=True), "bc_r")

    c.wait(ve, "r_rs")
    c.inc(ve.tensor_scalar_add(scrA[0:1, 0:T], scrA2[0:1, 0:T], 1e-5), "r_t")
    c.wait(ve, "r_t")
    c.wait(ve, "bc_r", c.val("bc_r"))
    c.inc(ve.reciprocal(iota16[0:1, 0:T], scrA[0:1, 0:T]), "r_u1")
    c.wait(ve, "r_u1")
    c.inc(ve.tensor_mul(scrA[0:1, 0:T], scrA2[0:1, 0:T], iota16[0:1, 0:T]), "r_u")
    c.wait(sc, "r_u")
    c.inc(sc.activation(scrA2[0:1, 0:T], scrA[0:1, 0:T], AF.Sqrt), "r_w")
    c.wait(ve, "r_w")
    c.inc(ve.tensor_scalar_add(scrA[0:1, 0:T], scrA2[0:1, 0:T], 1e-8), "r_v8")
    c.wait(ve, "r_v8")
    c.inc(ve.reciprocal(scrA2[0:1, 0:T], scrA[0:1, 0:T]), "r_rf")
    c.wait(gp, "r_rf")
    c.wait(gp, "bc_m", c.val("bc_m"))
    c.dinc(gp.dma_start(out=rowbuf[0:1, 0:T], in_=scrA2[0:1, 0:T]), "r_rfd")

    c.wait(ve, "bc_m")
    c.wait(ve, "bc_r")
    c.wait(ve, "r_rf")
    c.wait(ve, "r_rfd", c.val("r_rfd"))
    for dc in range(DC):
        if dc >= 1:
            c.wait(ve, "xn_lo")
        c.inc(ve.tensor_sub(scrA[:, 0:T], x_res[:, dc * T:(dc + 1) * T], ps[0:128, 0:T]), "xn_t1")
        c.wait(ve, "xn_t1")
        c.inc(ve.tensor_mul(scrA2[:, 0:T], scrA[:, 0:T], ps[0:128, 2048:2048 + T]), "xn_f")
        c.wait(sc, "xn_f")
        c.inc(sc.activation(xn_hi[:, dc * T:(dc + 1) * T], scrA2[:, 0:T], AF.Copy), "xn_hic")
        c.wait(ve, "xn_hic")
        c.inc(ve.tensor_sub(xn_lo[:, dc * T:(dc + 1) * T], scrA2[:, 0:T],
                            xn_hi[:, dc * T:(dc + 1) * T]), "xn_lo")

    c.wait(sy, "xn_lo")
    if stop_after == "A":
        _early_finish()
        return nc
    phA.close()

    phB = ExitStack()
    v_hat = salloc(phB, "v_hat", [128, KC * H * 65], F32)
    pt0 = salloc(phB, "pt0", [128, 2048], F32)
    phB0 = ExitStack()
    wv_hi = salloc(phB0, "wv_hi", [128, DC * D], BF16)
    wv_lo = salloc(phB0, "wv_lo", [128, DC * D], BF16)
    bv_bf = salloc(phB0, "bv_bf", [128, 2 * D], BF16)

    vrr = v_hat.rearrange("p (kc h e) -> p kc h e", kc=KC, h=H)

    c.dinc(sy.dma_start(out=v_hat[0:1, 0:D], in_=bv_in), "d_bv")
    c.wait(sc, "d_bv")
    split_row(v_hat[0:1, 0:D], bv_bf[0:1, 0:D], bv_bf[0:1, D:2 * D], "b_v")

    for dc in range(DC):
        c.wait(sy, "wv_sp")
        half = (dc % 2) * D
        c.dinc(sy.dma_start(out=pt0[:, half:half + D], in_=wv_in[:, dc * D:(dc + 1) * D]), "d_wv")
        c.wait(sc, "d_wv")
        c.inc(sc.activation(wv_hi[:, dc * D:(dc + 1) * D], pt0[:, half:half + D], AF.Copy), "wv_h")
        c.wait(ve, "wv_h")
        c.inc(ve.tensor_sub(wv_lo[:, dc * D:(dc + 1) * D], pt0[:, half:half + D],
                            wv_hi[:, dc * D:(dc + 1) * D]), "wv_sp")

    if stop_after == "B0a":
        _early_finish()
        return nc
    c.inc(ve.memset(vrr[:, :, :, 64:65], 1.0), "vh_1")
    if stop_after == "B0b":
        _early_finish()
        return nc

    NVH = D // 512
    c.wait(te, "wv_sp")
    c.wait(te, "b_v")
    vh_bank = 0
    for kc in range(KC):
        for vh in range(NVH):
            b = vh_bank % 2
            vh_bank += 1
            c.wait(te, "vh_ev", max(0, c.val("vh_ev") - 1))
            c.inc(te.matmul(ps[0:128, b * 512:(b + 1) * 512], ones_bf[0:1, 0:128],
                            bv_bf[0:1, vh * 512:(vh + 1) * 512], start=True, stop=False), "vh_mm0")
            c.inc(te.matmul(ps[0:128, b * 512:(b + 1) * 512], ones_bf[0:1, 0:128],
                            bv_bf[0:1, D + vh * 512: D + (vh + 1) * 512], start=False, stop=False), "vh_mm0")
            for dc in range(DC):
                xh = xn_hi[:, dc * T + kc * 128: dc * T + kc * 128 + 128]
                xl = xn_lo[:, dc * T + kc * 128: dc * T + kc * 128 + 128]
                wh = wv_hi[:, dc * D + vh * 512: dc * D + (vh + 1) * 512]
                wl = wv_lo[:, dc * D + vh * 512: dc * D + (vh + 1) * 512]
                last = dc == DC - 1
                c.inc(te.matmul(ps[0:128, b * 512:(b + 1) * 512], xh, wh, start=False, stop=False), "vh_mm0")
                c.inc(te.matmul(ps[0:128, b * 512:(b + 1) * 512], xh, wl, start=False, stop=False), "vh_mm0")
                c.inc(te.matmul(ps[0:128, b * 512:(b + 1) * 512], xl, wh, start=False,
                                stop=last), "vh_mm" if last else "vh_mm0")
            if stop_after == "B0c":
                continue
            c.wait(ve, "vh_mm")
            c.wait(ve, "vh_1")
            c.wait(ve, "b_vv", c.val("b_vv"))
            nheads = 512 // HD
            dst = vrr[:, kc, vh * nheads:(vh + 1) * nheads, 0:64]
            src = ps[0:128, b * 512:(b + 1) * 512].rearrange("p (h e) -> p h e", e=64)
            c.inc(ve.tensor_copy(dst, src), "vh_ev")

    c.wait(te, "d_l6")
    phB0.close()


    pt1 = salloc(phB, "pt1", [128, 2048], F32)
    q_hi = salloc(phB, "q_hi", [128, T], BF16)
    q_lo = salloc(phB, "q_lo", [128, T], BF16)
    k_hi = salloc(phB, "k_hi", [128, T], BF16)
    k_lo = salloc(phB, "k_lo", [128, T], BF16)
    rhs6m = salloc(phB, "rhs6m", [128, T], BF16)
    rhs6p = salloc(phB, "rhs6p", [128, T], BF16)
    c.wait(ve, "vh_mm", c.val("vh_mm"))
    c.wait(ve, "vh_ev", c.val("vh_ev"))
    c.inc(ve.memset(rhs6m, 0.0), "r6z")
    c.inc(ve.memset(rhs6p, 0.0), "r6z")
    wqk_hi = salloc(phB, "wqk_hi", [128, DC * 128], BF16)
    wqk_lo = salloc(phB, "wqk_lo", [128, DC * 128], BF16)
    wstage = salloc(phB, "wstage", [128, DC * 128 + 256], F32)
    if stop_after in ("B0", "B0c"):
        _early_finish()
        return nc

    phW = ExitStack()
    WP = 512
    wstg = salloc(phW, "wstg", [128, 2 * WP], F32)
    wabs = salloc(phW, "wabs", [128, WP], F32)
    wsgn = salloc(phW, "wsgn", [128, WP], BF16)
    wmask = salloc(phW, "wmask", [128, WP], BF16)
    wtout = salloc(phW, "wtout", [128, WP], BF16)
    wsum = salloc(phW, "wsum", [128, 8], F32)
    NPC_UP = (DC * HC * 128) // WP
    NPC_DN = (HC * FC * 128) // WP
    w_state = {"stats_done": False, "thr_done": False}

    def w_units():
        c.wait(sy, "vh_mm", c.val("vh_mm"))
        c.wait(sy, "wv_sp", c.val("wv_sp"))
        c.wait(ve, "vh_mm", c.val("vh_mm"))
        c.inc(ve.memset(wsum[:, 0:2], 0.0), "wsz")
        c.wait(sc, "wsz")
        yield
        for (win, sumcol, npieces) in ((wup_in, 0, NPC_UP), (wdn_in, 1, NPC_DN)):
            for i in range(npieces):
                half = (i % 2) * WP
                dtag = "d_ws" if i % 2 == 0 else "d_wsB"
                c.wait(sy, "ws_abs", max(0, c.val("ws_abs") - 1))
                c.dinc(sy.dma_start(out=wstg[:, half:half + WP],
                                    in_=win[:, i * WP:(i + 1) * WP]), dtag)
                yield
                c.wait(sc, dtag)
                c.wait(sc, "ws_add", c.val("ws_add"))
                c.inc(sc.activation(wabs, wstg[:, half:half + WP], AF.Abs,
                                    accum_out=wsum[:, 4 + (i % 2):5 + (i % 2)]), "ws_abs")
                c.wait(ve, "ws_abs")
                c.inc(ve.tensor_add(wsum[:, sumcol:sumcol + 1], wsum[:, sumcol:sumcol + 1],
                                    wsum[:, 4 + (i % 2):5 + (i % 2)]), "ws_add")
                yield
        w_state["stats_done"] = True
        yield
        c.wait(sy, "ws_abs", c.val("ws_abs"))
        for (win, wtscr, thcol, npieces) in ((wup_in, wup_t_scr, 0, NPC_UP),
                                             (wdn_in, wdn_t_scr, 1, NPC_DN)):
            for i in range(npieces):
                half = (i % 2) * WP
                dtag = "d_wt" if i % 2 == 0 else "d_wtB"
                c.wait(sy, "wt_abs", max(0, c.val("wt_abs") - 2))
                c.dinc(sy.dma_start(out=wstg[:, half:half + WP],
                                    in_=win[:, i * WP:(i + 1) * WP]), dtag)
                yield
                c.wait(sc, dtag)
                c.wait(sc, "wt_g", c.val("wt_g"))
                c.wait(sc, "wt_m", c.val("wt_m"))
                c.inc(sc.activation(wabs, wstg[:, half:half + WP], AF.Abs), "wt_abs")
                c.inc(sc.activation(wsgn, wstg[:, half:half + WP], AF.Sign), "wt_abs")
                yield
                c.wait(ve, "wt_abs")
                c.wait(ve, "w_tcp", 1)
                c.inc(ve.tensor_single_scalar(wmask, wabs, thr[:, thcol:thcol + 1],
                                              ALU.is_gt), "wt_g")
                c.wait(ve, "wt_g")
                otag = "d_wto" if i % 2 == 0 else "d_wtoB"
                c.wait(ve, "d_wto", c.val("d_wto"))
                c.wait(ve, "d_wtoB", c.val("d_wtoB"))
                c.inc(ve.tensor_mul(wtout[:, 0:WP], wsgn, wmask), "wt_m")
                c.wait(gp, "wt_m")
                c.dinc(gp.dma_start(out=wtscr[:, i * WP:(i + 1) * WP],
                                    in_=wtout[:, 0:WP]), otag)
                yield

    wgen = w_units()
    _W_DONE = object()

    def drain_w(n):
        for _ in range(n):
            if w_state["stats_done"] and not w_state["thr_done"]:
                return
            if next(wgen, _W_DONE) is _W_DONE:
                return

    def qk_psum(bias_t, bias_tag, tag):
        c.wait(te, bias_tag)
        for j in range(QCOL):
            c.inc(te.matmul(ps[0:128, j * 512:(j + 1) * 512], bias_t[0:1, 0:128],
                            ones_bf[0:1, 0:512], start=True, stop=False), tag + "0")
            c.inc(te.matmul(ps[0:128, j * 512:(j + 1) * 512], bias_t[0:1, 128:256],
                            ones_bf[0:1, 0:512], start=False, stop=False), tag + "0")
            for dc in range(DC):
                xh = xn_hi[:, dc * T + j * 512: dc * T + (j + 1) * 512]
                xl = xn_lo[:, dc * T + j * 512: dc * T + (j + 1) * 512]
                wh = wqk_hi[:, dc * 128:(dc + 1) * 128]
                wl = wqk_lo[:, dc * 128:(dc + 1) * 128]
                last = dc == DC - 1
                c.inc(te.matmul(ps[0:128, j * 512:(j + 1) * 512], wh, xh,
                                start=False, stop=False), tag + "0")
                c.inc(te.matmul(ps[0:128, j * 512:(j + 1) * 512], wl, xh,
                                start=False, stop=False), tag + "0")
                c.inc(te.matmul(ps[0:128, j * 512:(j + 1) * 512], wh, xl,
                                start=False, stop=last), tag if last else tag + "0")

    c.dinc(sy.dma_start(out=wstage[:, 0:DC * 128], in_=wq_in[0]), "d_wqs")
    c.dinc(sy.dma_start(out=wstage[0:1, DC * 128:DC * 128 + 128], in_=bq_in[0]), "d_wqs")
    c.dinc(sy.dma_start(out=wstage[0:1, DC * 128 + 128:DC * 128 + 256], in_=bk_in[0]), "d_wqs")
    c.wait(sc, "d_wqs")
    c.wait(sc, "vh_mm", c.val("vh_mm"))
    c.inc(sc.activation(wqk_hi, wstage[:, 0:DC * 128], AF.Copy), "wq_h")
    c.wait(ve, "wq_h")
    c.wait(ve, "vh_mm", c.val("vh_mm"))
    c.inc(ve.tensor_sub(wqk_lo, wstage[:, 0:DC * 128], wqk_hi), "wq_l")
    split_row(wstage[0:1, DC * 128:DC * 128 + 128], bq_bf[0:1, 0:128], bq_bf[0:1, 128:256], "bqs")
    split_row(wstage[0:1, DC * 128 + 128:DC * 128 + 256], bk_bf[0:1, 0:128], bk_bf[0:1, 128:256], "bks")

    for p in range(NP):
        hA, hB = 2 * p, 2 * p + 1
        if p == NP // 2:
            while not w_state["stats_done"]:
                next(wgen)
            c.wait(te, "exp", c.val("exp"))
            c.wait(te, "at_b", c.val("at_b"))
            c.wait(te, "nc_b", c.val("nc_b"))
            c.wait(te, "ws_add")
            c.inc(te.matmul(ps[0:1, 0:2], ones_row[:, 0:1], wsum[:, 0:2],
                            start=True, stop=True), "w_smm")
            c.wait(ve, "w_smm")
            c.inc(ve.tensor_scalar_mul(wsum[0:1, 2:4], ps[0:1, 0:2], 0.5 / (D * DFF)), "w_th")
            c.wait(te, "w_th")
            c.inc(te.matmul(ps[0:128, 512:514], ones_row[0:1, 0:128], wsum[0:1, 2:4],
                            start=True, stop=True), "w_tbc")
            c.wait(sc, "w_tbc")
            c.inc(sc.activation(thr[:, 0:2], ps[0:128, 512:514], AF.Copy), "w_tcp")
            c.wait(te, "w_tcp")
            w_state["thr_done"] = True

        c.wait(gp, "sc_mm", c.val("sc_mm"))
        c.wait(gp, "vh_ev", c.val("vh_ev"))
        c.wait(gp, "mb_lo", c.val("mb_lo"))
        c.wait(gp, "ab_d", c.val("ab_d"))
        c.wait(gp, "r6n", c.val("r6n"))
        c.wait(gp, "r6z", c.val("r6z"))
        for (dstrow, srcrow) in ((4, 0 + hA), (5, 32 + hA), (12, 0 + hB), (13, 32 + hB)):
            c.dinc(gp.dma_start(out=rhs6m[dstrow:dstrow + 1, :],
                                in_=Mb[srcrow:srcrow + 1, :]), "d_r6")
        c.dinc(gp.dma_start(out=rhs6m[68:70, :], in_=rhs6m[12:14, :]), "d_r6")
        c.wait(ve, "d_r6", c.val("d_r6"))
        c.wait(ve, "sc_mm", c.val("sc_mm"))
        c.inc(ve.tensor_scalar(rhs6m[0:4, :], rhs6m[0:4, :], 0.0, a4[0:4, hA:hA + 1],
                               ALU.mult, ALU.add), "r6a")
        c.inc(ve.tensor_scalar(rhs6m[64:68, :], rhs6m[64:68, :], 0.0, a4[64:68, hB:hB + 1],
                               ALU.mult, ALU.add), "r6a")
        if stop_after == f"P{p}w3":
            break
        c.wait(ve, "r6a")
        c.inc(ve.tensor_scalar_mul(rhs6p[0:6, :], rhs6m[0:6, :], -1.0), "r6n")
        c.inc(ve.tensor_scalar_mul(rhs6p[64:70, :], rhs6m[64:70, :], -1.0), "r6n")

        if stop_after == f"P{p}w":
            break
        c.wait(sc, "pt_fix", c.val("pt_fix"))
        c.wait(sc, "tr_cp")
        c.wait(sc, "con_r")
        c.inc(sc.activation(e2diag[:, 0:128], R_relu, AF.Exp, scale=neg2a[:, hA:hA + 1]), "e2d")
        c.inc(sc.activation(e2diag[:, 128:256], R_relu, AF.Exp, scale=neg2a[:, hB:hB + 1]), "e2d")
        drain_w(4)

        c.wait(te, "vh_ev")
        c.wait(te, "exp", c.val("exp"))
        c.wait(te, "wq_l")
        qk_psum(bq_bf, "bqs", "q_mm")
        if stop_after == f"P{p}qm":
            break
        c.wait(sc, "q_mm")
        c.wait(sc, "sc_mm", c.val("sc_mm"))
        c.inc(sc.activation(q_hi, ps[0:128, 0:T], AF.Copy, scale=0.125), "q_h")
        c.wait(ve, "q_h")
        c.inc(ve.scalar_tensor_tensor(q_lo, ps[0:128, 0:T], 0.125, q_hi,
                                      ALU.mult, ALU.subtract), "q_l")
        c.wait(ve, "pv_g", c.val("pv_g"))
        if stop_after == f"P{p}qs":
            break
        c.inc(ve.tensor_mul(pt0[:, 0:T], q_hi, q_hi), "q_sq")
        if stop_after == f"P{p}n1":
            break

        c.wait(sy, "pv_g", c.val("pv_g"))
        c.wait(sy, "at_b", c.val("at_b"))
        c.wait(sy, "nc_b", c.val("nc_b"))
        c.wait(sy, "d_at", c.val("d_at"))
        if p == 0:
            c.wait(sy, "vh_mm", c.val("vh_mm"))
            c.wait(sy, "vh_ev", c.val("vh_ev"))
        c.dinc(sy.dma_start(out=pt1[:, 0:DC * 128], in_=wk_in[p]), "d_wk")
        c.wait(sc, "d_wk")
        c.wait(sc, "q_mm", c.val("q_mm"))
        c.inc(sc.activation(wqk_hi, pt1[:, 0:DC * 128], AF.Copy), "wk_h")
        c.wait(ve, "wk_h")
        c.inc(ve.tensor_sub(wqk_lo, pt1[:, 0:DC * 128], wqk_hi), "wk_l")
        c.wait(te, "q_l")
        c.wait(te, "wk_l")
        qk_psum(bk_bf, "bks", "k_mm")
        c.wait(sc, "k_mm")
        c.inc(sc.activation(k_hi, ps[0:128, 0:T], AF.Copy), "k_h")
        c.wait(ve, "k_h")
        c.inc(ve.tensor_sub(k_lo, ps[0:128, 0:T], k_hi), "k_l")

        if p + 1 < NP:
            c.wait(sy, "wq_l", c.val("wq_l"))
            c.wait(sy, "bqsv", c.val("bqsv"))
            c.wait(sy, "bksv", c.val("bksv"))
            c.dinc(sy.dma_start(out=wstage[:, 0:DC * 128], in_=wq_in[p + 1]), "d_wqs")
            c.dinc(sy.dma_start(out=wstage[0:1, DC * 128:DC * 128 + 128],
                                in_=bq_in[p + 1]), "d_wqs")
            c.dinc(sy.dma_start(out=wstage[0:1, DC * 128 + 128:DC * 128 + 256],
                                in_=bk_in[p + 1]), "d_wqs")
            c.wait(sc, "d_wqs")
            c.wait(sc, "k_mm")
            c.inc(sc.activation(wqk_hi, wstage[:, 0:DC * 128], AF.Copy), "wq_h")
            c.wait(ve, "wq_h")
            c.inc(ve.tensor_sub(wqk_lo, wstage[:, 0:DC * 128], wqk_hi), "wq_l")
            split_row(wstage[0:1, DC * 128:DC * 128 + 128],
                      bq_bf[0:1, 0:128], bq_bf[0:1, 128:256], "bqs")
            split_row(wstage[0:1, DC * 128 + 128:DC * 128 + 256],
                      bk_bf[0:1, 0:128], bk_bf[0:1, 128:256], "bks")
        if stop_after == f"P{p}w1":
            break

        for j in range(QCOL):
            c.wait(te, "q_sq")
            c.wait(te, "at_b", c.val("at_b"))
            c.wait(te, "qn_rx", max(0, c.val("qn_rx") - 1))
            c.inc(te.matmul(ps[0:1, 2048 + (j % 2) * 512: 2048 + (j % 2) * 512 + 512],
                            ones_row[:, 0:1], pt0[:, j * 512:(j + 1) * 512],
                            start=True, stop=True), "qn_mm")
            if stop_after == f"P{p}n2":
                break
            c.wait(ve, "qn_mm")
            c.inc(ve.reduce_max(smalls[0:1, j:j + 1],
                                ps[0:1, 2048 + (j % 2) * 512: 2048 + (j % 2) * 512 + 512],
                                axis=AX.X), "qn_rx")
        if stop_after in (f"P{p}n2", f"P{p}n3"):
            break
        c.wait(ve, "qn_rx")
        c.inc(ve.reduce_max(smalls[0:1, 8:9], smalls[0:1, 0:QCOL], axis=AX.X), "qn_m")
        if stop_after == f"P{p}qn":
            break

        c.wait(ve, "qn_mm", c.val("qn_mm"))
        c.inc(ve.tensor_mul(pt0[:, 0:T], k_hi, k_hi), "k_sq")
        c.wait(ve, "qn_m")
        for j in range(QCOL):
            c.wait(te, "k_sq")
            c.wait(te, "qn_rx", c.val("qn_rx"))
            c.wait(te, "kn_rx", max(0, c.val("kn_rx") - 1))
            c.inc(te.matmul(ps[0:1, 2048 + (j % 2) * 512: 2048 + (j % 2) * 512 + 512],
                            ones_row[:, 0:1], pt0[:, j * 512:(j + 1) * 512],
                            start=True, stop=True), "kn_mm")
            c.wait(ve, "kn_mm")
            c.inc(ve.reduce_max(smalls[0:1, j:j + 1],
                                ps[0:1, 2048 + (j % 2) * 512: 2048 + (j % 2) * 512 + 512],
                                axis=AX.X), "kn_rx")
        c.wait(ve, "kn_rx")
        c.inc(ve.reduce_max(smalls[0:1, 9:10], smalls[0:1, 0:QCOL], axis=AX.X), "kn_m")
        drain_w(4)
        if stop_after == f"P{p}k":
            break

        c.wait(sc, "qn_m")
        c.wait(sc, "kn_m")
        c.inc(sc.activation(smalls[0:1, 10:11], smalls[0:1, 8:9], AF.Sqrt), "c_s")
        c.inc(sc.activation(smalls[0:1, 11:12], smalls[0:1, 9:10], AF.Sqrt), "c_s")
        c.wait(ve, "c_s")
        c.inc(ve.tensor_mul(smalls[0:1, 12:13], smalls[0:1, 10:11], smalls[0:1, 11:12]), "c_m")
        c.wait(ve, "c_m")
        c.inc(ve.tensor_scalar_mul(smalls[0:1, 13:14], smalls[0:1, 12:13], -1.02), "c_n")
        c.wait(te, "c_n")
        c.wait(te, "kn_rx", c.val("kn_rx"))
        c.inc(te.matmul(ps[0:128, 2048:2049], ones_row[0:1, 0:128], smalls[0:1, 13:14],
                        start=True, stop=True), "c_bc")
        c.wait(sc, "c_bc")
        c.wait(sc, "exp", c.val("exp"))
        c.inc(sc.activation(negc, ps[0:128, 2048:2049], AF.Copy), "c_cp")

        if stop_after == f"P{p}qk":
            break
        c.wait(te, "k_l")
        c.wait(te, "k_sq", c.val("k_sq"))
        c.wait(te, "c_cp", c.val("c_cp"))
        c.wait(te, "d_r6")
        c.wait(te, "e2d", c.val("e2d"))

        def score_mms(h, qb, kc, b, out_list):
            base = 0 if h == hA else 64
            pb = ps[0:128, b * 512:(b + 1) * 512]
            kh = k_hi[base:base + 64, kc * 128:(kc + 1) * 128]
            kl = k_lo[base:base + 64, kc * 128:(kc + 1) * 128]
            qh = q_hi[base:base + 64, qb * 512:(qb + 1) * 512]
            ql = q_lo[base:base + 64, qb * 512:(qb + 1) * 512]
            out_list.append(lambda: c.inc(te.matmul(pb, kh, qh, start=True, stop=False), "sc_mm0"))
            out_list.append(lambda: c.inc(te.matmul(pb, kh, ql, start=False, stop=False), "sc_mm0"))
            out_list.append(lambda: c.inc(te.matmul(pb, kl, qh, start=False, stop=False), "sc_mm0"))
            lhs = lhsT6[base:base + 6, kc * 128:(kc + 1) * 128]
            r6m = rhs6m[base:base + 6, :]
            r6p = rhs6p[base:base + 6, :]
            q0, q1 = qb * 512, (qb + 1) * 512
            kstart, kend = kc * 128, kc * 128 + 128
            segs = []
            if q0 < min(kstart, q1):
                segs.append((q0, min(kstart, q1), r6p))
            dlo, dhi = max(q0, kstart), min(q1, kend)
            if dlo < dhi:
                segs.append((dlo, dhi, r6m))
            if max(q0, kend) < q1:
                segs.append((max(q0, kend), q1, r6m))
            for idx, (a0, a1, rsrc) in enumerate(segs):
                last = idx == len(segs) - 1
                out_list.append(lambda a0=a0, a1=a1, rsrc=rsrc, last=last:
                                c.inc(te.matmul(ps[0:128, b * 512 + (a0 - q0): b * 512 + (a1 - q0)],
                                                lhs, rsrc[:, a0:a1], start=False, stop=last),
                                      "sc_mm" if last else "sc_mm0"))

        def emit_pv(ptb, kc0, kc1):
            c.wait(te, "exp", c.val("exp"))
            c.wait(te, "pt_fix", c.val("pt_fix"))
            if kc0 == 0:
                c.wait(te, "at_b", c.val("at_b"))
            npv = 0
            for (h, kc, pcol) in ((hA, kc0, 0), (hA, kc1, 512), (hB, kc0, 1024), (hB, kc1, 1536)):
                b = 4 if h == hA else 5
                npv += 1
                c.inc(te.matmul(ps[0:65, b * 512:(b + 1) * 512],
                                vrr[:, kc, h, 0:65], ptb[:, pcol:pcol + 512],
                                start=(kc == 0), stop=(kc == KC - 1)),
                      "pv_g" if npv == 4 else "pv_mm0")

        def emit_norm(qb):
            c.wait(sc, "pv_g", c.val("pv_g"))
            c.wait(sc, "nb_a", c.val("nb_a"))
            c.wait(sc, "nb_b", c.val("nb_b"))
            c.inc(sc.activation(pt1[64:65, 0:512],
                                ps[64:65, 4 * 512:4 * 512 + 512], AF.Copy), "nr_a")
            c.inc(sc.activation(pt1[64:65, 512:1024],
                                ps[64:65, 5 * 512:5 * 512 + 512], AF.Copy), "nr_b")
            c.wait(te, "nr_a")
            c.wait(te, "nr_b")
            c.wait(te, "nc_b", c.val("nc_b"))
            c.inc(te.matmul(ps[0:64, 6 * 512:7 * 512], ones_row[64:65, 0:64],
                            pt1[64:65, 0:512], start=True, stop=True), "nb_a")
            c.inc(te.matmul(ps[0:64, 7 * 512:8 * 512], ones_row[64:65, 0:64],
                            pt1[64:65, 512:1024], start=True, stop=True), "nb_b")
            c.wait(ve, "nb_a")
            c.wait(ve, "nb_b")
            c.wait(ve, "at_b", c.val("at_b"))
            c.inc(ve.reciprocal(pt1[0:64, 0:512], ps[0:64, 6 * 512:7 * 512]), "nc_a")
            c.inc(ve.reciprocal(pt1[0:64, 512:1024], ps[0:64, 7 * 512:8 * 512]), "nc_b")
            c.wait(ve, "nc_a")
            c.wait(ve, "nc_b")
            c.inc(ve.tensor_mul(pt1[64:128, 0:512], ps[0:64, 4 * 512:5 * 512],
                                pt1[0:64, 0:512]), "at_a")
            c.inc(ve.tensor_mul(pt1[64:128, 512:1024], ps[0:64, 5 * 512:6 * 512],
                                pt1[0:64, 512:1024]), "at_b")
            c.wait(gp, "at_b")
            c.dinc(gp.dma_start(out=attn_scr[hA, qb], in_=pt1[64:128, 0:512]), "d_at")
            c.dinc(gp.dma_start(out=attn_scr[hB, qb], in_=pt1[64:128, 512:1024]), "d_at")
            drain_w(2)

        pending_norm = None
        for qb in range(QB):
            pending = None
            for kcg in range(KCG):
                kc0, kc1 = 2 * kcg, 2 * kcg + 1
                ptb = pt0 if kcg % 2 == 0 else pt1
                c.wait(te, "exp", c.val("exp"))
                mmsA, mmsB = [], []
                score_mms(hA, qb, kc0, 0, mmsA)
                score_mms(hA, qb, kc1, 1, mmsA)
                score_mms(hB, qb, kc0, 2, mmsB)
                score_mms(hB, qb, kc1, 3, mmsB)
                for i in range(max(len(mmsA), len(mmsB))):
                    if i < len(mmsA):
                        mmsA[i]()
                    if i < len(mmsB):
                        mmsB[i]()
                if kcg == 1 and pending_norm is not None:
                    emit_norm(pending_norm)
                    pending_norm = None
                if pending is not None:
                    emit_pv(*pending)
                c.wait(sc, "sc_mm")
                c.wait(sc, "c_cp")
                c.wait(sc, "pv_g", max(0, c.val("pv_g") - 1))
                c.wait(sc, "at_b", c.val("at_b"))
                c.wait(sc, "nc_b", c.val("nc_b"))
                c.wait(sc, "d_at", c.val("d_at"))
                c.inc(sc.activation(ptb, ps[0:128, 0:2048], AF.Exp, bias=negc[:, 0:1]), "exp")
                for (h, col, pcol) in ((hA, 0, 0), (hA, 1, 512), (hB, 0, 1024), (hB, 1, 1536)):
                    kc = 2 * kcg + col
                    if kc * 128 >= qb * 512 and (kc + 1) * 128 <= (qb + 1) * 512:
                        off = pcol + (kc * 128 - qb * 512)
                        c.wait(ve, "exp")
                        c.inc(ve.tensor_mul(ptb[:, off:off + 128], ptb[:, off:off + 128],
                                            e2diag[:, (0 if h == hA else 128):(128 if h == hA else 256)]),
                              "pt_fix")
                pending = (ptb, kc0, kc1)
                drain_w(2 if p < NP // 2 else 3)
            emit_pv(*pending)
            pending_norm = qb
            if stop_after == f"P{p}q{qb}":
                break
        if pending_norm is not None:
            emit_norm(pending_norm)
            pending_norm = None

        if stop_after is not None and stop_after.startswith(f"P{p}"):
            break

    if stop_after == "B" or (stop_after or "").startswith("P"):
        _early_finish()
        return nc
    c.wait(sy, "d_at", c.val("d_at"))
    c.wait(sy, "pv_g", c.val("pv_g"))
    c.wait(sy, "at_b", c.val("at_b"))
    c.wait(sy, "nc_b", c.val("nc_b"))
    c.wait(sy, "exp", c.val("exp"))
    c.wait(te, "exp", c.val("exp"))
    c.wait(te, "at_b", c.val("at_b"))
    c.wait(te, "nc_b", c.val("nc_b"))
    while next(wgen, _W_DONE) is not _W_DONE:
        pass
    c.wait(sy, "d_wto")
    c.wait(sy, "d_wtoB")
    c.wait(sy, "wt_m", c.val("wt_m"))
    phW.close()
    phB.close()

    phC = ExitStack()
    wo_hi = salloc(phC, "wo_hi", [128, DC * FC * 128], BF16)
    wo_lo = salloc(phC, "wo_lo", [128, DC * FC * 128], BF16)
    wo_st = salloc(phC, "wo_st", [128, 2048], F32)
    bo_bf = salloc(phC, "bo_bf", [128, 2 * D], BF16)

    c.dinc(sy.dma_start(out=wo_st[0:1, 0:D], in_=bo_in), "d_bo")
    c.wait(sc, "d_bo")
    split_row(wo_st[0:1, 0:D], bo_bf[0:1, 0:D], bo_bf[0:1, D:2 * D], "b_o")
    attn_f = salloc(phC, "attn_f", [128, DC * 512], F32)
    attn_h2 = salloc(phC, "attn_h2", [128, DC * 512], BF16)
    attn_l2 = salloc(phC, "attn_l2", [128, DC * 512], BF16)
    s_qb = salloc(phC, "s_qb", [128, FC * 512], F32)

    NWO = (DC * FC * 128) // 2048
    for i in range(NWO):
        c.wait(sy, "wo_sp", c.val("wo_sp"))
        c.wait(sy, "b_ov", c.val("b_ov"))
        c.dinc(sy.dma_start(out=wo_st, in_=wo_in[:, i * 2048:(i + 1) * 2048]), "d_wo")
        c.wait(sc, "d_wo")
        c.inc(sc.activation(wo_hi[:, i * 2048:(i + 1) * 2048], wo_st, AF.Copy), "wo_h")
        c.wait(ve, "wo_h")
        c.inc(ve.tensor_sub(wo_lo[:, i * 2048:(i + 1) * 2048], wo_st,
                            wo_hi[:, i * 2048:(i + 1) * 2048]), "wo_sp")

    for qb in range(QB):
        c.wait(sy, "c_sp", c.val("c_sp"))
        for dc in range(DC):
            c.dinc(sy.dma_start(out=attn_f[0:64, dc * 512:(dc + 1) * 512],
                                in_=attn_scr[2 * dc, qb]), "d_af")
            c.dinc(sy.dma_start(out=attn_f[64:128, dc * 512:(dc + 1) * 512],
                                in_=attn_scr[2 * dc + 1, qb]), "d_af")
        c.wait(sc, "d_af")
        c.wait(sc, "s_mm", c.val("s_mm"))
        c.inc(sc.activation(attn_h2, attn_f, AF.Copy), "c_h")
        c.wait(ve, "c_h")
        c.inc(ve.tensor_sub(attn_l2, attn_f, attn_h2), "c_sp")
        c.wait(ve, "s_w", c.val("s_w"))
        c.wait(te, "c_sp")
        c.wait(te, "wo_sp")
        c.wait(te, "b_o")
        for fc in range(FC):
            b = fc % 2
            c.wait(te, "s_ev", max(0, c.val("s_ev") - 1))
            c.inc(te.matmul(ps[0:128, b * 512:(b + 1) * 512], bo_bf[0:1, fc * 128:(fc + 1) * 128],
                            ones_bf[0:1, 0:512], start=True, stop=False), "s_mm0")
            c.inc(te.matmul(ps[0:128, b * 512:(b + 1) * 512], bo_bf[0:1, D + fc * 128: D + (fc + 1) * 128],
                            ones_bf[0:1, 0:512], start=False, stop=False), "s_mm0")
            for dc in range(DC):
                w_hi = wo_hi[:, (dc * FC + fc) * 128:(dc * FC + fc) * 128 + 128]
                w_lo = wo_lo[:, (dc * FC + fc) * 128:(dc * FC + fc) * 128 + 128]
                ah = attn_h2[:, dc * 512:(dc + 1) * 512]
                al = attn_l2[:, dc * 512:(dc + 1) * 512]
                last = dc == DC - 1
                c.inc(te.matmul(ps[0:128, b * 512:(b + 1) * 512], w_hi, ah,
                                start=False, stop=False), "s_mm0")
                c.inc(te.matmul(ps[0:128, b * 512:(b + 1) * 512], w_lo, ah,
                                start=False, stop=False), "s_mm0")
                c.inc(te.matmul(ps[0:128, b * 512:(b + 1) * 512], w_hi, al,
                                start=False, stop=last), "s_mm" if last else "s_mm0")
            c.wait(ve, "s_mm")
            c.inc(ve.tensor_copy(s_qb[:, fc * 512:(fc + 1) * 512],
                                 ps[0:128, b * 512:(b + 1) * 512]), "s_ev")
        c.wait(gp, "s_ev")
        for fc in range(FC):
            c.dinc(gp.dma_start(out=s_scr[fc, qb], in_=s_qb[:, fc * 512:(fc + 1) * 512]), "s_w")

    c.wait(sy, "s_w")
    if stop_after == "C":
        _early_finish()
        return nc
    phC.close()

    phD1 = ExitStack()
    wt_up = salloc(phD1, "wt_up", [128, DC * HC * 128], BF16)
    hpre = salloc(phD1, "hpre", [128, 2 * 512], F32)
    hful = salloc(phD1, "hful", [128, 2 * 512], F32)
    h2b = salloc(phD1, "h2b", [128, 2 * 512], F32)
    hhi = salloc(phD1, "hhi", [128, 2 * 512], BF16)
    hlo = salloc(phD1, "hlo", [128, 2 * 512], BF16)
    rfb = salloc(phD1, "rfb", [128, 512], F32)
    rowd = salloc(phD1, "rowd", [128, 1024], F32)

    c.dinc(sy.dma_start(out=wt_up, in_=wup_t_scr), "d_wtu")
    c.wait(te, "s_ev", c.val("s_ev"))

    for tb in range(TB):
        c.wait(te, "rfb_c", c.val("rfb_c"))
        c.wait(te, "r_rfd")
        c.inc(te.matmul(ps[0:128, 6 * 512:7 * 512], ones_row[0:1, 0:128],
                        rowbuf[0:1, tb * 512:(tb + 1) * 512], start=True, stop=True), "rfb_m")
        c.wait(sc, "rfb_m")
        c.wait(sc, "up_ev", c.val("up_ev"))
        c.inc(sc.activation(rfb, ps[0:128, 6 * 512:7 * 512], AF.Copy), "rfb_c")
        c.wait(te, "d_wtu")
        c.wait(te, "rfb_c")
        for hc in range(HC):
            b = hc % 2
            c.wait(te, "up_ev", max(0, c.val("up_ev") - 1))
            for dc in range(DC):
                wt = wt_up[:, (dc * HC + hc) * 128:(dc * HC + hc) * 128 + 128]
                xh = xn_hi[:, dc * T + tb * 512: dc * T + (tb + 1) * 512]
                xl = xn_lo[:, dc * T + tb * 512: dc * T + (tb + 1) * 512]
                c.inc(te.matmul(ps[0:128, b * 512:(b + 1) * 512], wt, xh,
                                start=(dc == 0), stop=False), "up_mm0")
                c.inc(te.matmul(ps[0:128, b * 512:(b + 1) * 512], wt, xl,
                                start=False, stop=(dc == DC - 1)),
                      "up_mm" if dc == DC - 1 else "up_mm0")
            c.wait(ve, "up_mm")
            c.wait(ve, "g_h", max(0, c.val("g_h") - 1))
            c.inc(ve.tensor_mul(hpre[:, b * 512:(b + 1) * 512],
                                ps[0:128, b * 512:(b + 1) * 512], rfb), "up_ev")
            c.wait(sc, "up_ev")
            c.wait(sc, "h_lo", max(0, c.val("h_lo") - 1))
            c.inc(sc.activation(hful[:, b * 512:(b + 1) * 512],
                                hpre[:, b * 512:(b + 1) * 512], AF.Gelu), "g_h")
            c.wait(sc, "g_h")
            c.wait(sc, "d_h" if b == 0 else "d_hB")
            c.inc(sc.activation(hhi[:, b * 512:(b + 1) * 512],
                                hful[:, b * 512:(b + 1) * 512], AF.Copy), "h_hi")
            c.wait(ve, "h_hi")
            c.inc(ve.tensor_sub(hlo[:, b * 512:(b + 1) * 512],
                                hful[:, b * 512:(b + 1) * 512],
                                hhi[:, b * 512:(b + 1) * 512]), "h_lo")
            c.wait(sc, "h2_rd", max(0, c.val("h2_rd") - 1))
            c.inc(sc.activation(h2b[:, b * 512:(b + 1) * 512],
                                hful[:, b * 512:(b + 1) * 512], AF.Square), "h_sq")
            c.wait(te, "h_sq")
            c.inc(te.matmul(ps[0:1, 7 * 512:7 * 512 + 512], ones_row[:, 0:1],
                            h2b[:, b * 512:(b + 1) * 512],
                            start=(hc == 0), stop=(hc == HC - 1)), "h2_rd")
            c.wait(gp, "h_lo")
            htag = "d_h" if b == 0 else "d_hB"
            c.dinc(gp.dma_start(out=h_scr_hi[tb][:, hc * 512:(hc + 1) * 512],
                                in_=hhi[:, b * 512:(b + 1) * 512]), htag)
            c.dinc(gp.dma_start(out=h_scr_lo[tb][:, hc * 512:(hc + 1) * 512],
                                in_=hlo[:, b * 512:(b + 1) * 512]), htag)
        c.wait(ve, "h2_rd")
        c.wait(ve, "rh_d", c.val("rh_d"))
        c.inc(ve.tensor_scalar_mul(rowd[0:1, 0:512], ps[0:1, 7 * 512:7 * 512 + 512],
                                   1.0 / DFF), "rh_1")
        c.wait(sc, "rh_1")
        c.inc(sc.activation(rowd[0:1, 512:1024], rowd[0:1, 0:512], AF.Sqrt), "rh_2")
        c.wait(ve, "rh_2")
        c.inc(ve.tensor_scalar_add(rowd[0:1, 0:512], rowd[0:1, 512:1024], 1e-8), "rh_3")
        c.wait(ve, "rh_3")
        c.inc(ve.reciprocal(rowd[0:1, 512:1024], rowd[0:1, 0:512]), "rh_4")
        c.wait(gp, "rh_4")
        c.dinc(gp.dma_start(out=rowbuf[32:33, tb * 512:(tb + 1) * 512],
                            in_=rowd[0:1, 512:1024]), "rh_d")

    c.wait(sy, "d_h")
    c.wait(sy, "d_hB")
    c.wait(sy, "rh_d")
    if stop_after == "D1":
        _early_finish()
        return nc
    phD1.close()
    xn_stack.close()

    phD2 = ExitStack()
    wt_dn = salloc(phD2, "wt_dn", [128, HC * FC * 128], BF16)
    h_ahi = salloc(phD2, "h_ahi", [128, HC * 512], BF16)
    h_alo = salloc(phD2, "h_alo", [128, HC * 512], BF16)
    gbuf = salloc(phD2, "gbuf", [128, 2 * 512], F32)
    xsbuf = salloc(phD2, "xsbuf", [128, 4 * 512], F32)
    rhb = salloc(phD2, "rhb", [128, 512], F32)

    c.dinc(sy.dma_start(out=wt_dn, in_=wdn_t_scr), "d_wtd")
    c.wait(te, "up_ev", c.val("up_ev"))
    c.wait(te, "rfb_c", c.val("rfb_c"))
    c.wait(te, "rh_1", c.val("rh_1"))

    for tb in range(TB):
        c.wait(sy, "dn_mm", c.val("dn_mm"))
        c.dinc(sy.dma_start(out=h_ahi, in_=h_scr_hi[tb]), "d_ha")
        c.dinc(sy.dma_start(out=h_alo, in_=h_scr_lo[tb]), "d_ha")
        c.wait(te, "rhb_c", c.val("rhb_c"))
        c.wait(te, "rh_d")
        c.inc(te.matmul(ps[0:128, 6 * 512:7 * 512], ones_row[32:33, 0:128],
                        rowbuf[32:33, tb * 512:(tb + 1) * 512], start=True, stop=True), "rhb_m")
        c.wait(sc, "rhb_m")
        c.wait(sc, "dn_ev", c.val("dn_ev"))
        c.inc(sc.activation(rhb, ps[0:128, 6 * 512:7 * 512], AF.Copy), "rhb_c")
        c.wait(te, "d_wtd")
        c.wait(te, "d_ha")
        c.wait(te, "rhb_c")
        for fc in range(FC):
            b = fc % 2
            c.wait(te, "dn_ev", max(0, c.val("dn_ev") - 1))
            for hc in range(HC):
                wt = wt_dn[:, (hc * FC + fc) * 128:(hc * FC + fc) * 128 + 128]
                hh = h_ahi[:, hc * 512:(hc + 1) * 512]
                hl = h_alo[:, hc * 512:(hc + 1) * 512]
                c.inc(te.matmul(ps[0:128, b * 512:(b + 1) * 512], wt, hh,
                                start=(hc == 0), stop=False), "dn_mm0")
                c.inc(te.matmul(ps[0:128, b * 512:(b + 1) * 512], wt, hl,
                                start=False, stop=(hc == HC - 1)),
                      "dn_mm" if hc == HC - 1 else "dn_mm0")
            xoff = 0 if fc % 2 == 0 else 1024
            c.wait(sy, "fin_o", max(0, c.val("fin_o") - 16))
            c.dinc(sy.dma_start(out=xsbuf[:, xoff:xoff + 512],
                                in_=x_in[:, fc * T + tb * 512: fc * T + (tb + 1) * 512]), "d_xs")
            c.dinc(sy.dma_start(out=xsbuf[:, xoff + 512:xoff + 1024],
                                in_=s_scr[fc, tb]), "d_xs")
            c.wait(ve, "dn_mm")
            c.wait(ve, "g_s", max(0, c.val("g_s") - 1))
            c.wait(ve, "fin_o", max(0, c.val("fin_o") - 16))
            c.inc(ve.tensor_mul(gbuf[:, b * 512:(b + 1) * 512],
                                ps[0:128, b * 512:(b + 1) * 512], rhb), "dn_ev")
            c.wait(sc, "dn_ev")
            c.wait(sc, "fin_m", max(0, c.val("fin_m") - 1))
            c.inc(sc.activation(gbuf[:, b * 512:(b + 1) * 512],
                                gbuf[:, b * 512:(b + 1) * 512], AF.Silu), "g_s")
            c.wait(ve, "g_s")
            c.wait(ve, "d_xs")
            c.inc(ve.tensor_mul(gbuf[:, b * 512:(b + 1) * 512],
                                gbuf[:, b * 512:(b + 1) * 512],
                                xsbuf[:, xoff + 512:xoff + 1024]), "fin_m")
            c.wait(ve, "fin_m")
            c.inc(ve.tensor_add(gbuf[:, b * 512:(b + 1) * 512],
                                gbuf[:, b * 512:(b + 1) * 512],
                                xsbuf[:, xoff:xoff + 512]), "fin_a")
            c.wait(gp, "fin_a")
            c.dinc(gp.dma_start(out=out_hbm[:, fc * T + tb * 512: fc * T + (tb + 1) * 512],
                                in_=gbuf[:, b * 512:(b + 1) * 512]), "fin_o")

    c.wait(gp, "fin_o")
    phD2.close()
    return nc


def _prep_core(xb, qkv_w, qkv_b, out_w, out_b, alpha, w_up, w_down,
               T=2048, D=1024, H=16, DFF=4096):
    DC, HC, FC, NP = D // 128, DFF // 128, D // 128, H // 2

    x_prep = np.ascontiguousarray(
        xb.T.reshape(DC, 128, T).transpose(1, 0, 2).reshape(128, DC * T))
    wq = np.empty((NP, 128, DC * 128), np.float32)
    wk = np.empty((NP, 128, DC * 128), np.float32)
    for p in range(NP):
        Wq = qkv_w[128 * p:128 * p + 128, :]
        Wk = qkv_w[D + 128 * p: D + 128 * p + 128, :]
        wq[p] = Wq.reshape(128, DC, 128).transpose(2, 1, 0).reshape(128, DC * 128)
        wk[p] = Wk.reshape(128, DC, 128).transpose(2, 1, 0).reshape(128, DC * 128)
    Wv = qkv_w[2 * D:3 * D, :]
    wv = Wv.reshape(D, DC, 128).transpose(2, 1, 0).reshape(128, DC * D)
    bq = np.ascontiguousarray(qkv_b[0:D].reshape(NP, 1, 128))
    bk = np.ascontiguousarray(qkv_b[D:2 * D].reshape(NP, 1, 128))
    bv = np.ascontiguousarray(qkv_b[2 * D:3 * D].reshape(1, D))
    wo = out_w.reshape(FC, 128, DC, 128).transpose(3, 2, 0, 1).reshape(128, DC * FC * 128)
    bo = np.ascontiguousarray(out_b.reshape(1, D))
    wup = w_up.reshape(HC, 128, DC, 128).transpose(3, 2, 0, 1).reshape(128, DC * HC * 128)
    wdn = w_down.reshape(FC, 128, HC, 128).transpose(3, 2, 0, 1).reshape(128, HC * FC * 128)
    al = np.ascontiguousarray(alpha.reshape(H, 1))
    return {
        "x_in": x_prep, "wq_in": np.ascontiguousarray(wq), "wk_in": np.ascontiguousarray(wk),
        "wv_in": np.ascontiguousarray(wv), "bq_in": bq, "bk_in": bk, "bv_in": bv,
        "wo_in": np.ascontiguousarray(wo), "bo_in": bo,
        "wup_in": np.ascontiguousarray(wup), "wdn_in": np.ascontiguousarray(wdn),
        "alpha_in": al,
    }


def kernel(L0_attn_x, L0_attn_qkv_w, L0_attn_qkv_b, L0_attn_out_w, L0_attn_out_b,
           L0_attn_decay_alpha, L0_ffn_w_up, L0_ffn_w_down):
    T, D, H, DFF = 2048, 1024, 16, 4096
    x = np.ascontiguousarray(np.asarray(L0_attn_x, dtype=np.float32))
    qkv_w = np.ascontiguousarray(np.asarray(L0_attn_qkv_w, dtype=np.float32))
    qkv_b = np.asarray(L0_attn_qkv_b, dtype=np.float32)
    out_w = np.ascontiguousarray(np.asarray(L0_attn_out_w, dtype=np.float32))
    out_b = np.asarray(L0_attn_out_b, dtype=np.float32)
    alpha = np.asarray(L0_attn_decay_alpha, dtype=np.float32)
    w_up = np.ascontiguousarray(np.asarray(L0_ffn_w_up, dtype=np.float32))
    w_down = np.ascontiguousarray(np.asarray(L0_ffn_w_down, dtype=np.float32))

    nc = build_program(T=T, D=D, H=H, DFF=DFF)
    in_maps = [
        _prep_core(x[b], qkv_w, qkv_b, out_w, out_b, alpha, w_up, w_down,
                   T=T, D=D, H=H, DFF=DFF)
        for b in range(N_CORES)
    ]
    res = run_bass_kernel_spmd(nc, in_maps, core_ids=list(range(N_CORES)))
    FC = D // 128
    out = np.empty((N_CORES, T, D), np.float32)
    for b in range(N_CORES):
        o = np.asarray(res.results[b]["out_hbm"])
        out[b] = o.reshape(128, FC, T).transpose(1, 0, 2).reshape(D, T).T
    return out


# version 13
# speedup vs baseline: 1.0243x; 1.0239x over previous
from contextlib import ExitStack

import numpy as np
import ml_dtypes

import concourse.bass as bass
import concourse.mybir as mybir
from concourse.bass_utils import run_bass_kernel_spmd

F32 = mybir.dt.float32
BF16 = mybir.dt.bfloat16
AF = mybir.ActivationFunctionType
ALU = mybir.AluOpType
AX = mybir.AxisListType

N_CORES = 8


_DED = {"pv_g", "vh_ev", "s_ev", "up_ev", "dn_ev", "g_h", "h_lo", "h2_rd",
        "d_h", "d_hB", "d_wto", "d_wtoB", "d_ws", "d_wsB", "d_wt", "d_wtB",
        "fin_o", "qn_rx", "kn_rx", "g_s", "fin_m", "con_g",
        "ws_abs", "ws_add", "wt_abs", "wt_g", "wt_m", "w_tcp",
        *[f"dx{l}" for l in "ABCDEFGH"], *[f"wtu{l}" for l in "ABCDEFGH"],
        *[f"wtd{l}" for l in "ABCDEFGH"], *[f"ha{l}" for l in "ABCDEFGH"]}

_GROUPS = {
    "g_te": ["tr_mm", "ln_sx", "ln_x2mm", "bc_m", "bc_r", "q_mm", "k_mm",
             "qn_mm", "kn_mm", "c_bc", "sc_mm", "vh_mm", "pv_mm", "s_mm",
             "up_mm", "dn_mm", "rfb_m", "rhb_m", "w_smm", "w_tbc",
             "nb_a", "nb_b"],
    "g_sc": ["al_e", "al_a", "al_h", "al_lb", "ab_h", "ab_lb", "mb_hi", "jr_h", "tr_cp",
             "q_h", "k_h", "c_s", "c_cp", "e2d", "exp",
             "nr_a", "nr_b", "wo_h", "wv_h", "c_h", "h_hi", "h_sq",
             "rfb_c", "rhb_c", "ws_abs", "wt_abs", "w_tcp", "ln_sq",
             "wq_h", "wk_h", "tr_1", "tr_2",
             "r_m2", "r_sq", "r_w", "xn_hic", "b_v", "b_o", "bqs", "bks",
             "rh_2"],
    "g_ve": ["con_v", "con_r", "al_n", "al_h2", "al_r", "al_l2", "mb_f",
             "mb_lo", "ab_z", "ab_h2", "ab_l", "ab_l2", "r6n", "r6z", "jr_1", "jr_l",
             "tr_3", "b_vv", "b_ov", "bqsv", "bksv",
             "r_m", "r_e2", "r_var", "r_tz", "r_rs", "r_t", "r_u", "r_u1", "r_v8",
             "r_rf", "xn_t1", "xn_f", "xn_lo", "wq_l", "wk_l", "q_l",
             "k_l", "q_sq", "k_sq", "r6a", "qn_m", "kn_m", "c_m", "c_n", "nc_a", "nc_b", "at_a",
             "at_b", "pt_fix", "vh_1", "wv_sp", "wo_sp", "c_sp", "wt_g",
             "wt_m", "wsz", "ws_add", "w_th", "rh_1", "rh_3",
             "rh_4", "fin_a"],
    "g_gp": ["d_l6", "d_r6", "d_at", "s_w", "rh_d", "r_rsd",
             "r_rfd", "ab_d"],
    "g_sy": ["d_x", "d_al", "d_al2", "d_bv", "d_bo", "d_wv", "d_wq", "d_wqs", "d_wk",
             "d_wo", "d_af", "d_wtu", "d_wtd", "d_ha", "d_xs"],
}
_TAG2SEM = {t: g for g, ts in _GROUPS.items() for t in ts}


class Ctx:

    def __init__(self, nc):
        self.nc = nc
        self.sems = {}
        self.cnt = {}

    def phys(self, name):
        if name in _DED:
            return name
        return _TAG2SEM[name]

    def S(self, name):
        ph = self.phys(name)
        if ph not in self.sems:
            self.sems[ph] = self.nc.alloc_semaphore(ph)
            self.cnt[ph] = 0
        return self.sems[ph]

    def inc(self, instr, name, n=1):
        if name.endswith("0"):
            return instr
        instr.then_inc(self.S(name), n)
        self.cnt[self.phys(name)] += n
        return instr

    def dinc(self, instr, name):
        self.inc(instr, name, 16)
        eng = self.engines[instr.ins.engine]
        eng.wait_ge(self.S(name), self.cnt[self.phys(name)])
        return instr

    def wait(self, eng, name, val=None):
        ph = self.phys(name)
        v = self.cnt.get(ph, 0) if val is None else val
        if v > 0:
            eng.wait_ge(self.S(name), v)

    def val(self, name):
        return self.cnt.get(self.phys(name), 0)


def build_program(T=2048, D=1024, H=16, DFF=4096, stop_after=None):
    HD = 64
    NP = H // 2
    DC = D // 128
    KC = T // 128
    QB = T // 512
    HC = DFF // 128
    FC = DC
    TB = QB
    QCOL = T // 512
    KCG = KC // 2

    nc = bass.Bass("TRN2", target_bir_lowering=False)
    c = Ctx(nc)
    sy, te, ve, sc, gp = nc.sync, nc.tensor, nc.vector, nc.scalar, nc.gpsimd
    c.engines = {e.engine: e for e in (sy, te, ve, sc, gp)}

    x_in = nc.dram_tensor("x_in", [128, DC * T], F32, kind="ExternalInput").ap()
    wq_in = nc.dram_tensor("wq_in", [NP, 128, DC * 128], F32, kind="ExternalInput").ap()
    wk_in = nc.dram_tensor("wk_in", [NP, 128, DC * 128], F32, kind="ExternalInput").ap()
    wv_in = nc.dram_tensor("wv_in", [128, DC * D], F32, kind="ExternalInput").ap()
    bq_in = nc.dram_tensor("bq_in", [NP, 1, 128], F32, kind="ExternalInput").ap()
    bk_in = nc.dram_tensor("bk_in", [NP, 1, 128], F32, kind="ExternalInput").ap()
    bv_in = nc.dram_tensor("bv_in", [1, D], F32, kind="ExternalInput").ap()
    wo_in = nc.dram_tensor("wo_in", [128, DC * FC * 128], F32, kind="ExternalInput").ap()
    bo_in = nc.dram_tensor("bo_in", [1, D], F32, kind="ExternalInput").ap()
    wup_in = nc.dram_tensor("wup_in", [128, DC * HC * 128], F32, kind="ExternalInput").ap()
    wdn_in = nc.dram_tensor("wdn_in", [128, HC * FC * 128], F32, kind="ExternalInput").ap()
    alpha_in = nc.dram_tensor("alpha_in", [H, 1], F32, kind="ExternalInput").ap()

    attn_scr = nc.dram_tensor("attn_scr", [H, QB, 64, 512], F32, kind="Internal").ap()
    s_scr = nc.dram_tensor("s_scr", [FC, TB, 128, 512], F32, kind="Internal").ap()
    h_scr_hi = nc.dram_tensor("h_scr_hi", [TB, 128, HC * 512], BF16, kind="Internal").ap()
    h_scr_lo = nc.dram_tensor("h_scr_lo", [TB, 128, HC * 512], BF16, kind="Internal").ap()
    wup_t_scr = nc.dram_tensor("wup_t_scr", [128, DC * HC * 128], BF16, kind="Internal").ap()
    wdn_t_scr = nc.dram_tensor("wdn_t_scr", [128, HC * FC * 128], BF16, kind="Internal").ap()

    out_hbm = nc.dram_tensor("out_hbm", [128, FC * T], F32, kind="ExternalOutput").ap()

    ps = nc.alloc_psum_tensor("ps", [128, 4096], F32).ap()

    ones_row = nc.alloc_sbuf_tensor("ones_row", [128, 128], F32).ap()
    ones_bf = nc.alloc_sbuf_tensor("ones_bf", [128, 512], BF16).ap()
    lhsT6 = nc.alloc_sbuf_tensor("lhsT6", [128, T], BF16).ap()
    Mb = nc.alloc_sbuf_tensor("Mb", [128, T], BF16).ap()
    R_relu = nc.alloc_sbuf_tensor("R_relu", [128, 128], F32).ap()
    alpha_c = nc.alloc_sbuf_tensor("alpha_c", [16, 8], F32).ap()
    tiny_bf = nc.alloc_sbuf_tensor("tiny_bf", [16, 8], BF16).ap()
    arow = nc.alloc_sbuf_tensor("arow", [128, 96], F32).ap()
    a4 = nc.alloc_sbuf_tensor("a4", [128, 16], F32).ap()
    neg2a = nc.alloc_sbuf_tensor("neg2a", [128, 16], F32).ap()
    rowbuf = nc.alloc_sbuf_tensor("rowbuf", [128, T], F32).ap()
    thr = nc.alloc_sbuf_tensor("thr", [128, 2], F32).ap()
    smalls = nc.alloc_sbuf_tensor("smalls", [128, 16], F32).ap()
    negc = nc.alloc_sbuf_tensor("negc", [128, 1], F32).ap()
    bq_bf = nc.alloc_sbuf_tensor("bq_bf", [128, 256], BF16).ap()
    bk_bf = nc.alloc_sbuf_tensor("bk_bf", [128, 256], BF16).ap()
    e2diag = nc.alloc_sbuf_tensor("e2diag", [128, 256], F32).ap()

    def salloc(stack, name, shape, dtype):
        return stack.enter_context(nc.sbuf_tensor(name, list(shape), dtype)).ap()


    def _early_finish():
        c.wait(gp, "sc_mm", c.val("sc_mm"))
        c.dinc(gp.dma_start(out=out_hbm[0:1, 0:4], in_=ones_row[0:1, 0:4]), "fin_o")
        c.wait(gp, "fin_o")
    xn_stack = ExitStack()
    xn_hi = salloc(xn_stack, "xn_hi", [128, DC * T], BF16)
    xn_lo = salloc(xn_stack, "xn_lo", [128, DC * T], BF16)

    def split_row(src_f32_row0, hi_row0, lo_row0, tag):
        c.inc(sc.activation(hi_row0, src_f32_row0, AF.Copy), tag)
        c.wait(ve, tag)
        c.inc(ve.tensor_sub(lo_row0, src_f32_row0, hi_row0), tag + "v")

    c.inc(ve.memset(ones_row, 1.0), "con_v")
    c.inc(ve.memset(ones_bf, 1.0), "con_v")
    c.inc(gp.iota(R_relu, pattern=[[-1, 128]], base=0, channel_multiplier=1,
                  allow_small_or_imprecise_dtypes=True), "con_g")
    c.wait(ve, "con_g", 1)
    c.inc(ve.tensor_scalar_max(R_relu, R_relu, 0.0), "con_r")

    phA = ExitStack()
    x_res = salloc(phA, "x_res", [128, DC * T], F32)
    scrA = salloc(phA, "scrA", [128, T], F32)
    scrA2 = salloc(phA, "scrA2", [128, T], F32)
    iota16 = salloc(phA, "iota16", [16, T], F32)
    jrows = salloc(phA, "jrows", [128, T], BF16)
    jrows2 = salloc(phA, "jrows2", [128, T], BF16)
    jrows3 = salloc(phA, "jrows3", [128, T], BF16)

    for ch, l in enumerate("ABCDEFGH"):
        w8 = (DC * T) // 8
        c.inc(sy.dma_start(out=x_res[:, ch * w8:(ch + 1) * w8],
                           in_=x_in[:, ch * w8:(ch + 1) * w8]), f"dx{l}", 16)
    c.dinc(sy.dma_start(out=alpha_c[0:H, 0:1], in_=alpha_in[0:H, 0:1]), "d_al")

    c.inc(gp.iota(iota16, pattern=[[1, T]], base=0, channel_multiplier=0,
                  allow_small_or_imprecise_dtypes=True), "con_g")

    c.wait(sc, "d_al")
    c.inc(sc.activation(alpha_c[0:H, 1:2], alpha_c[0:H, 0:1], AF.Exp), "al_e")
    c.wait(sc, "al_e")
    c.inc(sc.activation(alpha_c[0:H, 2:3], alpha_c[0:H, 1:2], AF.Ln, bias=1.0), "al_a")
    c.wait(ve, "al_a")
    c.inc(ve.tensor_scalar_mul(alpha_c[0:H, 3:4], alpha_c[0:H, 2:3], -1.0), "al_n")
    c.wait(sc, "al_n")
    c.inc(sc.activation(tiny_bf[0:H, 0:1], alpha_c[0:H, 2:3], AF.Copy), "al_h")
    c.wait(ve, "al_h")
    c.inc(ve.tensor_copy(alpha_c[0:H, 4:5], tiny_bf[0:H, 0:1]), "al_h2")
    c.wait(ve, "al_h2")
    c.inc(ve.tensor_sub(alpha_c[0:H, 5:6], alpha_c[0:H, 2:3], alpha_c[0:H, 4:5]), "al_r")
    c.wait(sc, "al_r")
    c.inc(sc.activation(tiny_bf[0:H, 1:2], alpha_c[0:H, 5:6], AF.Copy), "al_lb")
    c.wait(ve, "al_lb")
    c.inc(ve.tensor_copy(alpha_c[0:H, 6:7], tiny_bf[0:H, 1:2]), "al_l2")

    c.wait(ve, "con_g")
    c.wait(ve, "al_n")
    c.inc(ve.tensor_scalar_mul(scrA[0:H, 0:T], iota16[0:H, :], alpha_c[0:H, 3:4]), "mb_f")
    c.wait(sc, "mb_f")
    c.inc(sc.activation(Mb[0:H, :], scrA[0:H, 0:T], AF.Copy), "mb_hi")
    c.wait(ve, "mb_hi")
    c.inc(ve.tensor_sub(Mb[32:32 + H, :], scrA[0:H, 0:T], Mb[0:H, :]), "mb_lo")
    c.dinc(sy.dma_start(out=arow[0:1, 0:H],
                        in_=alpha_in.rearrange("h one -> one h")), "d_al2")
    c.wait(sc, "d_al2")
    c.inc(sc.activation(arow[0:1, 16:16 + H], arow[0:1, 0:H], AF.Exp), "tr_1")
    c.wait(sc, "tr_1")
    c.inc(sc.activation(arow[0:1, 32:32 + H], arow[0:1, 16:16 + H], AF.Ln, bias=1.0), "tr_2")
    c.wait(ve, "tr_2")
    c.inc(ve.tensor_scalar_mul(arow[0:1, 48:48 + H], arow[0:1, 32:32 + H], -2.0), "tr_3")
    c.wait(te, "tr_3")
    c.wait(te, "con_v")
    c.inc(te.matmul(ps[0:128, 0:H], ones_row[0:1, 0:128], arow[0:1, 48:48 + H],
                    start=True, stop=True), "tr_mm")
    c.wait(sc, "tr_mm")
    c.inc(sc.activation(neg2a[:, 0:H], ps[0:128, 0:H], AF.Copy), "tr_cp")


    c.inc(ve.memset(arow[0:1, 64:96], 0.0), "ab_z")
    c.wait(sc, "tr_2")
    c.wait(sc, "ab_z")
    c.inc(sc.activation(jrows[0:1, 0:H], arow[0:1, 32:32 + H], AF.Copy), "ab_h")
    c.wait(ve, "ab_h")
    c.inc(ve.tensor_copy(arow[0:1, 64:64 + H], jrows[0:1, 0:H]), "ab_h2")
    c.wait(ve, "ab_h2")
    c.inc(ve.tensor_sub(arow[0:1, 16:16 + H], arow[0:1, 32:32 + H], arow[0:1, 64:64 + H]), "ab_l")
    c.wait(sc, "ab_l")
    c.inc(sc.activation(jrows[0:1, 0:H], arow[0:1, 16:16 + H], AF.Copy), "ab_lb")
    c.wait(ve, "ab_lb")
    c.inc(ve.tensor_copy(arow[0:1, 80:80 + H], jrows[0:1, 0:H]), "ab_l2")
    c.wait(gp, "ab_l2")
    for r, src0 in ((0, 64), (1, 64), (2, 80), (3, 80)):
        c.dinc(gp.dma_start(out=a4[r:r + 1, 0:16], in_=arow[0:1, src0:src0 + 16]), "ab_d")
    c.dinc(gp.dma_start(out=a4[64:68, 0:16], in_=a4[0:4, 0:16]), "ab_d")

    c.wait(ve, "ab_l2")
    c.inc(ve.memset(jrows[0:1, :], 1.0), "jr_1")
    c.wait(sc, "con_g")
    c.wait(sc, "jr_1")
    c.inc(sc.activation(jrows2[0:1, :], iota16[0:1, :], AF.Copy), "jr_h")
    c.wait(ve, "jr_h")
    c.inc(ve.tensor_sub(jrows3[0:1, :], iota16[0:1, :], jrows2[0:1, :]), "jr_l")
    c.wait(gp, "jr_1")
    c.wait(gp, "jr_l")
    for r in (4, 5, 12, 13):
        c.dinc(gp.dma_start(out=lhsT6[r:r + 1, :], in_=jrows[0:1, :]), "d_l6")
    for r in (0, 2, 8, 10):
        c.dinc(gp.dma_start(out=lhsT6[r:r + 1, :], in_=jrows2[0:1, :]), "d_l6")
    for r in (1, 3, 9, 11):
        c.dinc(gp.dma_start(out=lhsT6[r:r + 1, :], in_=jrows3[0:1, :]), "d_l6")
    c.dinc(gp.dma_start(out=lhsT6[64:70, :], in_=lhsT6[8:14, :]), "d_l6")

    for l in "ABCDEFGH":
        c.wait(te, f"dx{l}", 16)
    c.wait(te, "tr_cp")
    for dc in range(DC):
        for j in range(QCOL):
            c.inc(te.matmul(ps[0:1, j * 512:(j + 1) * 512], ones_row[:, 0:1],
                            x_res[:, dc * T + j * 512: dc * T + (j + 1) * 512],
                            start=(dc == 0), stop=(dc == DC - 1)),
                  "ln_sx" if dc == DC - 1 else "ln_mm0")
    for dc in range(DC):
        buf = scrA if dc % 2 == 0 else scrA2
        for l in "ABCDEFGH":
            c.wait(sc, f"dx{l}", 16)
        c.wait(sc, "mb_lo")
        if dc >= 2:
            c.wait(sc, "ln_x2mm")
        c.inc(sc.activation(buf[:, 0:T], x_res[:, dc * T:(dc + 1) * T], AF.Square), "ln_sq")
        c.wait(te, "ln_sq")
        for j in range(QCOL):
            c.inc(te.matmul(ps[0:1, 2048 + j * 512: 2048 + (j + 1) * 512], ones_row[:, 0:1],
                            buf[:, j * 512:(j + 1) * 512],
                            start=(dc == 0), stop=(dc == DC - 1)), "ln_x2mm")
    c.wait(ve, "ln_sx")
    c.wait(ve, "ln_x2mm")
    c.inc(ve.tensor_scalar_mul(rowbuf[0:1, 0:T], ps[0:1, 0:T], 1.0 / D), "r_m")
    c.inc(ve.tensor_scalar_mul(scrA2[0:1, 0:T], ps[0:1, 2048:2048 + T], 1.0 / D), "r_e2")
    c.wait(sc, "r_m")
    c.wait(sc, "ln_sq")
    c.inc(sc.activation(scrA[0:1, 0:T], rowbuf[0:1, 0:T], AF.Square), "r_m2")
    c.wait(ve, "r_m2")
    c.wait(ve, "r_e2")
    c.inc(ve.tensor_sub(scrA2[0:1, 0:T], scrA2[0:1, 0:T], scrA[0:1, 0:T]), "r_var")
    c.wait(ve, "r_var")
    c.wait(ve, "jr_h")
    c.wait(ve, "ab_l2")
    c.inc(ve.tensor_scalar_add(iota16[0:1, 0:T], scrA2[0:1, 0:T], 1e-5), "r_tz")
    c.wait(sc, "r_tz")
    c.inc(sc.activation(scrA[0:1, 0:T], iota16[0:1, 0:T], AF.Sqrt), "r_sq")
    c.wait(ve, "r_sq")
    c.inc(ve.reciprocal(iota16[0:1, 0:T], scrA[0:1, 0:T]), "r_rs")

    c.wait(te, "r_m2")
    c.wait(te, "r_e2")
    for j in range(QCOL):
        c.inc(te.matmul(ps[0:128, j * 512:(j + 1) * 512], ones_row[0:1, 0:128],
                        rowbuf[0:1, j * 512:(j + 1) * 512], start=True, stop=True), "bc_m")
    c.wait(te, "r_rs")
    for j in range(QCOL):
        c.inc(te.matmul(ps[0:128, 2048 + j * 512:2048 + (j + 1) * 512], ones_row[0:1, 0:128],
                        iota16[0:1, j * 512:(j + 1) * 512], start=True, stop=True), "bc_r")

    c.wait(ve, "r_rs")
    c.inc(ve.tensor_scalar_add(scrA[0:1, 0:T], scrA2[0:1, 0:T], 1e-5), "r_t")
    c.wait(ve, "r_t")
    c.wait(ve, "bc_r", c.val("bc_r"))
    c.inc(ve.reciprocal(iota16[0:1, 0:T], scrA[0:1, 0:T]), "r_u1")
    c.wait(ve, "r_u1")
    c.inc(ve.tensor_mul(scrA[0:1, 0:T], scrA2[0:1, 0:T], iota16[0:1, 0:T]), "r_u")
    c.wait(sc, "r_u")
    c.inc(sc.activation(scrA2[0:1, 0:T], scrA[0:1, 0:T], AF.Sqrt), "r_w")
    c.wait(ve, "r_w")
    c.inc(ve.tensor_scalar_add(scrA[0:1, 0:T], scrA2[0:1, 0:T], 1e-8), "r_v8")
    c.wait(ve, "r_v8")
    c.inc(ve.reciprocal(scrA2[0:1, 0:T], scrA[0:1, 0:T]), "r_rf")
    c.wait(gp, "r_rf")
    c.wait(gp, "bc_m", c.val("bc_m"))
    c.dinc(gp.dma_start(out=rowbuf[0:1, 0:T], in_=scrA2[0:1, 0:T]), "r_rfd")

    c.wait(ve, "bc_m")
    c.wait(ve, "bc_r")
    c.wait(ve, "r_rf")
    c.wait(ve, "r_rfd", c.val("r_rfd"))
    for dc in range(DC):
        if dc >= 1:
            c.wait(ve, "xn_lo")
        c.inc(ve.tensor_sub(scrA[:, 0:T], x_res[:, dc * T:(dc + 1) * T], ps[0:128, 0:T]), "xn_t1")
        c.wait(ve, "xn_t1")
        c.inc(ve.tensor_mul(scrA2[:, 0:T], scrA[:, 0:T], ps[0:128, 2048:2048 + T]), "xn_f")
        c.wait(sc, "xn_f")
        c.inc(sc.activation(xn_hi[:, dc * T:(dc + 1) * T], scrA2[:, 0:T], AF.Copy), "xn_hic")
        c.wait(ve, "xn_hic")
        c.inc(ve.tensor_sub(xn_lo[:, dc * T:(dc + 1) * T], scrA2[:, 0:T],
                            xn_hi[:, dc * T:(dc + 1) * T]), "xn_lo")

    c.wait(sy, "xn_lo")
    if stop_after == "A":
        _early_finish()
        return nc
    phA.close()

    phB = ExitStack()
    v_hat = salloc(phB, "v_hat", [128, KC * H * 65], F32)
    pt0 = salloc(phB, "pt0", [128, 2048], F32)
    phB0 = ExitStack()
    wv_hi = salloc(phB0, "wv_hi", [128, DC * D], BF16)
    wv_lo = salloc(phB0, "wv_lo", [128, DC * D], BF16)
    bv_bf = salloc(phB0, "bv_bf", [128, 2 * D], BF16)

    vrr = v_hat.rearrange("p (kc h e) -> p kc h e", kc=KC, h=H)

    c.dinc(sy.dma_start(out=v_hat[0:1, 0:D], in_=bv_in), "d_bv")
    c.wait(sc, "d_bv")
    split_row(v_hat[0:1, 0:D], bv_bf[0:1, 0:D], bv_bf[0:1, D:2 * D], "b_v")

    for dc in range(DC):
        c.wait(sy, "wv_sp")
        half = (dc % 2) * D
        c.dinc(sy.dma_start(out=pt0[:, half:half + D], in_=wv_in[:, dc * D:(dc + 1) * D]), "d_wv")
        c.wait(sc, "d_wv")
        c.inc(sc.activation(wv_hi[:, dc * D:(dc + 1) * D], pt0[:, half:half + D], AF.Copy), "wv_h")
        c.wait(ve, "wv_h")
        c.inc(ve.tensor_sub(wv_lo[:, dc * D:(dc + 1) * D], pt0[:, half:half + D],
                            wv_hi[:, dc * D:(dc + 1) * D]), "wv_sp")

    if stop_after == "B0a":
        _early_finish()
        return nc
    c.inc(ve.memset(vrr[:, :, :, 64:65], 1.0), "vh_1")
    if stop_after == "B0b":
        _early_finish()
        return nc

    NVH = D // 512
    c.wait(te, "wv_sp")
    c.wait(te, "b_v")
    vh_bank = 0
    for kc in range(KC):
        for vh in range(NVH):
            b = vh_bank % 2
            vh_bank += 1
            c.wait(te, "vh_ev", max(0, c.val("vh_ev") - 1))
            c.inc(te.matmul(ps[0:128, b * 512:(b + 1) * 512], ones_bf[0:1, 0:128],
                            bv_bf[0:1, vh * 512:(vh + 1) * 512], start=True, stop=False), "vh_mm0")
            c.inc(te.matmul(ps[0:128, b * 512:(b + 1) * 512], ones_bf[0:1, 0:128],
                            bv_bf[0:1, D + vh * 512: D + (vh + 1) * 512], start=False, stop=False), "vh_mm0")
            for dc in range(DC):
                xh = xn_hi[:, dc * T + kc * 128: dc * T + kc * 128 + 128]
                xl = xn_lo[:, dc * T + kc * 128: dc * T + kc * 128 + 128]
                wh = wv_hi[:, dc * D + vh * 512: dc * D + (vh + 1) * 512]
                wl = wv_lo[:, dc * D + vh * 512: dc * D + (vh + 1) * 512]
                last = dc == DC - 1
                c.inc(te.matmul(ps[0:128, b * 512:(b + 1) * 512], xh, wh, start=False, stop=False), "vh_mm0")
                c.inc(te.matmul(ps[0:128, b * 512:(b + 1) * 512], xh, wl, start=False, stop=False), "vh_mm0")
                c.inc(te.matmul(ps[0:128, b * 512:(b + 1) * 512], xl, wh, start=False,
                                stop=last), "vh_mm" if last else "vh_mm0")
            if stop_after == "B0c":
                continue
            c.wait(ve, "vh_mm")
            c.wait(ve, "vh_1")
            c.wait(ve, "b_vv", c.val("b_vv"))
            nheads = 512 // HD
            dst = vrr[:, kc, vh * nheads:(vh + 1) * nheads, 0:64]
            src = ps[0:128, b * 512:(b + 1) * 512].rearrange("p (h e) -> p h e", e=64)
            c.inc(ve.tensor_copy(dst, src), "vh_ev")

    c.wait(te, "d_l6")
    phB0.close()


    pt1 = salloc(phB, "pt1", [128, 2048], F32)
    q_hi = salloc(phB, "q_hi", [128, T], BF16)
    q_lo = salloc(phB, "q_lo", [128, T], BF16)
    k_hi = salloc(phB, "k_hi", [128, T], BF16)
    k_lo = salloc(phB, "k_lo", [128, T], BF16)
    rhs6m = salloc(phB, "rhs6m", [128, T], BF16)
    rhs6p = salloc(phB, "rhs6p", [128, T], BF16)
    c.wait(ve, "vh_mm", c.val("vh_mm"))
    c.wait(ve, "vh_ev", c.val("vh_ev"))
    c.inc(ve.memset(rhs6m, 0.0), "r6z")
    c.inc(ve.memset(rhs6p, 0.0), "r6z")
    wqk_hi = salloc(phB, "wqk_hi", [128, DC * 128], BF16)
    wqk_lo = salloc(phB, "wqk_lo", [128, DC * 128], BF16)
    wstage = salloc(phB, "wstage", [128, DC * 128 + 256], F32)
    if stop_after in ("B0", "B0c"):
        _early_finish()
        return nc

    phW = ExitStack()
    WP = 512
    wstg = salloc(phW, "wstg", [128, 2 * WP], F32)
    wabs = salloc(phW, "wabs", [128, WP], F32)
    wsgn = salloc(phW, "wsgn", [128, WP], BF16)
    wmask = salloc(phW, "wmask", [128, WP], BF16)
    wtout = salloc(phW, "wtout", [128, WP], BF16)
    wsum = salloc(phW, "wsum", [128, 8], F32)
    NPC_UP = (DC * HC * 128) // WP
    NPC_DN = (HC * FC * 128) // WP
    w_state = {"stats_done": False, "thr_done": False}

    def w_units():
        c.wait(sy, "vh_mm", c.val("vh_mm"))
        c.wait(sy, "wv_sp", c.val("wv_sp"))
        c.wait(ve, "vh_mm", c.val("vh_mm"))
        c.inc(ve.memset(wsum[:, 0:2], 0.0), "wsz")
        c.wait(sc, "wsz")
        yield
        for (win, sumcol, npieces) in ((wup_in, 0, NPC_UP), (wdn_in, 1, NPC_DN)):
            for i in range(npieces):
                half = (i % 2) * WP
                dtag = "d_ws" if i % 2 == 0 else "d_wsB"
                c.wait(sy, "ws_abs", max(0, c.val("ws_abs") - 1))
                c.dinc(sy.dma_start(out=wstg[:, half:half + WP],
                                    in_=win[:, i * WP:(i + 1) * WP]), dtag)
                yield
                c.wait(sc, dtag)
                c.wait(sc, "ws_add", c.val("ws_add"))
                c.inc(sc.activation(wabs, wstg[:, half:half + WP], AF.Abs,
                                    accum_out=wsum[:, 4 + (i % 2):5 + (i % 2)]), "ws_abs")
                c.wait(ve, "ws_abs")
                c.inc(ve.tensor_add(wsum[:, sumcol:sumcol + 1], wsum[:, sumcol:sumcol + 1],
                                    wsum[:, 4 + (i % 2):5 + (i % 2)]), "ws_add")
                yield
        w_state["stats_done"] = True
        yield
        c.wait(sy, "ws_abs", c.val("ws_abs"))
        for (win, wtscr, thcol, npieces) in ((wup_in, wup_t_scr, 0, NPC_UP),
                                             (wdn_in, wdn_t_scr, 1, NPC_DN)):
            for i in range(npieces):
                half = (i % 2) * WP
                dtag = "d_wt" if i % 2 == 0 else "d_wtB"
                c.wait(sy, "wt_abs", max(0, c.val("wt_abs") - 2))
                c.dinc(sy.dma_start(out=wstg[:, half:half + WP],
                                    in_=win[:, i * WP:(i + 1) * WP]), dtag)
                yield
                c.wait(sc, dtag)
                c.wait(sc, "wt_g", c.val("wt_g"))
                c.wait(sc, "wt_m", c.val("wt_m"))
                c.inc(sc.activation(wabs, wstg[:, half:half + WP], AF.Abs), "wt_abs")
                c.inc(sc.activation(wsgn, wstg[:, half:half + WP], AF.Sign), "wt_abs")
                yield
                c.wait(ve, "wt_abs")
                c.wait(ve, "w_tcp", 1)
                c.inc(ve.tensor_single_scalar(wmask, wabs, thr[:, thcol:thcol + 1],
                                              ALU.is_gt), "wt_g")
                c.wait(ve, "wt_g")
                otag = "d_wto" if i % 2 == 0 else "d_wtoB"
                c.wait(ve, "d_wto", c.val("d_wto"))
                c.wait(ve, "d_wtoB", c.val("d_wtoB"))
                c.inc(ve.tensor_mul(wtout[:, 0:WP], wsgn, wmask), "wt_m")
                c.wait(gp, "wt_m")
                c.dinc(gp.dma_start(out=wtscr[:, i * WP:(i + 1) * WP],
                                    in_=wtout[:, 0:WP]), otag)
                yield

    wgen = w_units()
    _W_DONE = object()

    def drain_w(n):
        for _ in range(n):
            if w_state["stats_done"] and not w_state["thr_done"]:
                return
            if next(wgen, _W_DONE) is _W_DONE:
                return

    def qk_psum(bias_t, bias_tag, tag):
        c.wait(te, bias_tag)
        for j in range(QCOL):
            c.inc(te.matmul(ps[0:128, j * 512:(j + 1) * 512], bias_t[0:1, 0:128],
                            ones_bf[0:1, 0:512], start=True, stop=False), tag + "0")
            c.inc(te.matmul(ps[0:128, j * 512:(j + 1) * 512], bias_t[0:1, 128:256],
                            ones_bf[0:1, 0:512], start=False, stop=False), tag + "0")
            for dc in range(DC):
                xh = xn_hi[:, dc * T + j * 512: dc * T + (j + 1) * 512]
                xl = xn_lo[:, dc * T + j * 512: dc * T + (j + 1) * 512]
                wh = wqk_hi[:, dc * 128:(dc + 1) * 128]
                wl = wqk_lo[:, dc * 128:(dc + 1) * 128]
                last = dc == DC - 1
                c.inc(te.matmul(ps[0:128, j * 512:(j + 1) * 512], wh, xh,
                                start=False, stop=False), tag + "0")
                c.inc(te.matmul(ps[0:128, j * 512:(j + 1) * 512], wl, xh,
                                start=False, stop=False), tag + "0")
                c.inc(te.matmul(ps[0:128, j * 512:(j + 1) * 512], wh, xl,
                                start=False, stop=last), tag if last else tag + "0")

    c.dinc(sy.dma_start(out=wstage[:, 0:DC * 128], in_=wq_in[0]), "d_wqs")
    c.dinc(sy.dma_start(out=wstage[0:1, DC * 128:DC * 128 + 128], in_=bq_in[0]), "d_wqs")
    c.dinc(sy.dma_start(out=wstage[0:1, DC * 128 + 128:DC * 128 + 256], in_=bk_in[0]), "d_wqs")
    c.wait(sc, "d_wqs")
    c.wait(sc, "vh_mm", c.val("vh_mm"))
    c.inc(sc.activation(wqk_hi, wstage[:, 0:DC * 128], AF.Copy), "wq_h")
    c.wait(ve, "wq_h")
    c.wait(ve, "vh_mm", c.val("vh_mm"))
    c.inc(ve.tensor_sub(wqk_lo, wstage[:, 0:DC * 128], wqk_hi), "wq_l")
    split_row(wstage[0:1, DC * 128:DC * 128 + 128], bq_bf[0:1, 0:128], bq_bf[0:1, 128:256], "bqs")
    split_row(wstage[0:1, DC * 128 + 128:DC * 128 + 256], bk_bf[0:1, 0:128], bk_bf[0:1, 128:256], "bks")

    for p in range(NP):
        hA, hB = 2 * p, 2 * p + 1
        if p == NP // 2:
            while not w_state["stats_done"]:
                next(wgen)
            c.wait(te, "exp", c.val("exp"))
            c.wait(te, "at_b", c.val("at_b"))
            c.wait(te, "nc_b", c.val("nc_b"))
            c.wait(te, "ws_add")
            c.inc(te.matmul(ps[0:1, 0:2], ones_row[:, 0:1], wsum[:, 0:2],
                            start=True, stop=True), "w_smm")
            c.wait(ve, "w_smm")
            c.inc(ve.tensor_scalar_mul(wsum[0:1, 2:4], ps[0:1, 0:2], 0.5 / (D * DFF)), "w_th")
            c.wait(te, "w_th")
            c.inc(te.matmul(ps[0:128, 512:514], ones_row[0:1, 0:128], wsum[0:1, 2:4],
                            start=True, stop=True), "w_tbc")
            c.wait(sc, "w_tbc")
            c.inc(sc.activation(thr[:, 0:2], ps[0:128, 512:514], AF.Copy), "w_tcp")
            c.wait(te, "w_tcp")
            w_state["thr_done"] = True

        c.wait(gp, "sc_mm", c.val("sc_mm"))
        c.wait(gp, "vh_ev", c.val("vh_ev"))
        c.wait(gp, "mb_lo", c.val("mb_lo"))
        c.wait(gp, "ab_d", c.val("ab_d"))
        c.wait(gp, "r6n", c.val("r6n"))
        c.wait(gp, "r6z", c.val("r6z"))
        for (dstrow, srcrow) in ((4, 0 + hA), (5, 32 + hA), (12, 0 + hB), (13, 32 + hB)):
            c.dinc(gp.dma_start(out=rhs6m[dstrow:dstrow + 1, :],
                                in_=Mb[srcrow:srcrow + 1, :]), "d_r6")
        c.dinc(gp.dma_start(out=rhs6m[68:70, :], in_=rhs6m[12:14, :]), "d_r6")
        c.wait(ve, "d_r6", c.val("d_r6"))
        c.wait(ve, "sc_mm", c.val("sc_mm"))
        c.inc(ve.tensor_scalar(rhs6m[0:4, :], rhs6m[0:4, :], 0.0, a4[0:4, hA:hA + 1],
                               ALU.mult, ALU.add), "r6a")
        c.inc(ve.tensor_scalar(rhs6m[64:68, :], rhs6m[64:68, :], 0.0, a4[64:68, hB:hB + 1],
                               ALU.mult, ALU.add), "r6a")
        if stop_after == f"P{p}w3":
            break
        c.wait(ve, "r6a")
        c.inc(ve.tensor_scalar_mul(rhs6p[0:6, :], rhs6m[0:6, :], -1.0), "r6n")
        c.inc(ve.tensor_scalar_mul(rhs6p[64:70, :], rhs6m[64:70, :], -1.0), "r6n")

        if stop_after == f"P{p}w":
            break
        c.wait(sc, "pt_fix", c.val("pt_fix"))
        c.wait(sc, "tr_cp")
        c.wait(sc, "con_r")
        c.inc(sc.activation(e2diag[:, 0:128], R_relu, AF.Exp, scale=neg2a[:, hA:hA + 1]), "e2d")
        c.inc(sc.activation(e2diag[:, 128:256], R_relu, AF.Exp, scale=neg2a[:, hB:hB + 1]), "e2d")
        drain_w(4)

        c.wait(te, "vh_ev")
        c.wait(te, "exp", c.val("exp"))
        c.wait(te, "wq_l")
        qk_psum(bq_bf, "bqs", "q_mm")
        if stop_after == f"P{p}qm":
            break
        c.wait(sc, "q_mm")
        c.wait(sc, "sc_mm", c.val("sc_mm"))
        c.inc(sc.activation(q_hi, ps[0:128, 0:T], AF.Copy, scale=0.125), "q_h")
        c.wait(ve, "q_h")
        c.inc(ve.scalar_tensor_tensor(q_lo, ps[0:128, 0:T], 0.125, q_hi,
                                      ALU.mult, ALU.subtract), "q_l")
        c.wait(ve, "pv_g", c.val("pv_g"))
        c.wait(ve, "d_at", c.val("d_at"))
        if stop_after == f"P{p}qs":
            break
        c.inc(ve.tensor_mul(pt0[:, 0:T], q_hi, q_hi), "q_sq")
        if stop_after == f"P{p}n1":
            break

        c.wait(sy, "pv_g", c.val("pv_g"))
        c.wait(sy, "at_b", c.val("at_b"))
        c.wait(sy, "nc_b", c.val("nc_b"))
        c.wait(sy, "d_at", c.val("d_at"))
        if p == 0:
            c.wait(sy, "vh_mm", c.val("vh_mm"))
            c.wait(sy, "vh_ev", c.val("vh_ev"))
        c.dinc(sy.dma_start(out=pt1[:, 0:DC * 128], in_=wk_in[p]), "d_wk")
        c.wait(sc, "d_wk")
        c.wait(sc, "q_mm", c.val("q_mm"))
        c.inc(sc.activation(wqk_hi, pt1[:, 0:DC * 128], AF.Copy), "wk_h")
        c.wait(ve, "wk_h")
        c.inc(ve.tensor_sub(wqk_lo, pt1[:, 0:DC * 128], wqk_hi), "wk_l")
        c.wait(te, "q_l")
        c.wait(te, "wk_l")
        qk_psum(bk_bf, "bks", "k_mm")
        c.wait(sc, "k_mm")
        c.inc(sc.activation(k_hi, ps[0:128, 0:T], AF.Copy), "k_h")
        c.wait(ve, "k_h")
        c.inc(ve.tensor_sub(k_lo, ps[0:128, 0:T], k_hi), "k_l")

        if p + 1 < NP:
            c.wait(sy, "wq_l", c.val("wq_l"))
            c.wait(sy, "bqsv", c.val("bqsv"))
            c.wait(sy, "bksv", c.val("bksv"))
            c.dinc(sy.dma_start(out=wstage[:, 0:DC * 128], in_=wq_in[p + 1]), "d_wqs")
            c.dinc(sy.dma_start(out=wstage[0:1, DC * 128:DC * 128 + 128],
                                in_=bq_in[p + 1]), "d_wqs")
            c.dinc(sy.dma_start(out=wstage[0:1, DC * 128 + 128:DC * 128 + 256],
                                in_=bk_in[p + 1]), "d_wqs")
            c.wait(sc, "d_wqs")
            c.wait(sc, "k_mm")
            c.inc(sc.activation(wqk_hi, wstage[:, 0:DC * 128], AF.Copy), "wq_h")
            c.wait(ve, "wq_h")
            c.inc(ve.tensor_sub(wqk_lo, wstage[:, 0:DC * 128], wqk_hi), "wq_l")
            split_row(wstage[0:1, DC * 128:DC * 128 + 128],
                      bq_bf[0:1, 0:128], bq_bf[0:1, 128:256], "bqs")
            split_row(wstage[0:1, DC * 128 + 128:DC * 128 + 256],
                      bk_bf[0:1, 0:128], bk_bf[0:1, 128:256], "bks")
        if stop_after == f"P{p}w1":
            break

        for j in range(QCOL):
            c.wait(te, "q_sq")
            c.wait(te, "at_b", c.val("at_b"))
            c.wait(te, "qn_rx", max(0, c.val("qn_rx") - 1))
            c.inc(te.matmul(ps[0:1, 2048 + (j % 2) * 512: 2048 + (j % 2) * 512 + 512],
                            ones_row[:, 0:1], pt0[:, j * 512:(j + 1) * 512],
                            start=True, stop=True), "qn_mm")
            if stop_after == f"P{p}n2":
                break
            c.wait(ve, "qn_mm")
            c.inc(ve.reduce_max(smalls[0:1, j:j + 1],
                                ps[0:1, 2048 + (j % 2) * 512: 2048 + (j % 2) * 512 + 512],
                                axis=AX.X), "qn_rx")
        if stop_after in (f"P{p}n2", f"P{p}n3"):
            break
        c.wait(ve, "qn_rx")
        c.inc(ve.reduce_max(smalls[0:1, 8:9], smalls[0:1, 0:QCOL], axis=AX.X), "qn_m")
        if stop_after == f"P{p}qn":
            break

        c.wait(ve, "qn_mm", c.val("qn_mm"))
        c.inc(ve.tensor_mul(pt0[:, 0:T], k_hi, k_hi), "k_sq")
        c.wait(ve, "qn_m")
        for j in range(QCOL):
            c.wait(te, "k_sq")
            c.wait(te, "qn_rx", c.val("qn_rx"))
            c.wait(te, "kn_rx", max(0, c.val("kn_rx") - 1))
            c.inc(te.matmul(ps[0:1, 2048 + (j % 2) * 512: 2048 + (j % 2) * 512 + 512],
                            ones_row[:, 0:1], pt0[:, j * 512:(j + 1) * 512],
                            start=True, stop=True), "kn_mm")
            c.wait(ve, "kn_mm")
            c.inc(ve.reduce_max(smalls[0:1, j:j + 1],
                                ps[0:1, 2048 + (j % 2) * 512: 2048 + (j % 2) * 512 + 512],
                                axis=AX.X), "kn_rx")
        c.wait(ve, "kn_rx")
        c.inc(ve.reduce_max(smalls[0:1, 9:10], smalls[0:1, 0:QCOL], axis=AX.X), "kn_m")
        drain_w(4)
        if stop_after == f"P{p}k":
            break

        c.wait(sc, "qn_m")
        c.wait(sc, "kn_m")
        c.inc(sc.activation(smalls[0:1, 10:11], smalls[0:1, 8:9], AF.Sqrt), "c_s")
        c.inc(sc.activation(smalls[0:1, 11:12], smalls[0:1, 9:10], AF.Sqrt), "c_s")
        c.wait(ve, "c_s")
        c.inc(ve.tensor_mul(smalls[0:1, 12:13], smalls[0:1, 10:11], smalls[0:1, 11:12]), "c_m")
        c.wait(ve, "c_m")
        c.inc(ve.tensor_scalar_mul(smalls[0:1, 13:14], smalls[0:1, 12:13], -1.02), "c_n")
        c.wait(te, "c_n")
        c.wait(te, "kn_rx", c.val("kn_rx"))
        c.inc(te.matmul(ps[0:128, 2048:2049], ones_row[0:1, 0:128], smalls[0:1, 13:14],
                        start=True, stop=True), "c_bc")
        c.wait(sc, "c_bc")
        c.wait(sc, "exp", c.val("exp"))
        c.inc(sc.activation(negc, ps[0:128, 2048:2049], AF.Copy), "c_cp")

        if stop_after == f"P{p}qk":
            break
        c.wait(te, "k_l")
        c.wait(te, "k_sq", c.val("k_sq"))
        c.wait(te, "c_cp", c.val("c_cp"))
        c.wait(te, "d_r6")
        c.wait(te, "e2d", c.val("e2d"))

        def score_mms(h, qb, kc, b, out_list):
            base = 0 if h == hA else 64
            pb = ps[0:128, b * 512:(b + 1) * 512]
            kh = k_hi[base:base + 64, kc * 128:(kc + 1) * 128]
            kl = k_lo[base:base + 64, kc * 128:(kc + 1) * 128]
            qh = q_hi[base:base + 64, qb * 512:(qb + 1) * 512]
            ql = q_lo[base:base + 64, qb * 512:(qb + 1) * 512]
            out_list.append(lambda: c.inc(te.matmul(pb, kh, qh, start=True, stop=False), "sc_mm0"))
            out_list.append(lambda: c.inc(te.matmul(pb, kh, ql, start=False, stop=False), "sc_mm0"))
            out_list.append(lambda: c.inc(te.matmul(pb, kl, qh, start=False, stop=False), "sc_mm0"))
            lhs = lhsT6[base:base + 6, kc * 128:(kc + 1) * 128]
            r6m = rhs6m[base:base + 6, :]
            r6p = rhs6p[base:base + 6, :]
            q0, q1 = qb * 512, (qb + 1) * 512
            kstart, kend = kc * 128, kc * 128 + 128
            segs = []
            if q0 < min(kstart, q1):
                segs.append((q0, min(kstart, q1), r6p))
            dlo, dhi = max(q0, kstart), min(q1, kend)
            if dlo < dhi:
                segs.append((dlo, dhi, r6m))
            if max(q0, kend) < q1:
                segs.append((max(q0, kend), q1, r6m))
            for idx, (a0, a1, rsrc) in enumerate(segs):
                last = idx == len(segs) - 1
                out_list.append(lambda a0=a0, a1=a1, rsrc=rsrc, last=last:
                                c.inc(te.matmul(ps[0:128, b * 512 + (a0 - q0): b * 512 + (a1 - q0)],
                                                lhs, rsrc[:, a0:a1], start=False, stop=last),
                                      "sc_mm" if last else "sc_mm0"))

        def emit_pv(ptb, kc0, kc1):
            c.wait(te, "exp", c.val("exp"))
            c.wait(te, "pt_fix", c.val("pt_fix"))
            if kc0 == 0:
                c.wait(te, "at_b", c.val("at_b"))
            npv = 0
            for (h, kc, pcol) in ((hA, kc0, 0), (hA, kc1, 512), (hB, kc0, 1024), (hB, kc1, 1536)):
                b = 4 if h == hA else 5
                npv += 1
                c.inc(te.matmul(ps[0:65, b * 512:(b + 1) * 512],
                                vrr[:, kc, h, 0:65], ptb[:, pcol:pcol + 512],
                                start=(kc == 0), stop=(kc == KC - 1)),
                      "pv_g" if npv == 4 else "pv_mm0")

        for qb in range(QB):
            pending = None
            for kcg in range(KCG):
                kc0, kc1 = 2 * kcg, 2 * kcg + 1
                ptb = pt0 if kcg % 2 == 0 else pt1
                c.wait(te, "exp", c.val("exp"))
                mmsA, mmsB = [], []
                score_mms(hA, qb, kc0, 0, mmsA)
                score_mms(hA, qb, kc1, 1, mmsA)
                score_mms(hB, qb, kc0, 2, mmsB)
                score_mms(hB, qb, kc1, 3, mmsB)
                for i in range(max(len(mmsA), len(mmsB))):
                    if i < len(mmsA):
                        mmsA[i]()
                    if i < len(mmsB):
                        mmsB[i]()
                if pending is not None:
                    emit_pv(*pending)
                c.wait(sc, "sc_mm")
                c.wait(sc, "c_cp")
                c.wait(sc, "pv_g", max(0, c.val("pv_g") - 1))
                c.wait(sc, "at_b", c.val("at_b"))
                c.wait(sc, "nc_b", c.val("nc_b"))
                c.wait(sc, "d_at", c.val("d_at"))
                c.inc(sc.activation(ptb, ps[0:128, 0:2048], AF.Exp, bias=negc[:, 0:1]), "exp")
                for (h, col, pcol) in ((hA, 0, 0), (hA, 1, 512), (hB, 0, 1024), (hB, 1, 1536)):
                    kc = 2 * kcg + col
                    if kc * 128 >= qb * 512 and (kc + 1) * 128 <= (qb + 1) * 512:
                        off = pcol + (kc * 128 - qb * 512)
                        c.wait(ve, "exp")
                        c.inc(ve.tensor_mul(ptb[:, off:off + 128], ptb[:, off:off + 128],
                                            e2diag[:, (0 if h == hA else 128):(128 if h == hA else 256)]),
                              "pt_fix")
                pending = (ptb, kc0, kc1)
                drain_w(2 if p < NP // 2 else 3)
            emit_pv(*pending)

            c.wait(sc, "pv_g", c.val("pv_g"))
            c.wait(sc, "nb_a", c.val("nb_a"))
            c.wait(sc, "nb_b", c.val("nb_b"))
            c.inc(sc.activation(pt1[64:65, 0:512],
                                ps[64:65, 4 * 512:4 * 512 + 512], AF.Copy), "nr_a")
            c.inc(sc.activation(pt1[64:65, 512:1024],
                                ps[64:65, 5 * 512:5 * 512 + 512], AF.Copy), "nr_b")
            c.wait(te, "nr_a")
            c.wait(te, "nr_b")
            c.wait(te, "nc_b", c.val("nc_b"))
            c.inc(te.matmul(ps[0:64, 6 * 512:7 * 512], ones_row[64:65, 0:64],
                            pt1[64:65, 0:512], start=True, stop=True), "nb_a")
            c.inc(te.matmul(ps[0:64, 7 * 512:8 * 512], ones_row[64:65, 0:64],
                            pt1[64:65, 512:1024], start=True, stop=True), "nb_b")
            c.wait(ve, "nb_a")
            c.wait(ve, "nb_b")
            c.wait(ve, "at_b", c.val("at_b"))
            c.inc(ve.reciprocal(pt1[0:64, 0:512], ps[0:64, 6 * 512:7 * 512]), "nc_a")
            c.inc(ve.reciprocal(pt1[0:64, 512:1024], ps[0:64, 7 * 512:8 * 512]), "nc_b")
            c.wait(ve, "nc_a")
            c.wait(ve, "nc_b")
            c.inc(ve.tensor_mul(pt0[0:64, 0:512], ps[0:64, 4 * 512:5 * 512],
                                pt1[0:64, 0:512]), "at_a")
            c.inc(ve.tensor_mul(pt0[0:64, 512:1024], ps[0:64, 5 * 512:6 * 512],
                                pt1[0:64, 512:1024]), "at_b")
            c.wait(gp, "at_b")
            c.dinc(gp.dma_start(out=attn_scr[hA, qb], in_=pt0[0:64, 0:512]), "d_at")
            c.dinc(gp.dma_start(out=attn_scr[hB, qb], in_=pt0[0:64, 512:1024]), "d_at")
            drain_w(2)
            if stop_after == f"P{p}q{qb}":
                break

        if stop_after is not None and stop_after.startswith(f"P{p}"):
            break

    if stop_after == "B" or (stop_after or "").startswith("P"):
        _early_finish()
        return nc
    c.wait(sy, "d_at", c.val("d_at"))
    c.wait(sy, "pv_g", c.val("pv_g"))
    c.wait(sy, "at_b", c.val("at_b"))
    c.wait(sy, "nc_b", c.val("nc_b"))
    c.wait(sy, "exp", c.val("exp"))
    c.wait(te, "exp", c.val("exp"))
    c.wait(te, "at_b", c.val("at_b"))
    c.wait(te, "nc_b", c.val("nc_b"))
    while next(wgen, _W_DONE) is not _W_DONE:
        pass
    c.wait(sy, "d_wto")
    c.wait(sy, "d_wtoB")
    c.wait(sy, "wt_m", c.val("wt_m"))
    phW.close()
    phB.close()

    phC = ExitStack()
    wo_hi = salloc(phC, "wo_hi", [128, DC * FC * 128], BF16)
    wo_lo = salloc(phC, "wo_lo", [128, DC * FC * 128], BF16)
    wo_st = salloc(phC, "wo_st", [128, 2048], F32)
    bo_bf = salloc(phC, "bo_bf", [128, 2 * D], BF16)

    c.dinc(sy.dma_start(out=wo_st[0:1, 0:D], in_=bo_in), "d_bo")
    c.wait(sc, "d_bo")
    split_row(wo_st[0:1, 0:D], bo_bf[0:1, 0:D], bo_bf[0:1, D:2 * D], "b_o")
    attn_f = salloc(phC, "attn_f", [128, DC * 512], F32)
    attn_h2 = salloc(phC, "attn_h2", [128, DC * 512], BF16)
    attn_l2 = salloc(phC, "attn_l2", [128, DC * 512], BF16)
    s_qb = salloc(phC, "s_qb", [128, FC * 512], F32)

    NWO = (DC * FC * 128) // 2048
    for i in range(NWO):
        c.wait(sy, "wo_sp", c.val("wo_sp"))
        c.wait(sy, "b_ov", c.val("b_ov"))
        c.dinc(sy.dma_start(out=wo_st, in_=wo_in[:, i * 2048:(i + 1) * 2048]), "d_wo")
        c.wait(sc, "d_wo")
        c.inc(sc.activation(wo_hi[:, i * 2048:(i + 1) * 2048], wo_st, AF.Copy), "wo_h")
        c.wait(ve, "wo_h")
        c.inc(ve.tensor_sub(wo_lo[:, i * 2048:(i + 1) * 2048], wo_st,
                            wo_hi[:, i * 2048:(i + 1) * 2048]), "wo_sp")

    for qb in range(QB):
        c.wait(sy, "c_sp", c.val("c_sp"))
        for dc in range(DC):
            c.dinc(sy.dma_start(out=attn_f[0:64, dc * 512:(dc + 1) * 512],
                                in_=attn_scr[2 * dc, qb]), "d_af")
            c.dinc(sy.dma_start(out=attn_f[64:128, dc * 512:(dc + 1) * 512],
                                in_=attn_scr[2 * dc + 1, qb]), "d_af")
        c.wait(sc, "d_af")
        c.wait(sc, "s_mm", c.val("s_mm"))
        c.inc(sc.activation(attn_h2, attn_f, AF.Copy), "c_h")
        c.wait(ve, "c_h")
        c.inc(ve.tensor_sub(attn_l2, attn_f, attn_h2), "c_sp")
        c.wait(ve, "s_w", c.val("s_w"))
        c.wait(te, "c_sp")
        c.wait(te, "wo_sp")
        c.wait(te, "b_o")
        for fc in range(FC):
            b = fc % 2
            c.wait(te, "s_ev", max(0, c.val("s_ev") - 1))
            c.inc(te.matmul(ps[0:128, b * 512:(b + 1) * 512], bo_bf[0:1, fc * 128:(fc + 1) * 128],
                            ones_bf[0:1, 0:512], start=True, stop=False), "s_mm0")
            c.inc(te.matmul(ps[0:128, b * 512:(b + 1) * 512], bo_bf[0:1, D + fc * 128: D + (fc + 1) * 128],
                            ones_bf[0:1, 0:512], start=False, stop=False), "s_mm0")
            for dc in range(DC):
                w_hi = wo_hi[:, (dc * FC + fc) * 128:(dc * FC + fc) * 128 + 128]
                w_lo = wo_lo[:, (dc * FC + fc) * 128:(dc * FC + fc) * 128 + 128]
                ah = attn_h2[:, dc * 512:(dc + 1) * 512]
                al = attn_l2[:, dc * 512:(dc + 1) * 512]
                last = dc == DC - 1
                c.inc(te.matmul(ps[0:128, b * 512:(b + 1) * 512], w_hi, ah,
                                start=False, stop=False), "s_mm0")
                c.inc(te.matmul(ps[0:128, b * 512:(b + 1) * 512], w_lo, ah,
                                start=False, stop=False), "s_mm0")
                c.inc(te.matmul(ps[0:128, b * 512:(b + 1) * 512], w_hi, al,
                                start=False, stop=last), "s_mm" if last else "s_mm0")
            c.wait(ve, "s_mm")
            c.inc(ve.tensor_copy(s_qb[:, fc * 512:(fc + 1) * 512],
                                 ps[0:128, b * 512:(b + 1) * 512]), "s_ev")
        c.wait(gp, "s_ev")
        for fc in range(FC):
            c.dinc(gp.dma_start(out=s_scr[fc, qb], in_=s_qb[:, fc * 512:(fc + 1) * 512]), "s_w")

    c.wait(sy, "s_w")
    if stop_after == "C":
        _early_finish()
        return nc
    phC.close()

    phD1 = ExitStack()
    wt_up = salloc(phD1, "wt_up", [128, DC * HC * 128], BF16)
    hpre = salloc(phD1, "hpre", [128, 2 * 512], F32)
    hful = salloc(phD1, "hful", [128, 2 * 512], F32)
    h2b = salloc(phD1, "h2b", [128, 2 * 512], F32)
    hhi = salloc(phD1, "hhi", [128, 2 * 512], BF16)
    hlo = salloc(phD1, "hlo", [128, 2 * 512], BF16)
    rfb = salloc(phD1, "rfb", [128, 512], F32)
    rowd = salloc(phD1, "rowd", [128, 1024], F32)

    for ch, l in enumerate("ABCDEFGH"):
        w8 = (DC * HC * 128) // 8
        c.inc(sy.dma_start(out=wt_up[:, ch * w8:(ch + 1) * w8],
                           in_=wup_t_scr[:, ch * w8:(ch + 1) * w8]), f"wtu{l}", 16)
    c.wait(te, "s_ev", c.val("s_ev"))

    for tb in range(TB):
        c.wait(te, "rfb_c", c.val("rfb_c"))
        c.wait(te, "r_rfd")
        c.inc(te.matmul(ps[0:128, 6 * 512:7 * 512], ones_row[0:1, 0:128],
                        rowbuf[0:1, tb * 512:(tb + 1) * 512], start=True, stop=True), "rfb_m")
        c.wait(sc, "rfb_m")
        c.wait(sc, "up_ev", c.val("up_ev"))
        c.inc(sc.activation(rfb, ps[0:128, 6 * 512:7 * 512], AF.Copy), "rfb_c")
        for l in "ABCDEFGH":
            c.wait(te, f"wtu{l}", 16)
        c.wait(te, "rfb_c")
        for hc in range(HC):
            b = hc % 2
            c.wait(te, "up_ev", max(0, c.val("up_ev") - 1))
            for dc in range(DC):
                wt = wt_up[:, (dc * HC + hc) * 128:(dc * HC + hc) * 128 + 128]
                xh = xn_hi[:, dc * T + tb * 512: dc * T + (tb + 1) * 512]
                xl = xn_lo[:, dc * T + tb * 512: dc * T + (tb + 1) * 512]
                c.inc(te.matmul(ps[0:128, b * 512:(b + 1) * 512], wt, xh,
                                start=(dc == 0), stop=False), "up_mm0")
                c.inc(te.matmul(ps[0:128, b * 512:(b + 1) * 512], wt, xl,
                                start=False, stop=(dc == DC - 1)),
                      "up_mm" if dc == DC - 1 else "up_mm0")
            c.wait(ve, "up_mm")
            c.wait(ve, "g_h", max(0, c.val("g_h") - 1))
            c.inc(ve.tensor_mul(hpre[:, b * 512:(b + 1) * 512],
                                ps[0:128, b * 512:(b + 1) * 512], rfb), "up_ev")
            c.wait(sc, "up_ev")
            c.wait(sc, "h_lo", max(0, c.val("h_lo") - 1))
            c.inc(sc.activation(hful[:, b * 512:(b + 1) * 512],
                                hpre[:, b * 512:(b + 1) * 512], AF.Gelu), "g_h")
            c.wait(sc, "g_h")
            c.wait(sc, "d_h" if b == 0 else "d_hB")
            c.inc(sc.activation(hhi[:, b * 512:(b + 1) * 512],
                                hful[:, b * 512:(b + 1) * 512], AF.Copy), "h_hi")
            c.wait(ve, "h_hi")
            c.inc(ve.tensor_sub(hlo[:, b * 512:(b + 1) * 512],
                                hful[:, b * 512:(b + 1) * 512],
                                hhi[:, b * 512:(b + 1) * 512]), "h_lo")
            c.wait(sc, "h2_rd", max(0, c.val("h2_rd") - 1))
            c.inc(sc.activation(h2b[:, b * 512:(b + 1) * 512],
                                hful[:, b * 512:(b + 1) * 512], AF.Square), "h_sq")
            c.wait(te, "h_sq")
            c.inc(te.matmul(ps[0:1, 7 * 512:7 * 512 + 512], ones_row[:, 0:1],
                            h2b[:, b * 512:(b + 1) * 512],
                            start=(hc == 0), stop=(hc == HC - 1)), "h2_rd")
            c.wait(gp, "h_lo")
            htag = "d_h" if b == 0 else "d_hB"
            c.dinc(gp.dma_start(out=h_scr_hi[tb][:, hc * 512:(hc + 1) * 512],
                                in_=hhi[:, b * 512:(b + 1) * 512]), htag)
            c.dinc(gp.dma_start(out=h_scr_lo[tb][:, hc * 512:(hc + 1) * 512],
                                in_=hlo[:, b * 512:(b + 1) * 512]), htag)
        c.wait(ve, "h2_rd")
        c.wait(ve, "rh_d", c.val("rh_d"))
        c.inc(ve.tensor_scalar_mul(rowd[0:1, 0:512], ps[0:1, 7 * 512:7 * 512 + 512],
                                   1.0 / DFF), "rh_1")
        c.wait(sc, "rh_1")
        c.inc(sc.activation(rowd[0:1, 512:1024], rowd[0:1, 0:512], AF.Sqrt), "rh_2")
        c.wait(ve, "rh_2")
        c.inc(ve.tensor_scalar_add(rowd[0:1, 0:512], rowd[0:1, 512:1024], 1e-8), "rh_3")
        c.wait(ve, "rh_3")
        c.inc(ve.reciprocal(rowd[0:1, 512:1024], rowd[0:1, 0:512]), "rh_4")
        c.wait(gp, "rh_4")
        c.dinc(gp.dma_start(out=rowbuf[32:33, tb * 512:(tb + 1) * 512],
                            in_=rowd[0:1, 512:1024]), "rh_d")

    c.wait(sy, "d_h")
    c.wait(sy, "d_hB")
    c.wait(sy, "rh_d")
    if stop_after == "D1":
        _early_finish()
        return nc
    phD1.close()
    xn_stack.close()

    phD2 = ExitStack()
    wt_dn = salloc(phD2, "wt_dn", [128, HC * FC * 128], BF16)
    h_ahi = salloc(phD2, "h_ahi", [128, HC * 512], BF16)
    h_alo = salloc(phD2, "h_alo", [128, HC * 512], BF16)
    gbuf = salloc(phD2, "gbuf", [128, 2 * 512], F32)
    xsbuf = salloc(phD2, "xsbuf", [128, 4 * 512], F32)
    rhb = salloc(phD2, "rhb", [128, 512], F32)

    for ch, l in enumerate("ABCDEFGH"):
        w8 = (HC * FC * 128) // 8
        c.inc(sy.dma_start(out=wt_dn[:, ch * w8:(ch + 1) * w8],
                           in_=wdn_t_scr[:, ch * w8:(ch + 1) * w8]), f"wtd{l}", 16)
    c.wait(te, "up_ev", c.val("up_ev"))
    c.wait(te, "rfb_c", c.val("rfb_c"))
    c.wait(te, "rh_1", c.val("rh_1"))

    for tb in range(TB):
        c.wait(sy, "dn_mm", c.val("dn_mm"))
        for l in "ABCDEFGH":
            c.wait(sy, f"ha{l}", 16 * tb)
        for ch in range(4):
            w4 = (HC * 512) // 4
            c.inc(sy.dma_start(out=h_ahi[:, ch * w4:(ch + 1) * w4],
                               in_=h_scr_hi[tb][:, ch * w4:(ch + 1) * w4]),
                  "ha" + "ABCDEFGH"[2 * ch], 16)
            c.inc(sy.dma_start(out=h_alo[:, ch * w4:(ch + 1) * w4],
                               in_=h_scr_lo[tb][:, ch * w4:(ch + 1) * w4]),
                  "ha" + "ABCDEFGH"[2 * ch + 1], 16)
        c.wait(te, "rhb_c", c.val("rhb_c"))
        c.wait(te, "rh_d")
        c.inc(te.matmul(ps[0:128, 6 * 512:7 * 512], ones_row[32:33, 0:128],
                        rowbuf[32:33, tb * 512:(tb + 1) * 512], start=True, stop=True), "rhb_m")
        c.wait(sc, "rhb_m")
        c.wait(sc, "dn_ev", c.val("dn_ev"))
        c.inc(sc.activation(rhb, ps[0:128, 6 * 512:7 * 512], AF.Copy), "rhb_c")
        for l in "ABCDEFGH":
            c.wait(te, f"wtd{l}", 16)
        for l in "ABCDEFGH":
            c.wait(te, f"ha{l}", 16 * (tb + 1))
        c.wait(te, "rhb_c")
        for fc in range(FC):
            b = fc % 2
            c.wait(te, "dn_ev", max(0, c.val("dn_ev") - 1))
            for hc in range(HC):
                wt = wt_dn[:, (hc * FC + fc) * 128:(hc * FC + fc) * 128 + 128]
                hh = h_ahi[:, hc * 512:(hc + 1) * 512]
                hl = h_alo[:, hc * 512:(hc + 1) * 512]
                c.inc(te.matmul(ps[0:128, b * 512:(b + 1) * 512], wt, hh,
                                start=(hc == 0), stop=False), "dn_mm0")
                c.inc(te.matmul(ps[0:128, b * 512:(b + 1) * 512], wt, hl,
                                start=False, stop=(hc == HC - 1)),
                      "dn_mm" if hc == HC - 1 else "dn_mm0")
            xoff = 0 if fc % 2 == 0 else 1024
            c.wait(sy, "fin_o", max(0, c.val("fin_o") - 16))
            c.dinc(sy.dma_start(out=xsbuf[:, xoff:xoff + 512],
                                in_=x_in[:, fc * T + tb * 512: fc * T + (tb + 1) * 512]), "d_xs")
            c.dinc(sy.dma_start(out=xsbuf[:, xoff + 512:xoff + 1024],
                                in_=s_scr[fc, tb]), "d_xs")
            c.wait(ve, "dn_mm")
            c.wait(ve, "g_s", max(0, c.val("g_s") - 1))
            c.wait(ve, "fin_o", max(0, c.val("fin_o") - 16))
            c.inc(ve.tensor_mul(gbuf[:, b * 512:(b + 1) * 512],
                                ps[0:128, b * 512:(b + 1) * 512], rhb), "dn_ev")
            c.wait(sc, "dn_ev")
            c.wait(sc, "fin_m", max(0, c.val("fin_m") - 1))
            c.inc(sc.activation(gbuf[:, b * 512:(b + 1) * 512],
                                gbuf[:, b * 512:(b + 1) * 512], AF.Silu), "g_s")
            c.wait(ve, "g_s")
            c.wait(ve, "d_xs")
            c.inc(ve.tensor_mul(gbuf[:, b * 512:(b + 1) * 512],
                                gbuf[:, b * 512:(b + 1) * 512],
                                xsbuf[:, xoff + 512:xoff + 1024]), "fin_m")
            c.wait(ve, "fin_m")
            c.inc(ve.tensor_add(gbuf[:, b * 512:(b + 1) * 512],
                                gbuf[:, b * 512:(b + 1) * 512],
                                xsbuf[:, xoff:xoff + 512]), "fin_a")
            c.wait(gp, "fin_a")
            c.dinc(gp.dma_start(out=out_hbm[:, fc * T + tb * 512: fc * T + (tb + 1) * 512],
                                in_=gbuf[:, b * 512:(b + 1) * 512]), "fin_o")

    c.wait(gp, "fin_o")
    phD2.close()
    return nc


def _prep_core(xb, qkv_w, qkv_b, out_w, out_b, alpha, w_up, w_down,
               T=2048, D=1024, H=16, DFF=4096):
    DC, HC, FC, NP = D // 128, DFF // 128, D // 128, H // 2

    x_prep = np.ascontiguousarray(
        xb.T.reshape(DC, 128, T).transpose(1, 0, 2).reshape(128, DC * T))
    wq = np.empty((NP, 128, DC * 128), np.float32)
    wk = np.empty((NP, 128, DC * 128), np.float32)
    for p in range(NP):
        Wq = qkv_w[128 * p:128 * p + 128, :]
        Wk = qkv_w[D + 128 * p: D + 128 * p + 128, :]
        wq[p] = Wq.reshape(128, DC, 128).transpose(2, 1, 0).reshape(128, DC * 128)
        wk[p] = Wk.reshape(128, DC, 128).transpose(2, 1, 0).reshape(128, DC * 128)
    Wv = qkv_w[2 * D:3 * D, :]
    wv = Wv.reshape(D, DC, 128).transpose(2, 1, 0).reshape(128, DC * D)
    bq = np.ascontiguousarray(qkv_b[0:D].reshape(NP, 1, 128))
    bk = np.ascontiguousarray(qkv_b[D:2 * D].reshape(NP, 1, 128))
    bv = np.ascontiguousarray(qkv_b[2 * D:3 * D].reshape(1, D))
    wo = out_w.reshape(FC, 128, DC, 128).transpose(3, 2, 0, 1).reshape(128, DC * FC * 128)
    bo = np.ascontiguousarray(out_b.reshape(1, D))
    wup = w_up.reshape(HC, 128, DC, 128).transpose(3, 2, 0, 1).reshape(128, DC * HC * 128)
    wdn = w_down.reshape(FC, 128, HC, 128).transpose(3, 2, 0, 1).reshape(128, HC * FC * 128)
    al = np.ascontiguousarray(alpha.reshape(H, 1))
    return {
        "x_in": x_prep, "wq_in": np.ascontiguousarray(wq), "wk_in": np.ascontiguousarray(wk),
        "wv_in": np.ascontiguousarray(wv), "bq_in": bq, "bk_in": bk, "bv_in": bv,
        "wo_in": np.ascontiguousarray(wo), "bo_in": bo,
        "wup_in": np.ascontiguousarray(wup), "wdn_in": np.ascontiguousarray(wdn),
        "alpha_in": al,
    }


def kernel(L0_attn_x, L0_attn_qkv_w, L0_attn_qkv_b, L0_attn_out_w, L0_attn_out_b,
           L0_attn_decay_alpha, L0_ffn_w_up, L0_ffn_w_down):
    T, D, H, DFF = 2048, 1024, 16, 4096
    x = np.ascontiguousarray(np.asarray(L0_attn_x, dtype=np.float32))
    qkv_w = np.ascontiguousarray(np.asarray(L0_attn_qkv_w, dtype=np.float32))
    qkv_b = np.asarray(L0_attn_qkv_b, dtype=np.float32)
    out_w = np.ascontiguousarray(np.asarray(L0_attn_out_w, dtype=np.float32))
    out_b = np.asarray(L0_attn_out_b, dtype=np.float32)
    alpha = np.asarray(L0_attn_decay_alpha, dtype=np.float32)
    w_up = np.ascontiguousarray(np.asarray(L0_ffn_w_up, dtype=np.float32))
    w_down = np.ascontiguousarray(np.asarray(L0_ffn_w_down, dtype=np.float32))

    nc = build_program(T=T, D=D, H=H, DFF=DFF)
    in_maps = [
        _prep_core(x[b], qkv_w, qkv_b, out_w, out_b, alpha, w_up, w_down,
                   T=T, D=D, H=H, DFF=DFF)
        for b in range(N_CORES)
    ]
    res = run_bass_kernel_spmd(nc, in_maps, core_ids=list(range(N_CORES)))
    FC = D // 128
    out = np.empty((N_CORES, T, D), np.float32)
    for b in range(N_CORES):
        o = np.asarray(res.results[b]["out_hbm"])
        out[b] = o.reshape(128, FC, T).transpose(1, 0, 2).reshape(D, T).T
    return out


# version 17
# speedup vs baseline: 1.0324x; 1.0079x over previous
from contextlib import ExitStack

import numpy as np
import ml_dtypes

import concourse.bass as bass
import concourse.mybir as mybir
from concourse.bass_utils import run_bass_kernel_spmd

F32 = mybir.dt.float32
BF16 = mybir.dt.bfloat16
AF = mybir.ActivationFunctionType
ALU = mybir.AluOpType
AX = mybir.AxisListType

N_CORES = 8


_DED = {"pv_g", "vh_ev", "s_ev", "up_ev", "dn_ev", "g_h", "h_lo", "h2_rd",
        "d_h", "d_hB", "d_wto", "d_wtoB", "d_ws", "d_wsB", "d_wt", "d_wtB",
        "fin_o", "qn_rx", "kn_rx", "g_s", "fin_m", "con_g",
        "ws_abs", "ws_add", "wt_abs", "wt_g", "wt_m", "w_tcp",
        *[f"dx{l}" for l in "ABCDEFGH"], *[f"wtu{l}" for l in "ABCDEFGH"],
        *[f"wtd{l}" for l in "ABCDEFGH"], *[f"ha{l}" for l in "ABCDEFGH"]}

_GROUPS = {
    "g_te": ["tr_mm", "ln_sx", "ln_x2mm", "bc_m", "bc_r", "q_mm", "k_mm",
             "qn_mm", "kn_mm", "c_bc", "sc_mm", "vh_mm", "pv_mm", "s_mm",
             "up_mm", "dn_mm", "rfb_m", "rhb_m", "w_smm", "w_tbc",
             "nb_a", "nb_b"],
    "g_sc": ["al_e", "al_a", "al_h", "al_lb", "ab_h", "ab_lb", "mb_hi", "jr_h", "tr_cp",
             "q_h", "k_h", "c_s", "c_cp", "e2d", "exp",
             "nr_a", "nr_b", "wo_h", "wv_h", "c_h", "h_hi", "h_sq",
             "rfb_c", "rhb_c", "ws_abs", "wt_abs", "w_tcp", "ln_sq",
             "wq_h", "wk_h", "tr_1", "tr_2",
             "r_m2", "r_sq", "r_w", "r_rs", "r_t", "r_rf", "r_u",
             "xn_hic", "b_v", "b_o", "bqs", "bks",
             "rh_2"],
    "g_ve": ["con_v", "con_r", "al_n", "al_h2", "al_r", "al_l2", "mb_f",
             "mb_lo", "ab_z", "ab_h2", "ab_l", "ab_l2", "r6n", "r6z", "jr_1", "jr_l",
             "tr_3", "b_vv", "b_ov", "bqsv", "bksv",
             "r_m", "r_e2", "r_var", "r_tz", "r_u1", "r_v8",
             "xn_t1", "xn_f", "xn_lo", "wq_l", "wk_l", "q_l",
             "k_l", "q_sq", "k_sq", "r6a", "qn_m", "kn_m", "c_m", "c_n", "nc_a", "nc_b", "at_a",
             "at_b", "pt_fix", "vh_1", "wv_sp", "wo_sp", "c_sp", "wt_g",
             "wt_m", "wsz", "ws_add", "w_th", "rh_1", "rh_3",
             "rh_4", "fin_a"],
    "g_gp": ["d_l6", "d_r6", "d_at", "s_w", "rh_d", "r_rsd",
             "r_rfd", "ab_d"],
    "g_sy": ["d_x", "d_al", "d_al2", "d_bv", "d_bo", "d_wv", "d_wq", "d_wqs", "d_wk",
             "d_wo", "d_af", "d_wtu", "d_wtd", "d_ha", "d_xs"],
}
_TAG2SEM = {t: g for g, ts in _GROUPS.items() for t in ts}


class Ctx:

    def __init__(self, nc):
        self.nc = nc
        self.sems = {}
        self.cnt = {}

    def phys(self, name):
        if name in _DED:
            return name
        return _TAG2SEM[name]

    def S(self, name):
        ph = self.phys(name)
        if ph not in self.sems:
            self.sems[ph] = self.nc.alloc_semaphore(ph)
            self.cnt[ph] = 0
        return self.sems[ph]

    def inc(self, instr, name, n=1):
        if name.endswith("0"):
            return instr
        instr.then_inc(self.S(name), n)
        self.cnt[self.phys(name)] += n
        return instr

    def dinc(self, instr, name):
        self.inc(instr, name, 16)
        eng = self.engines[instr.ins.engine]
        eng.wait_ge(self.S(name), self.cnt[self.phys(name)])
        return instr

    def wait(self, eng, name, val=None):
        ph = self.phys(name)
        v = self.cnt.get(ph, 0) if val is None else val
        if v > 0:
            eng.wait_ge(self.S(name), v)

    def val(self, name):
        return self.cnt.get(self.phys(name), 0)


def build_program(T=2048, D=1024, H=16, DFF=4096, stop_after=None):
    HD = 64
    NP = H // 2
    DC = D // 128
    KC = T // 128
    QB = T // 512
    HC = DFF // 128
    FC = DC
    TB = QB
    QCOL = T // 512
    KCG = KC // 2

    nc = bass.Bass("TRN2", target_bir_lowering=False)
    c = Ctx(nc)
    sy, te, ve, sc, gp = nc.sync, nc.tensor, nc.vector, nc.scalar, nc.gpsimd
    c.engines = {e.engine: e for e in (sy, te, ve, sc, gp)}

    x_in = nc.dram_tensor("x_in", [128, DC * T], F32, kind="ExternalInput").ap()
    wq_in = nc.dram_tensor("wq_in", [NP, 128, DC * 128], F32, kind="ExternalInput").ap()
    wk_in = nc.dram_tensor("wk_in", [NP, 128, DC * 128], F32, kind="ExternalInput").ap()
    wv_in = nc.dram_tensor("wv_in", [128, DC * D], F32, kind="ExternalInput").ap()
    bq_in = nc.dram_tensor("bq_in", [NP, 1, 128], F32, kind="ExternalInput").ap()
    bk_in = nc.dram_tensor("bk_in", [NP, 1, 128], F32, kind="ExternalInput").ap()
    bv_in = nc.dram_tensor("bv_in", [1, D], F32, kind="ExternalInput").ap()
    wo_in = nc.dram_tensor("wo_in", [128, DC * FC * 128], F32, kind="ExternalInput").ap()
    bo_in = nc.dram_tensor("bo_in", [1, D], F32, kind="ExternalInput").ap()
    wup_in = nc.dram_tensor("wup_in", [128, DC * HC * 128], F32, kind="ExternalInput").ap()
    wdn_in = nc.dram_tensor("wdn_in", [128, HC * FC * 128], F32, kind="ExternalInput").ap()
    alpha_in = nc.dram_tensor("alpha_in", [H, 1], F32, kind="ExternalInput").ap()

    attn_scr = nc.dram_tensor("attn_scr", [H, QB, 64, 512], F32, kind="Internal").ap()
    s_scr = nc.dram_tensor("s_scr", [FC, TB, 128, 512], F32, kind="Internal").ap()
    h_scr_hi = nc.dram_tensor("h_scr_hi", [TB, 128, HC * 512], BF16, kind="Internal").ap()
    h_scr_lo = nc.dram_tensor("h_scr_lo", [TB, 128, HC * 512], BF16, kind="Internal").ap()
    wup_t_scr = nc.dram_tensor("wup_t_scr", [128, DC * HC * 128], BF16, kind="Internal").ap()
    wdn_t_scr = nc.dram_tensor("wdn_t_scr", [128, HC * FC * 128], BF16, kind="Internal").ap()

    out_hbm = nc.dram_tensor("out_hbm", [128, FC * T], F32, kind="ExternalOutput").ap()

    ps = nc.alloc_psum_tensor("ps", [128, 4096], F32).ap()

    ones_row = nc.alloc_sbuf_tensor("ones_row", [128, 128], F32).ap()
    ones_bf = nc.alloc_sbuf_tensor("ones_bf", [128, 512], BF16).ap()
    lhsT6 = nc.alloc_sbuf_tensor("lhsT6", [128, T], BF16).ap()
    Mb = nc.alloc_sbuf_tensor("Mb", [128, T], BF16).ap()
    R_relu = nc.alloc_sbuf_tensor("R_relu", [128, 128], F32).ap()
    alpha_c = nc.alloc_sbuf_tensor("alpha_c", [16, 8], F32).ap()
    tiny_bf = nc.alloc_sbuf_tensor("tiny_bf", [16, 8], BF16).ap()
    arow = nc.alloc_sbuf_tensor("arow", [128, 96], F32).ap()
    a4 = nc.alloc_sbuf_tensor("a4", [128, 16], F32).ap()
    neg2a = nc.alloc_sbuf_tensor("neg2a", [128, 16], F32).ap()
    rowbuf = nc.alloc_sbuf_tensor("rowbuf", [128, T], F32).ap()
    thr = nc.alloc_sbuf_tensor("thr", [128, 2], F32).ap()
    smalls = nc.alloc_sbuf_tensor("smalls", [128, 16], F32).ap()
    negc = nc.alloc_sbuf_tensor("negc", [128, 1], F32).ap()
    bq_bf = nc.alloc_sbuf_tensor("bq_bf", [128, 256], BF16).ap()
    bk_bf = nc.alloc_sbuf_tensor("bk_bf", [128, 256], BF16).ap()
    e2diag = nc.alloc_sbuf_tensor("e2diag", [128, 256], F32).ap()

    def salloc(stack, name, shape, dtype):
        return stack.enter_context(nc.sbuf_tensor(name, list(shape), dtype)).ap()


    def _early_finish():
        c.wait(gp, "sc_mm", c.val("sc_mm"))
        c.dinc(gp.dma_start(out=out_hbm[0:1, 0:4], in_=ones_row[0:1, 0:4]), "fin_o")
        c.wait(gp, "fin_o")
    xn_stack = ExitStack()
    xn_hi = salloc(xn_stack, "xn_hi", [128, DC * T], BF16)
    xn_lo = salloc(xn_stack, "xn_lo", [128, DC * T], BF16)

    def split_row(src_f32_row0, hi_row0, lo_row0, tag):
        c.inc(sc.activation(hi_row0, src_f32_row0, AF.Copy), tag)
        c.wait(ve, tag)
        c.inc(ve.tensor_sub(lo_row0, src_f32_row0, hi_row0), tag + "v")

    c.inc(ve.memset(ones_row, 1.0), "con_v")
    c.inc(ve.memset(ones_bf, 1.0), "con_v")
    c.inc(gp.iota(R_relu, pattern=[[-1, 128]], base=0, channel_multiplier=1,
                  allow_small_or_imprecise_dtypes=True), "con_g")
    c.wait(ve, "con_g", 1)
    c.inc(ve.tensor_scalar_max(R_relu, R_relu, 0.0), "con_r")

    phA = ExitStack()
    x_res = salloc(phA, "x_res", [128, DC * T], F32)
    scrA = salloc(phA, "scrA", [128, T], F32)
    scrA2 = salloc(phA, "scrA2", [128, T], F32)
    iota16 = salloc(phA, "iota16", [16, T], F32)
    jrows = salloc(phA, "jrows", [128, T], BF16)
    jrows2 = salloc(phA, "jrows2", [128, T], BF16)
    jrows3 = salloc(phA, "jrows3", [128, T], BF16)

    for ch, l in enumerate("ABCDEFGH"):
        w8 = (DC * T) // 8
        c.inc(sy.dma_start(out=x_res[:, ch * w8:(ch + 1) * w8],
                           in_=x_in[:, ch * w8:(ch + 1) * w8]), f"dx{l}", 16)
    c.dinc(sy.dma_start(out=alpha_c[0:H, 0:1], in_=alpha_in[0:H, 0:1]), "d_al")

    c.inc(gp.iota(iota16, pattern=[[1, T]], base=0, channel_multiplier=0,
                  allow_small_or_imprecise_dtypes=True), "con_g")

    c.wait(sc, "d_al")
    c.inc(sc.activation(alpha_c[0:H, 1:2], alpha_c[0:H, 0:1], AF.Exp), "al_e")
    c.wait(sc, "al_e")
    c.inc(sc.activation(alpha_c[0:H, 2:3], alpha_c[0:H, 1:2], AF.Ln, bias=1.0), "al_a")
    c.wait(ve, "al_a")
    c.inc(ve.tensor_scalar_mul(alpha_c[0:H, 3:4], alpha_c[0:H, 2:3], -1.0), "al_n")
    c.wait(sc, "al_n")
    c.inc(sc.activation(tiny_bf[0:H, 0:1], alpha_c[0:H, 2:3], AF.Copy), "al_h")
    c.wait(ve, "al_h")
    c.inc(ve.tensor_copy(alpha_c[0:H, 4:5], tiny_bf[0:H, 0:1]), "al_h2")
    c.wait(ve, "al_h2")
    c.inc(ve.tensor_sub(alpha_c[0:H, 5:6], alpha_c[0:H, 2:3], alpha_c[0:H, 4:5]), "al_r")
    c.wait(sc, "al_r")
    c.inc(sc.activation(tiny_bf[0:H, 1:2], alpha_c[0:H, 5:6], AF.Copy), "al_lb")
    c.wait(ve, "al_lb")
    c.inc(ve.tensor_copy(alpha_c[0:H, 6:7], tiny_bf[0:H, 1:2]), "al_l2")

    c.wait(ve, "con_g")
    c.wait(ve, "al_n")
    c.inc(ve.tensor_scalar_mul(scrA[0:H, 0:T], iota16[0:H, :], alpha_c[0:H, 3:4]), "mb_f")
    c.wait(sc, "mb_f")
    c.inc(sc.activation(Mb[0:H, :], scrA[0:H, 0:T], AF.Copy), "mb_hi")
    c.wait(ve, "mb_hi")
    c.inc(ve.tensor_sub(Mb[32:32 + H, :], scrA[0:H, 0:T], Mb[0:H, :]), "mb_lo")
    c.dinc(sy.dma_start(out=arow[0:1, 0:H],
                        in_=alpha_in.rearrange("h one -> one h")), "d_al2")
    c.wait(sc, "d_al2")
    c.inc(sc.activation(arow[0:1, 16:16 + H], arow[0:1, 0:H], AF.Exp), "tr_1")
    c.wait(sc, "tr_1")
    c.inc(sc.activation(arow[0:1, 32:32 + H], arow[0:1, 16:16 + H], AF.Ln, bias=1.0), "tr_2")
    c.wait(ve, "tr_2")
    c.inc(ve.tensor_scalar_mul(arow[0:1, 48:48 + H], arow[0:1, 32:32 + H], -2.0), "tr_3")
    c.wait(te, "tr_3")
    c.wait(te, "con_v")
    c.inc(te.matmul(ps[0:128, 0:H], ones_row[0:1, 0:128], arow[0:1, 48:48 + H],
                    start=True, stop=True), "tr_mm")
    c.wait(sc, "tr_mm")
    c.inc(sc.activation(neg2a[:, 0:H], ps[0:128, 0:H], AF.Copy), "tr_cp")


    c.inc(ve.memset(arow[0:1, 64:96], 0.0), "ab_z")
    c.wait(sc, "tr_2")
    c.wait(sc, "ab_z")
    c.inc(sc.activation(jrows[0:1, 0:H], arow[0:1, 32:32 + H], AF.Copy), "ab_h")
    c.wait(ve, "ab_h")
    c.inc(ve.tensor_copy(arow[0:1, 64:64 + H], jrows[0:1, 0:H]), "ab_h2")
    c.wait(ve, "ab_h2")
    c.inc(ve.tensor_sub(arow[0:1, 16:16 + H], arow[0:1, 32:32 + H], arow[0:1, 64:64 + H]), "ab_l")
    c.wait(sc, "ab_l")
    c.inc(sc.activation(jrows[0:1, 0:H], arow[0:1, 16:16 + H], AF.Copy), "ab_lb")
    c.wait(ve, "ab_lb")
    c.inc(ve.tensor_copy(arow[0:1, 80:80 + H], jrows[0:1, 0:H]), "ab_l2")
    c.wait(gp, "ab_l2")
    for r, src0 in ((0, 64), (1, 64), (2, 80), (3, 80)):
        c.dinc(gp.dma_start(out=a4[r:r + 1, 0:16], in_=arow[0:1, src0:src0 + 16]), "ab_d")
    c.dinc(gp.dma_start(out=a4[64:68, 0:16], in_=a4[0:4, 0:16]), "ab_d")

    c.wait(ve, "ab_l2")
    c.inc(ve.memset(jrows[0:1, :], 1.0), "jr_1")
    c.wait(sc, "con_g")
    c.wait(sc, "jr_1")
    c.inc(sc.activation(jrows2[0:1, :], iota16[0:1, :], AF.Copy), "jr_h")
    c.wait(ve, "jr_h")
    c.inc(ve.tensor_sub(jrows3[0:1, :], iota16[0:1, :], jrows2[0:1, :]), "jr_l")
    c.wait(gp, "jr_1")
    c.wait(gp, "jr_l")
    for r in (4, 5, 12, 13):
        c.dinc(gp.dma_start(out=lhsT6[r:r + 1, :], in_=jrows[0:1, :]), "d_l6")
    for r in (0, 2, 8, 10):
        c.dinc(gp.dma_start(out=lhsT6[r:r + 1, :], in_=jrows2[0:1, :]), "d_l6")
    for r in (1, 3, 9, 11):
        c.dinc(gp.dma_start(out=lhsT6[r:r + 1, :], in_=jrows3[0:1, :]), "d_l6")
    c.dinc(gp.dma_start(out=lhsT6[64:70, :], in_=lhsT6[8:14, :]), "d_l6")

    for l in "ABCDEFGH":
        c.wait(te, f"dx{l}", 16)
    c.wait(te, "tr_cp")
    for dc in range(DC):
        for j in range(QCOL):
            c.inc(te.matmul(ps[0:1, j * 512:(j + 1) * 512], ones_row[:, 0:1],
                            x_res[:, dc * T + j * 512: dc * T + (j + 1) * 512],
                            start=(dc == 0), stop=(dc == DC - 1)),
                  "ln_sx" if dc == DC - 1 else "ln_mm0")
    for dc in range(DC):
        buf = scrA if dc % 2 == 0 else scrA2
        for l in "ABCDEFGH":
            c.wait(sc, f"dx{l}", 16)
        c.wait(sc, "mb_lo")
        if dc >= 2:
            c.wait(sc, "ln_x2mm")
        c.inc(sc.activation(buf[:, 0:T], x_res[:, dc * T:(dc + 1) * T], AF.Square), "ln_sq")
        c.wait(te, "ln_sq")
        for j in range(QCOL):
            c.inc(te.matmul(ps[0:1, 2048 + j * 512: 2048 + (j + 1) * 512], ones_row[:, 0:1],
                            buf[:, j * 512:(j + 1) * 512],
                            start=(dc == 0), stop=(dc == DC - 1)), "ln_x2mm")
    c.wait(ve, "ln_sx")
    c.wait(ve, "ln_x2mm")
    c.inc(ve.tensor_scalar_mul(rowbuf[0:1, 0:T], ps[0:1, 0:T], 1.0 / D), "r_m")
    c.inc(ve.tensor_scalar_mul(scrA2[0:1, 0:T], ps[0:1, 2048:2048 + T], 1.0 / D), "r_e2")
    c.wait(sc, "r_m")
    c.wait(sc, "ln_sq")
    c.inc(sc.activation(scrA[0:1, 0:T], rowbuf[0:1, 0:T], AF.Square), "r_m2")
    c.wait(ve, "r_m2")
    c.wait(ve, "r_e2")
    c.inc(ve.tensor_sub(scrA2[0:1, 0:T], scrA2[0:1, 0:T], scrA[0:1, 0:T]), "r_var")
    c.wait(ve, "r_var")
    c.wait(ve, "jr_h")
    c.wait(ve, "ab_l2")
    c.inc(ve.tensor_scalar_add(iota16[0:1, 0:T], scrA2[0:1, 0:T], 1e-5), "r_tz")
    c.wait(sc, "r_tz")
    c.inc(sc.activation(scrA[0:1, 0:T], iota16[0:1, 0:T], AF.Ln), "r_sq")
    c.wait(sc, "r_sq")
    c.inc(sc.activation(iota16[0:1, 0:T], scrA[0:1, 0:T], AF.Exp, scale=-0.5), "r_rs")

    c.wait(te, "r_m2")
    c.wait(te, "r_e2")
    for j in range(QCOL):
        c.inc(te.matmul(ps[0:128, j * 512:(j + 1) * 512], ones_row[0:1, 0:128],
                        rowbuf[0:1, j * 512:(j + 1) * 512], start=True, stop=True), "bc_m")
    c.wait(te, "r_rs")
    for j in range(QCOL):
        c.inc(te.matmul(ps[0:128, 2048 + j * 512:2048 + (j + 1) * 512], ones_row[0:1, 0:128],
                        iota16[0:1, j * 512:(j + 1) * 512], start=True, stop=True), "bc_r")

    c.wait(sc, "bc_r", c.val("bc_r"))
    c.inc(sc.activation(iota16[0:1, 0:T], scrA2[0:1, 0:T], AF.Ln), "r_u")
    c.wait(ve, "r_u")
    c.inc(ve.tensor_sub(scrA2[0:1, 0:T], iota16[0:1, 0:T], scrA[0:1, 0:T]), "r_u1")
    c.wait(sc, "r_u1")
    c.inc(sc.activation(scrA[0:1, 0:T], scrA2[0:1, 0:T], AF.Exp, scale=0.5), "r_w")
    c.wait(ve, "r_w")
    c.inc(ve.tensor_scalar_add(scrA2[0:1, 0:T], scrA[0:1, 0:T], 1e-8), "r_v8")
    c.wait(sc, "r_v8")
    c.inc(sc.activation(scrA[0:1, 0:T], scrA2[0:1, 0:T], AF.Ln), "r_t")
    c.wait(sc, "r_t")
    c.wait(sc, "bc_m", c.val("bc_m"))
    c.inc(sc.activation(rowbuf[0:1, 0:T], scrA[0:1, 0:T], AF.Exp, scale=-1.0), "r_rf")

    c.wait(ve, "bc_m")
    c.wait(ve, "bc_r")
    c.wait(ve, "r_rf", c.val("r_rf"))
    for dc in range(DC):
        if dc >= 1:
            c.wait(ve, "xn_lo")
        c.inc(ve.tensor_sub(scrA[:, 0:T], x_res[:, dc * T:(dc + 1) * T], ps[0:128, 0:T]), "xn_t1")
        c.wait(ve, "xn_t1")
        c.inc(ve.tensor_mul(scrA2[:, 0:T], scrA[:, 0:T], ps[0:128, 2048:2048 + T]), "xn_f")
        c.wait(sc, "xn_f")
        c.inc(sc.activation(xn_hi[:, dc * T:(dc + 1) * T], scrA2[:, 0:T], AF.Copy), "xn_hic")
        c.wait(ve, "xn_hic")
        c.inc(ve.tensor_sub(xn_lo[:, dc * T:(dc + 1) * T], scrA2[:, 0:T],
                            xn_hi[:, dc * T:(dc + 1) * T]), "xn_lo")

    if stop_after == "ADBG":
        c.wait(gp, "r_rf", c.val("r_rf"))
        c.wait(gp, "xn_lo", c.val("xn_lo"))
        c.wait(gp, "sc_mm", c.val("sc_mm"))
        c.dinc(gp.dma_start(out=out_hbm[0:1, 0:T], in_=rowbuf[0:1, 0:T]), "fin_o")
        c.dinc(gp.dma_start(out=out_hbm[1:2, 0:T], in_=iota16[0:1, 0:T]), "fin_o")
        c.dinc(gp.dma_start(out=out_hbm[2:3, 0:2048], in_=xn_hi[5:6, 0:2048]), "fin_o")
        c.dinc(gp.dma_start(out=out_hbm[3:4, 0:2048], in_=xn_lo[5:6, 0:2048]), "fin_o")
        c.wait(gp, "fin_o")
        return nc
    c.wait(sy, "xn_lo")
    if stop_after == "A":
        _early_finish()
        return nc
    phA.close()

    phB = ExitStack()
    v_hat = salloc(phB, "v_hat", [128, KC * H * 65], F32)
    pt0 = salloc(phB, "pt0", [128, 2048], F32)
    phB0 = ExitStack()
    wv_hi = salloc(phB0, "wv_hi", [128, DC * D], BF16)
    wv_lo = salloc(phB0, "wv_lo", [128, DC * D], BF16)
    bv_bf = salloc(phB0, "bv_bf", [128, 2 * D], BF16)

    vrr = v_hat.rearrange("p (kc h e) -> p kc h e", kc=KC, h=H)

    c.dinc(sy.dma_start(out=v_hat[0:1, 0:D], in_=bv_in), "d_bv")
    c.wait(sc, "d_bv")
    split_row(v_hat[0:1, 0:D], bv_bf[0:1, 0:D], bv_bf[0:1, D:2 * D], "b_v")

    c.wait(sc, "d_l6", c.val("d_l6"))
    c.wait(ve, "d_l6", c.val("d_l6"))
    for dc in range(DC):
        c.wait(sy, "wv_sp")
        half = (dc % 2) * D
        c.dinc(sy.dma_start(out=pt0[:, half:half + D], in_=wv_in[:, dc * D:(dc + 1) * D]), "d_wv")
        c.wait(sc, "d_wv")
        c.inc(sc.activation(wv_hi[:, dc * D:(dc + 1) * D], pt0[:, half:half + D], AF.Copy), "wv_h")
        c.wait(ve, "wv_h")
        c.inc(ve.tensor_sub(wv_lo[:, dc * D:(dc + 1) * D], pt0[:, half:half + D],
                            wv_hi[:, dc * D:(dc + 1) * D]), "wv_sp")

    if stop_after == "B0a":
        _early_finish()
        return nc
    c.inc(ve.memset(vrr[:, :, :, 64:65], 1.0), "vh_1")
    if stop_after == "B0b":
        _early_finish()
        return nc

    NVH = D // 512
    c.wait(te, "wv_sp")
    c.wait(te, "b_v")
    vh_bank = 0
    for kc in range(KC):
        for vh in range(NVH):
            b = vh_bank % 2
            vh_bank += 1
            c.wait(te, "vh_ev", max(0, c.val("vh_ev") - 1))
            c.inc(te.matmul(ps[0:128, b * 512:(b + 1) * 512], ones_bf[0:1, 0:128],
                            bv_bf[0:1, vh * 512:(vh + 1) * 512], start=True, stop=False), "vh_mm0")
            c.inc(te.matmul(ps[0:128, b * 512:(b + 1) * 512], ones_bf[0:1, 0:128],
                            bv_bf[0:1, D + vh * 512: D + (vh + 1) * 512], start=False, stop=False), "vh_mm0")
            for dc in range(DC):
                xh = xn_hi[:, dc * T + kc * 128: dc * T + kc * 128 + 128]
                xl = xn_lo[:, dc * T + kc * 128: dc * T + kc * 128 + 128]
                wh = wv_hi[:, dc * D + vh * 512: dc * D + (vh + 1) * 512]
                wl = wv_lo[:, dc * D + vh * 512: dc * D + (vh + 1) * 512]
                last = dc == DC - 1
                c.inc(te.matmul(ps[0:128, b * 512:(b + 1) * 512], xh, wh, start=False, stop=False), "vh_mm0")
                c.inc(te.matmul(ps[0:128, b * 512:(b + 1) * 512], xh, wl, start=False, stop=False), "vh_mm0")
                c.inc(te.matmul(ps[0:128, b * 512:(b + 1) * 512], xl, wh, start=False,
                                stop=last), "vh_mm" if last else "vh_mm0")
            if stop_after == "B0c":
                continue
            c.wait(ve, "vh_mm")
            c.wait(ve, "vh_1")
            c.wait(ve, "b_vv", c.val("b_vv"))
            nheads = 512 // HD
            dst = vrr[:, kc, vh * nheads:(vh + 1) * nheads, 0:64]
            src = ps[0:128, b * 512:(b + 1) * 512].rearrange("p (h e) -> p h e", e=64)
            c.inc(ve.tensor_copy(dst, src), "vh_ev")

    c.wait(te, "d_l6")
    phB0.close()


    pt1 = salloc(phB, "pt1", [128, 2048], F32)
    q_hi = salloc(phB, "q_hi", [128, T], BF16)
    q_lo = salloc(phB, "q_lo", [128, T], BF16)
    k_hi = salloc(phB, "k_hi", [128, T], BF16)
    k_lo = salloc(phB, "k_lo", [128, T], BF16)
    rhs6m = salloc(phB, "rhs6m", [128, T], BF16)
    rhs6p = salloc(phB, "rhs6p", [128, T], BF16)
    c.wait(ve, "vh_mm", c.val("vh_mm"))
    c.wait(ve, "vh_ev", c.val("vh_ev"))
    c.inc(ve.memset(rhs6m, 0.0), "r6z")
    c.inc(ve.memset(rhs6p, 0.0), "r6z")
    wqk_hi = salloc(phB, "wqk_hi", [128, DC * 128], BF16)
    wqk_lo = salloc(phB, "wqk_lo", [128, DC * 128], BF16)
    wstage = salloc(phB, "wstage", [128, DC * 128 + 256], F32)
    if stop_after in ("B0", "B0c"):
        _early_finish()
        return nc

    phW = ExitStack()
    WP = 512
    wstg = salloc(phW, "wstg", [128, 2 * WP], F32)
    wabs = salloc(phW, "wabs", [128, WP], F32)
    wsgn = salloc(phW, "wsgn", [128, WP], BF16)
    wmask = salloc(phW, "wmask", [128, WP], BF16)
    wtout = salloc(phW, "wtout", [128, WP], BF16)
    wsum = salloc(phW, "wsum", [128, 8], F32)
    NPC_UP = (DC * HC * 128) // WP
    NPC_DN = (HC * FC * 128) // WP
    w_state = {"stats_done": False, "thr_done": False}

    def w_units():
        c.wait(sy, "vh_mm", c.val("vh_mm"))
        c.wait(sy, "wv_sp", c.val("wv_sp"))
        c.wait(ve, "vh_mm", c.val("vh_mm"))
        c.inc(ve.memset(wsum[:, 0:2], 0.0), "wsz")
        c.wait(sc, "wsz")
        yield
        for (win, sumcol, npieces) in ((wup_in, 0, NPC_UP), (wdn_in, 1, NPC_DN)):
            for i in range(npieces):
                half = (i % 2) * WP
                dtag = "d_ws" if i % 2 == 0 else "d_wsB"
                c.wait(sy, "ws_abs", max(0, c.val("ws_abs") - 1))
                c.dinc(sy.dma_start(out=wstg[:, half:half + WP],
                                    in_=win[:, i * WP:(i + 1) * WP]), dtag)
                yield
                c.wait(sc, dtag)
                c.wait(sc, "ws_add", c.val("ws_add"))
                c.inc(sc.activation(wabs, wstg[:, half:half + WP], AF.Abs,
                                    accum_out=wsum[:, 4 + (i % 2):5 + (i % 2)]), "ws_abs")
                c.wait(ve, "ws_abs")
                c.inc(ve.tensor_add(wsum[:, sumcol:sumcol + 1], wsum[:, sumcol:sumcol + 1],
                                    wsum[:, 4 + (i % 2):5 + (i % 2)]), "ws_add")
                yield
        w_state["stats_done"] = True
        yield
        c.wait(sy, "ws_abs", c.val("ws_abs"))
        for (win, wtscr, thcol, npieces) in ((wup_in, wup_t_scr, 0, NPC_UP),
                                             (wdn_in, wdn_t_scr, 1, NPC_DN)):
            for i in range(npieces):
                half = (i % 2) * WP
                dtag = "d_wt" if i % 2 == 0 else "d_wtB"
                c.wait(sy, "wt_abs", max(0, c.val("wt_abs") - 2))
                c.dinc(sy.dma_start(out=wstg[:, half:half + WP],
                                    in_=win[:, i * WP:(i + 1) * WP]), dtag)
                yield
                c.wait(sc, dtag)
                c.wait(sc, "wt_g", c.val("wt_g"))
                c.wait(sc, "wt_m", c.val("wt_m"))
                c.inc(sc.activation(wabs, wstg[:, half:half + WP], AF.Abs), "wt_abs")
                c.inc(sc.activation(wsgn, wstg[:, half:half + WP], AF.Sign), "wt_abs")
                yield
                c.wait(ve, "wt_abs")
                c.wait(ve, "w_tcp", 1)
                c.inc(ve.tensor_single_scalar(wmask, wabs, thr[:, thcol:thcol + 1],
                                              ALU.is_gt), "wt_g")
                c.wait(ve, "wt_g")
                otag = "d_wto" if i % 2 == 0 else "d_wtoB"
                c.wait(ve, "d_wto", c.val("d_wto"))
                c.wait(ve, "d_wtoB", c.val("d_wtoB"))
                c.inc(ve.tensor_mul(wtout[:, 0:WP], wsgn, wmask), "wt_m")
                c.wait(gp, "wt_m")
                c.dinc(gp.dma_start(out=wtscr[:, i * WP:(i + 1) * WP],
                                    in_=wtout[:, 0:WP]), otag)
                yield

    wgen = w_units()
    _W_DONE = object()

    def drain_w(n):
        for _ in range(n):
            if w_state["stats_done"] and not w_state["thr_done"]:
                return
            if next(wgen, _W_DONE) is _W_DONE:
                return

    def qk_psum(bias_t, bias_tag, tag):
        c.wait(te, bias_tag)
        for j in range(QCOL):
            c.inc(te.matmul(ps[0:128, j * 512:(j + 1) * 512], bias_t[0:1, 0:128],
                            ones_bf[0:1, 0:512], start=True, stop=False), tag + "0")
            c.inc(te.matmul(ps[0:128, j * 512:(j + 1) * 512], bias_t[0:1, 128:256],
                            ones_bf[0:1, 0:512], start=False, stop=False), tag + "0")
            for dc in range(DC):
                xh = xn_hi[:, dc * T + j * 512: dc * T + (j + 1) * 512]
                xl = xn_lo[:, dc * T + j * 512: dc * T + (j + 1) * 512]
                wh = wqk_hi[:, dc * 128:(dc + 1) * 128]
                wl = wqk_lo[:, dc * 128:(dc + 1) * 128]
                last = dc == DC - 1
                c.inc(te.matmul(ps[0:128, j * 512:(j + 1) * 512], wh, xh,
                                start=False, stop=False), tag + "0")
                c.inc(te.matmul(ps[0:128, j * 512:(j + 1) * 512], wl, xh,
                                start=False, stop=False), tag + "0")
                c.inc(te.matmul(ps[0:128, j * 512:(j + 1) * 512], wh, xl,
                                start=False, stop=last), tag if last else tag + "0")

    c.dinc(sy.dma_start(out=wstage[:, 0:DC * 128], in_=wq_in[0]), "d_wqs")
    c.dinc(sy.dma_start(out=wstage[0:1, DC * 128:DC * 128 + 128], in_=bq_in[0]), "d_wqs")
    c.dinc(sy.dma_start(out=wstage[0:1, DC * 128 + 128:DC * 128 + 256], in_=bk_in[0]), "d_wqs")
    c.wait(sc, "d_wqs")
    c.wait(sc, "vh_mm", c.val("vh_mm"))
    c.inc(sc.activation(wqk_hi, wstage[:, 0:DC * 128], AF.Copy), "wq_h")
    c.wait(ve, "wq_h")
    c.wait(ve, "vh_mm", c.val("vh_mm"))
    c.inc(ve.tensor_sub(wqk_lo, wstage[:, 0:DC * 128], wqk_hi), "wq_l")
    split_row(wstage[0:1, DC * 128:DC * 128 + 128], bq_bf[0:1, 0:128], bq_bf[0:1, 128:256], "bqs")
    split_row(wstage[0:1, DC * 128 + 128:DC * 128 + 256], bk_bf[0:1, 0:128], bk_bf[0:1, 128:256], "bks")

    for p in range(NP):
        hA, hB = 2 * p, 2 * p + 1
        if p == NP // 2:
            while not w_state["stats_done"]:
                next(wgen)
            c.wait(te, "exp", c.val("exp"))
            c.wait(te, "at_b", c.val("at_b"))
            c.wait(te, "nc_b", c.val("nc_b"))
            c.wait(te, "ws_add")
            c.inc(te.matmul(ps[0:1, 0:2], ones_row[:, 0:1], wsum[:, 0:2],
                            start=True, stop=True), "w_smm")
            c.wait(ve, "w_smm")
            c.inc(ve.tensor_scalar_mul(wsum[0:1, 2:4], ps[0:1, 0:2], 0.5 / (D * DFF)), "w_th")
            c.wait(te, "w_th")
            c.inc(te.matmul(ps[0:128, 512:514], ones_row[0:1, 0:128], wsum[0:1, 2:4],
                            start=True, stop=True), "w_tbc")
            c.wait(sc, "w_tbc")
            c.inc(sc.activation(thr[:, 0:2], ps[0:128, 512:514], AF.Copy), "w_tcp")
            c.wait(te, "w_tcp")
            w_state["thr_done"] = True

        c.wait(gp, "sc_mm", c.val("sc_mm"))
        c.wait(gp, "vh_ev", c.val("vh_ev"))
        c.wait(gp, "mb_lo", c.val("mb_lo"))
        c.wait(gp, "ab_d", c.val("ab_d"))
        c.wait(gp, "r6n", c.val("r6n"))
        c.wait(gp, "r6z", c.val("r6z"))
        for (dstrow, srcrow) in ((4, 0 + hA), (5, 32 + hA), (12, 0 + hB), (13, 32 + hB)):
            c.dinc(gp.dma_start(out=rhs6m[dstrow:dstrow + 1, :],
                                in_=Mb[srcrow:srcrow + 1, :]), "d_r6")
        c.dinc(gp.dma_start(out=rhs6m[68:70, :], in_=rhs6m[12:14, :]), "d_r6")
        c.wait(ve, "d_r6", c.val("d_r6"))
        c.wait(ve, "sc_mm", c.val("sc_mm"))
        c.inc(ve.tensor_scalar(rhs6m[0:4, :], rhs6m[0:4, :], 0.0, a4[0:4, hA:hA + 1],
                               ALU.mult, ALU.add), "r6a")
        c.inc(ve.tensor_scalar(rhs6m[64:68, :], rhs6m[64:68, :], 0.0, a4[64:68, hB:hB + 1],
                               ALU.mult, ALU.add), "r6a")
        if stop_after == f"P{p}w3":
            break
        c.wait(ve, "r6a")
        c.inc(ve.tensor_scalar_mul(rhs6p[0:6, :], rhs6m[0:6, :], -1.0), "r6n")
        c.inc(ve.tensor_scalar_mul(rhs6p[64:70, :], rhs6m[64:70, :], -1.0), "r6n")

        if stop_after == f"P{p}w":
            break
        c.wait(sc, "pt_fix", c.val("pt_fix"))
        c.wait(sc, "tr_cp")
        c.wait(sc, "con_r")
        c.inc(sc.activation(e2diag[:, 0:128], R_relu, AF.Exp, scale=neg2a[:, hA:hA + 1]), "e2d")
        c.inc(sc.activation(e2diag[:, 128:256], R_relu, AF.Exp, scale=neg2a[:, hB:hB + 1]), "e2d")
        drain_w(4)

        c.wait(te, "vh_ev")
        c.wait(te, "exp", c.val("exp"))
        c.wait(te, "wq_l")
        qk_psum(bq_bf, "bqs", "q_mm")
        if stop_after == f"P{p}qm":
            break
        c.wait(sc, "q_mm")
        c.wait(sc, "sc_mm", c.val("sc_mm"))
        c.inc(sc.activation(q_hi, ps[0:128, 0:T], AF.Copy, scale=0.125), "q_h")
        c.wait(ve, "q_h")
        c.inc(ve.scalar_tensor_tensor(q_lo, ps[0:128, 0:T], 0.125, q_hi,
                                      ALU.mult, ALU.subtract), "q_l")
        c.wait(ve, "pv_g", c.val("pv_g"))
        c.wait(ve, "d_at", c.val("d_at"))
        if stop_after == f"P{p}qs":
            break
        c.inc(ve.tensor_mul(pt0[:, 0:T], q_hi, q_hi), "q_sq")
        if stop_after == f"P{p}n1":
            break

        c.wait(sy, "pv_g", c.val("pv_g"))
        c.wait(sy, "at_b", c.val("at_b"))
        c.wait(sy, "nc_b", c.val("nc_b"))
        c.wait(sy, "d_at", c.val("d_at"))
        if p == 0:
            c.wait(sy, "vh_mm", c.val("vh_mm"))
            c.wait(sy, "vh_ev", c.val("vh_ev"))
        c.dinc(sy.dma_start(out=pt1[:, 0:DC * 128], in_=wk_in[p]), "d_wk")
        c.wait(sc, "d_wk")
        c.wait(sc, "q_mm", c.val("q_mm"))
        c.inc(sc.activation(wqk_hi, pt1[:, 0:DC * 128], AF.Copy), "wk_h")
        c.wait(ve, "wk_h")
        c.inc(ve.tensor_sub(wqk_lo, pt1[:, 0:DC * 128], wqk_hi), "wk_l")
        c.wait(te, "q_l")
        c.wait(te, "wk_l")
        qk_psum(bk_bf, "bks", "k_mm")
        c.wait(sc, "k_mm")
        c.inc(sc.activation(k_hi, ps[0:128, 0:T], AF.Copy), "k_h")
        c.wait(ve, "k_h")
        c.inc(ve.tensor_sub(k_lo, ps[0:128, 0:T], k_hi), "k_l")

        if p + 1 < NP:
            c.wait(sy, "wq_l", c.val("wq_l"))
            c.wait(sy, "bqsv", c.val("bqsv"))
            c.wait(sy, "bksv", c.val("bksv"))
            c.dinc(sy.dma_start(out=wstage[:, 0:DC * 128], in_=wq_in[p + 1]), "d_wqs")
            c.dinc(sy.dma_start(out=wstage[0:1, DC * 128:DC * 128 + 128],
                                in_=bq_in[p + 1]), "d_wqs")
            c.dinc(sy.dma_start(out=wstage[0:1, DC * 128 + 128:DC * 128 + 256],
                                in_=bk_in[p + 1]), "d_wqs")
            c.wait(sc, "d_wqs")
            c.wait(sc, "k_mm")
            c.inc(sc.activation(wqk_hi, wstage[:, 0:DC * 128], AF.Copy), "wq_h")
            c.wait(ve, "wq_h")
            c.inc(ve.tensor_sub(wqk_lo, wstage[:, 0:DC * 128], wqk_hi), "wq_l")
            split_row(wstage[0:1, DC * 128:DC * 128 + 128],
                      bq_bf[0:1, 0:128], bq_bf[0:1, 128:256], "bqs")
            split_row(wstage[0:1, DC * 128 + 128:DC * 128 + 256],
                      bk_bf[0:1, 0:128], bk_bf[0:1, 128:256], "bks")
        if stop_after == f"P{p}w1":
            break

        for j in range(QCOL):
            c.wait(te, "q_sq")
            c.wait(te, "at_b", c.val("at_b"))
            c.wait(te, "qn_rx", max(0, c.val("qn_rx") - 1))
            c.inc(te.matmul(ps[0:1, 2048 + (j % 2) * 512: 2048 + (j % 2) * 512 + 512],
                            ones_row[:, 0:1], pt0[:, j * 512:(j + 1) * 512],
                            start=True, stop=True), "qn_mm")
            if stop_after == f"P{p}n2":
                break
            c.wait(ve, "qn_mm")
            c.inc(ve.reduce_max(smalls[0:1, j:j + 1],
                                ps[0:1, 2048 + (j % 2) * 512: 2048 + (j % 2) * 512 + 512],
                                axis=AX.X), "qn_rx")
        if stop_after in (f"P{p}n2", f"P{p}n3"):
            break
        c.wait(ve, "qn_rx")
        c.inc(ve.reduce_max(smalls[0:1, 8:9], smalls[0:1, 0:QCOL], axis=AX.X), "qn_m")
        if stop_after == f"P{p}qn":
            break

        c.wait(ve, "qn_mm", c.val("qn_mm"))
        c.inc(ve.tensor_mul(pt0[:, 0:T], k_hi, k_hi), "k_sq")
        c.wait(ve, "qn_m")
        for j in range(QCOL):
            c.wait(te, "k_sq")
            c.wait(te, "qn_rx", c.val("qn_rx"))
            c.wait(te, "kn_rx", max(0, c.val("kn_rx") - 1))
            c.inc(te.matmul(ps[0:1, 2048 + (j % 2) * 512: 2048 + (j % 2) * 512 + 512],
                            ones_row[:, 0:1], pt0[:, j * 512:(j + 1) * 512],
                            start=True, stop=True), "kn_mm")
            c.wait(ve, "kn_mm")
            c.inc(ve.reduce_max(smalls[0:1, j:j + 1],
                                ps[0:1, 2048 + (j % 2) * 512: 2048 + (j % 2) * 512 + 512],
                                axis=AX.X), "kn_rx")
        c.wait(ve, "kn_rx")
        c.inc(ve.reduce_max(smalls[0:1, 9:10], smalls[0:1, 0:QCOL], axis=AX.X), "kn_m")
        drain_w(4)
        if stop_after == f"P{p}k":
            break

        c.wait(sc, "qn_m")
        c.wait(sc, "kn_m")
        c.inc(sc.activation(smalls[0:1, 10:11], smalls[0:1, 8:9], AF.Sqrt), "c_s")
        c.inc(sc.activation(smalls[0:1, 11:12], smalls[0:1, 9:10], AF.Sqrt), "c_s")
        c.wait(ve, "c_s")
        c.inc(ve.tensor_mul(smalls[0:1, 12:13], smalls[0:1, 10:11], smalls[0:1, 11:12]), "c_m")
        c.wait(ve, "c_m")
        c.inc(ve.tensor_scalar_mul(smalls[0:1, 13:14], smalls[0:1, 12:13], -1.02), "c_n")
        c.wait(te, "c_n")
        c.wait(te, "kn_rx", c.val("kn_rx"))
        c.inc(te.matmul(ps[0:128, 2048:2049], ones_row[0:1, 0:128], smalls[0:1, 13:14],
                        start=True, stop=True), "c_bc")
        c.wait(sc, "c_bc")
        c.wait(sc, "exp", c.val("exp"))
        c.inc(sc.activation(negc, ps[0:128, 2048:2049], AF.Copy), "c_cp")

        if stop_after == f"P{p}qk":
            break
        c.wait(te, "k_l")
        c.wait(te, "k_sq", c.val("k_sq"))
        c.wait(te, "c_cp", c.val("c_cp"))
        c.wait(te, "d_r6")
        c.wait(te, "e2d", c.val("e2d"))

        def score_mms(h, qb, kc, b, out_list):
            base = 0 if h == hA else 64
            pb = ps[0:128, b * 512:(b + 1) * 512]
            kh = k_hi[base:base + 64, kc * 128:(kc + 1) * 128]
            kl = k_lo[base:base + 64, kc * 128:(kc + 1) * 128]
            qh = q_hi[base:base + 64, qb * 512:(qb + 1) * 512]
            ql = q_lo[base:base + 64, qb * 512:(qb + 1) * 512]
            out_list.append(lambda: c.inc(te.matmul(pb, kh, qh, start=True, stop=False), "sc_mm0"))
            out_list.append(lambda: c.inc(te.matmul(pb, kh, ql, start=False, stop=False), "sc_mm0"))
            out_list.append(lambda: c.inc(te.matmul(pb, kl, qh, start=False, stop=False), "sc_mm0"))
            lhs = lhsT6[base:base + 6, kc * 128:(kc + 1) * 128]
            r6m = rhs6m[base:base + 6, :]
            r6p = rhs6p[base:base + 6, :]
            q0, q1 = qb * 512, (qb + 1) * 512
            kstart, kend = kc * 128, kc * 128 + 128
            segs = []
            if q0 < min(kstart, q1):
                segs.append((q0, min(kstart, q1), r6p))
            dlo, dhi = max(q0, kstart), min(q1, kend)
            if dlo < dhi:
                segs.append((dlo, dhi, r6m))
            if max(q0, kend) < q1:
                segs.append((max(q0, kend), q1, r6m))
            for idx, (a0, a1, rsrc) in enumerate(segs):
                last = idx == len(segs) - 1
                out_list.append(lambda a0=a0, a1=a1, rsrc=rsrc, last=last:
                                c.inc(te.matmul(ps[0:128, b * 512 + (a0 - q0): b * 512 + (a1 - q0)],
                                                lhs, rsrc[:, a0:a1], start=False, stop=last),
                                      "sc_mm" if last else "sc_mm0"))

        def emit_pv(ptb, kc0, kc1):
            c.wait(te, "exp", c.val("exp"))
            c.wait(te, "pt_fix", c.val("pt_fix"))
            if kc0 == 0:
                c.wait(te, "at_b", c.val("at_b"))
            npv = 0
            for (h, kc, pcol) in ((hA, kc0, 0), (hA, kc1, 512), (hB, kc0, 1024), (hB, kc1, 1536)):
                b = 4 if h == hA else 5
                npv += 1
                c.inc(te.matmul(ps[0:65, b * 512:(b + 1) * 512],
                                vrr[:, kc, h, 0:65], ptb[:, pcol:pcol + 512],
                                start=(kc == 0), stop=(kc == KC - 1)),
                      "pv_g" if npv == 4 else "pv_mm0")

        for qb in range(QB):
            pending = None
            for kcg in range(KCG):
                kc0, kc1 = 2 * kcg, 2 * kcg + 1
                ptb = pt0 if kcg % 2 == 0 else pt1
                c.wait(te, "exp", c.val("exp"))
                mmsA, mmsB = [], []
                score_mms(hA, qb, kc0, 0, mmsA)
                score_mms(hA, qb, kc1, 1, mmsA)
                score_mms(hB, qb, kc0, 2, mmsB)
                score_mms(hB, qb, kc1, 3, mmsB)
                for i in range(max(len(mmsA), len(mmsB))):
                    if i < len(mmsA):
                        mmsA[i]()
                    if i < len(mmsB):
                        mmsB[i]()
                if pending is not None:
                    emit_pv(*pending)
                c.wait(sc, "sc_mm")
                c.wait(sc, "c_cp")
                c.wait(sc, "pv_g", max(0, c.val("pv_g") - 1))
                c.wait(sc, "at_b", c.val("at_b"))
                c.wait(sc, "nc_b", c.val("nc_b"))
                c.wait(sc, "d_at", c.val("d_at"))
                c.inc(sc.activation(ptb, ps[0:128, 0:2048], AF.Exp, bias=negc[:, 0:1]), "exp")
                for (h, col, pcol) in ((hA, 0, 0), (hA, 1, 512), (hB, 0, 1024), (hB, 1, 1536)):
                    kc = 2 * kcg + col
                    if kc * 128 >= qb * 512 and (kc + 1) * 128 <= (qb + 1) * 512:
                        off = pcol + (kc * 128 - qb * 512)
                        c.wait(ve, "exp")
                        c.inc(ve.tensor_mul(ptb[:, off:off + 128], ptb[:, off:off + 128],
                                            e2diag[:, (0 if h == hA else 128):(128 if h == hA else 256)]),
                              "pt_fix")
                pending = (ptb, kc0, kc1)
                drain_w(2 if p < NP // 2 else 3)
            emit_pv(*pending)

            c.wait(sc, "pv_g", c.val("pv_g"))
            c.wait(sc, "nb_a", c.val("nb_a"))
            c.wait(sc, "nb_b", c.val("nb_b"))
            c.inc(sc.activation(pt1[64:65, 0:512],
                                ps[64:65, 4 * 512:4 * 512 + 512], AF.Copy), "nr_a")
            c.inc(sc.activation(pt1[64:65, 512:1024],
                                ps[64:65, 5 * 512:5 * 512 + 512], AF.Copy), "nr_b")
            c.wait(te, "nr_a")
            c.wait(te, "nr_b")
            c.wait(te, "nc_b", c.val("nc_b"))
            c.inc(te.matmul(ps[0:64, 6 * 512:7 * 512], ones_row[64:65, 0:64],
                            pt1[64:65, 0:512], start=True, stop=True), "nb_a")
            c.inc(te.matmul(ps[0:64, 7 * 512:8 * 512], ones_row[64:65, 0:64],
                            pt1[64:65, 512:1024], start=True, stop=True), "nb_b")
            c.wait(ve, "nb_a")
            c.wait(ve, "nb_b")
            c.wait(ve, "at_b", c.val("at_b"))
            c.inc(ve.reciprocal(pt1[0:64, 0:512], ps[0:64, 6 * 512:7 * 512]), "nc_a")
            c.inc(ve.reciprocal(pt1[0:64, 512:1024], ps[0:64, 7 * 512:8 * 512]), "nc_b")
            c.wait(ve, "nc_a")
            c.wait(ve, "nc_b")
            c.inc(ve.tensor_mul(pt0[0:64, 0:512], ps[0:64, 4 * 512:5 * 512],
                                pt1[0:64, 0:512]), "at_a")
            c.inc(ve.tensor_mul(pt0[0:64, 512:1024], ps[0:64, 5 * 512:6 * 512],
                                pt1[0:64, 512:1024]), "at_b")
            c.wait(gp, "at_b")
            c.dinc(gp.dma_start(out=attn_scr[hA, qb], in_=pt0[0:64, 0:512]), "d_at")
            c.dinc(gp.dma_start(out=attn_scr[hB, qb], in_=pt0[0:64, 512:1024]), "d_at")
            drain_w(2)
            if stop_after == f"P{p}q{qb}":
                break

        if stop_after is not None and stop_after.startswith(f"P{p}"):
            break

    if stop_after == "B" or (stop_after or "").startswith("P"):
        _early_finish()
        return nc
    c.wait(sy, "d_at", c.val("d_at"))
    c.wait(sy, "pv_g", c.val("pv_g"))
    c.wait(sy, "at_b", c.val("at_b"))
    c.wait(sy, "nc_b", c.val("nc_b"))
    c.wait(sy, "exp", c.val("exp"))
    c.wait(te, "exp", c.val("exp"))
    c.wait(te, "at_b", c.val("at_b"))
    c.wait(te, "nc_b", c.val("nc_b"))
    while next(wgen, _W_DONE) is not _W_DONE:
        pass
    c.wait(sy, "d_wto")
    c.wait(sy, "d_wtoB")
    c.wait(sy, "wt_m", c.val("wt_m"))
    phW.close()
    phB.close()

    phC = ExitStack()
    wo_hi = salloc(phC, "wo_hi", [128, DC * FC * 128], BF16)
    wo_lo = salloc(phC, "wo_lo", [128, DC * FC * 128], BF16)
    wo_st = salloc(phC, "wo_st", [128, 2048], F32)
    bo_bf = salloc(phC, "bo_bf", [128, 2 * D], BF16)

    c.dinc(sy.dma_start(out=wo_st[0:1, 0:D], in_=bo_in), "d_bo")
    c.wait(sc, "d_bo")
    split_row(wo_st[0:1, 0:D], bo_bf[0:1, 0:D], bo_bf[0:1, D:2 * D], "b_o")
    attn_f = salloc(phC, "attn_f", [128, DC * 512], F32)
    attn_h2 = salloc(phC, "attn_h2", [128, DC * 512], BF16)
    attn_l2 = salloc(phC, "attn_l2", [128, DC * 512], BF16)
    s_qb = salloc(phC, "s_qb", [128, FC * 512], F32)

    NWO = (DC * FC * 128) // 2048
    for i in range(NWO):
        c.wait(sy, "wo_sp", c.val("wo_sp"))
        c.wait(sy, "b_ov", c.val("b_ov"))
        c.dinc(sy.dma_start(out=wo_st, in_=wo_in[:, i * 2048:(i + 1) * 2048]), "d_wo")
        c.wait(sc, "d_wo")
        c.inc(sc.activation(wo_hi[:, i * 2048:(i + 1) * 2048], wo_st, AF.Copy), "wo_h")
        c.wait(ve, "wo_h")
        c.inc(ve.tensor_sub(wo_lo[:, i * 2048:(i + 1) * 2048], wo_st,
                            wo_hi[:, i * 2048:(i + 1) * 2048]), "wo_sp")

    for qb in range(QB):
        c.wait(sy, "c_sp", c.val("c_sp"))
        for dc in range(DC):
            c.dinc(sy.dma_start(out=attn_f[0:64, dc * 512:(dc + 1) * 512],
                                in_=attn_scr[2 * dc, qb]), "d_af")
            c.dinc(sy.dma_start(out=attn_f[64:128, dc * 512:(dc + 1) * 512],
                                in_=attn_scr[2 * dc + 1, qb]), "d_af")
        c.wait(sc, "d_af")
        c.wait(sc, "s_mm", c.val("s_mm"))
        c.inc(sc.activation(attn_h2, attn_f, AF.Copy), "c_h")
        c.wait(ve, "c_h")
        c.inc(ve.tensor_sub(attn_l2, attn_f, attn_h2), "c_sp")
        c.wait(ve, "s_w", c.val("s_w"))
        c.wait(te, "c_sp")
        c.wait(te, "wo_sp")
        c.wait(te, "b_o")
        for fc in range(FC):
            b = fc % 2
            c.wait(te, "s_ev", max(0, c.val("s_ev") - 1))
            c.inc(te.matmul(ps[0:128, b * 512:(b + 1) * 512], bo_bf[0:1, fc * 128:(fc + 1) * 128],
                            ones_bf[0:1, 0:512], start=True, stop=False), "s_mm0")
            c.inc(te.matmul(ps[0:128, b * 512:(b + 1) * 512], bo_bf[0:1, D + fc * 128: D + (fc + 1) * 128],
                            ones_bf[0:1, 0:512], start=False, stop=False), "s_mm0")
            for dc in range(DC):
                w_hi = wo_hi[:, (dc * FC + fc) * 128:(dc * FC + fc) * 128 + 128]
                w_lo = wo_lo[:, (dc * FC + fc) * 128:(dc * FC + fc) * 128 + 128]
                ah = attn_h2[:, dc * 512:(dc + 1) * 512]
                al = attn_l2[:, dc * 512:(dc + 1) * 512]
                last = dc == DC - 1
                c.inc(te.matmul(ps[0:128, b * 512:(b + 1) * 512], w_hi, ah,
                                start=False, stop=False), "s_mm0")
                c.inc(te.matmul(ps[0:128, b * 512:(b + 1) * 512], w_lo, ah,
                                start=False, stop=False), "s_mm0")
                c.inc(te.matmul(ps[0:128, b * 512:(b + 1) * 512], w_hi, al,
                                start=False, stop=last), "s_mm" if last else "s_mm0")
            c.wait(ve, "s_mm")
            c.inc(ve.tensor_copy(s_qb[:, fc * 512:(fc + 1) * 512],
                                 ps[0:128, b * 512:(b + 1) * 512]), "s_ev")
        c.wait(gp, "s_ev")
        for fc in range(FC):
            c.dinc(gp.dma_start(out=s_scr[fc, qb], in_=s_qb[:, fc * 512:(fc + 1) * 512]), "s_w")

    c.wait(sy, "s_w")
    if stop_after == "C":
        _early_finish()
        return nc
    phC.close()

    phD1 = ExitStack()
    wt_up = salloc(phD1, "wt_up", [128, DC * HC * 128], BF16)
    hpre = salloc(phD1, "hpre", [128, 2 * 512], F32)
    hful = salloc(phD1, "hful", [128, 2 * 512], F32)
    h2b = salloc(phD1, "h2b", [128, 2 * 512], F32)
    hhi = salloc(phD1, "hhi", [128, 2 * 512], BF16)
    hlo = salloc(phD1, "hlo", [128, 2 * 512], BF16)
    rfb = salloc(phD1, "rfb", [128, 512], F32)
    rowd = salloc(phD1, "rowd", [128, 1024], F32)

    for ch, l in enumerate("ABCDEFGH"):
        w8 = (DC * HC * 128) // 8
        c.inc(sy.dma_start(out=wt_up[:, ch * w8:(ch + 1) * w8],
                           in_=wup_t_scr[:, ch * w8:(ch + 1) * w8]), f"wtu{l}", 16)
    c.wait(te, "s_ev", c.val("s_ev"))
    c.wait(sc, "s_w", c.val("s_w"))
    c.wait(ve, "s_w", c.val("s_w"))

    for tb in range(TB):
        c.wait(te, "rfb_c", c.val("rfb_c"))
        c.wait(te, "r_rf", c.val("r_rf"))
        c.inc(te.matmul(ps[0:128, 6 * 512:7 * 512], ones_row[0:1, 0:128],
                        rowbuf[0:1, tb * 512:(tb + 1) * 512], start=True, stop=True), "rfb_m")
        c.wait(sc, "rfb_m")
        c.wait(sc, "up_ev", c.val("up_ev"))
        c.inc(sc.activation(rfb, ps[0:128, 6 * 512:7 * 512], AF.Copy), "rfb_c")
        for l in "ABCDEFGH":
            c.wait(te, f"wtu{l}", 16)
        c.wait(te, "rfb_c")
        for hc in range(HC):
            b = hc % 2
            c.wait(te, "up_ev", max(0, c.val("up_ev") - 1))
            for dc in range(DC):
                wt = wt_up[:, (dc * HC + hc) * 128:(dc * HC + hc) * 128 + 128]
                xh = xn_hi[:, dc * T + tb * 512: dc * T + (tb + 1) * 512]
                xl = xn_lo[:, dc * T + tb * 512: dc * T + (tb + 1) * 512]
                c.inc(te.matmul(ps[0:128, b * 512:(b + 1) * 512], wt, xh,
                                start=(dc == 0), stop=False), "up_mm0")
                c.inc(te.matmul(ps[0:128, b * 512:(b + 1) * 512], wt, xl,
                                start=False, stop=(dc == DC - 1)),
                      "up_mm" if dc == DC - 1 else "up_mm0")
            c.wait(ve, "up_mm")
            c.wait(ve, "g_h", max(0, c.val("g_h") - 1))
            c.inc(ve.tensor_mul(hpre[:, b * 512:(b + 1) * 512],
                                ps[0:128, b * 512:(b + 1) * 512], rfb), "up_ev")
            c.wait(sc, "up_ev")
            c.wait(sc, "h_lo", max(0, c.val("h_lo") - 1))
            c.inc(sc.activation(hful[:, b * 512:(b + 1) * 512],
                                hpre[:, b * 512:(b + 1) * 512], AF.Gelu), "g_h")
            c.wait(sc, "g_h")
            c.wait(sc, "d_h" if b == 0 else "d_hB")
            c.inc(sc.activation(hhi[:, b * 512:(b + 1) * 512],
                                hful[:, b * 512:(b + 1) * 512], AF.Copy), "h_hi")
            c.wait(ve, "h_hi")
            c.inc(ve.tensor_sub(hlo[:, b * 512:(b + 1) * 512],
                                hful[:, b * 512:(b + 1) * 512],
                                hhi[:, b * 512:(b + 1) * 512]), "h_lo")
            c.wait(sc, "h2_rd", max(0, c.val("h2_rd") - 1))
            c.inc(sc.activation(h2b[:, b * 512:(b + 1) * 512],
                                hful[:, b * 512:(b + 1) * 512], AF.Square), "h_sq")
            c.wait(te, "h_sq")
            c.inc(te.matmul(ps[0:1, 7 * 512:7 * 512 + 512], ones_row[:, 0:1],
                            h2b[:, b * 512:(b + 1) * 512],
                            start=(hc == 0), stop=(hc == HC - 1)), "h2_rd")
            c.wait(gp, "h_lo")
            htag = "d_h" if b == 0 else "d_hB"
            c.dinc(gp.dma_start(out=h_scr_hi[tb][:, hc * 512:(hc + 1) * 512],
                                in_=hhi[:, b * 512:(b + 1) * 512]), htag)
            c.dinc(gp.dma_start(out=h_scr_lo[tb][:, hc * 512:(hc + 1) * 512],
                                in_=hlo[:, b * 512:(b + 1) * 512]), htag)
        c.wait(ve, "h2_rd")
        c.wait(ve, "rh_d", c.val("rh_d"))
        c.inc(ve.tensor_scalar_mul(rowd[0:1, 0:512], ps[0:1, 7 * 512:7 * 512 + 512],
                                   1.0 / DFF), "rh_1")
        c.wait(sc, "rh_1")
        c.inc(sc.activation(rowd[0:1, 512:1024], rowd[0:1, 0:512], AF.Sqrt), "rh_2")
        c.wait(ve, "rh_2")
        c.inc(ve.tensor_scalar_add(rowd[0:1, 0:512], rowd[0:1, 512:1024], 1e-8), "rh_3")
        c.wait(ve, "rh_3")
        c.inc(ve.reciprocal(rowd[0:1, 512:1024], rowd[0:1, 0:512]), "rh_4")
        c.wait(gp, "rh_4")
        c.dinc(gp.dma_start(out=rowbuf[32:33, tb * 512:(tb + 1) * 512],
                            in_=rowd[0:1, 512:1024]), "rh_d")

    c.wait(sy, "d_h")
    c.wait(sy, "d_hB")
    c.wait(sy, "rh_d")
    if stop_after == "D1":
        _early_finish()
        return nc
    phD1.close()
    xn_stack.close()

    phD2 = ExitStack()
    wt_dn = salloc(phD2, "wt_dn", [128, HC * FC * 128], BF16)
    h_ahi = salloc(phD2, "h_ahi", [128, HC * 512], BF16)
    h_alo = salloc(phD2, "h_alo", [128, HC * 512], BF16)
    gbuf = salloc(phD2, "gbuf", [128, 2 * 512], F32)
    xsbuf = salloc(phD2, "xsbuf", [128, 4 * 512], F32)
    rhb = salloc(phD2, "rhb", [128, 512], F32)

    for ch, l in enumerate("ABCDEFGH"):
        w8 = (HC * FC * 128) // 8
        c.inc(sy.dma_start(out=wt_dn[:, ch * w8:(ch + 1) * w8],
                           in_=wdn_t_scr[:, ch * w8:(ch + 1) * w8]), f"wtd{l}", 16)
    c.wait(te, "up_ev", c.val("up_ev"))
    c.wait(te, "rfb_c", c.val("rfb_c"))
    c.wait(te, "rh_1", c.val("rh_1"))

    for tb in range(TB):
        c.wait(sy, "dn_mm", c.val("dn_mm"))
        for l in "ABCDEFGH":
            c.wait(sy, f"ha{l}", 16 * tb)
        for ch in range(4):
            w4 = (HC * 512) // 4
            c.inc(sy.dma_start(out=h_ahi[:, ch * w4:(ch + 1) * w4],
                               in_=h_scr_hi[tb][:, ch * w4:(ch + 1) * w4]),
                  "ha" + "ABCDEFGH"[2 * ch], 16)
            c.inc(sy.dma_start(out=h_alo[:, ch * w4:(ch + 1) * w4],
                               in_=h_scr_lo[tb][:, ch * w4:(ch + 1) * w4]),
                  "ha" + "ABCDEFGH"[2 * ch + 1], 16)
        c.wait(te, "rhb_c", c.val("rhb_c"))
        c.wait(te, "rh_d")
        c.inc(te.matmul(ps[0:128, 6 * 512:7 * 512], ones_row[32:33, 0:128],
                        rowbuf[32:33, tb * 512:(tb + 1) * 512], start=True, stop=True), "rhb_m")
        c.wait(sc, "rhb_m")
        c.wait(sc, "dn_ev", c.val("dn_ev"))
        c.inc(sc.activation(rhb, ps[0:128, 6 * 512:7 * 512], AF.Copy), "rhb_c")
        for l in "ABCDEFGH":
            c.wait(te, f"wtd{l}", 16)
        for l in "ABCDEFGH":
            c.wait(te, f"ha{l}", 16 * (tb + 1))
        c.wait(te, "rhb_c")
        for fc in range(FC):
            b = fc % 2
            c.wait(te, "dn_ev", max(0, c.val("dn_ev") - 1))
            for hc in range(HC):
                wt = wt_dn[:, (hc * FC + fc) * 128:(hc * FC + fc) * 128 + 128]
                hh = h_ahi[:, hc * 512:(hc + 1) * 512]
                hl = h_alo[:, hc * 512:(hc + 1) * 512]
                c.inc(te.matmul(ps[0:128, b * 512:(b + 1) * 512], wt, hh,
                                start=(hc == 0), stop=False), "dn_mm0")
                c.inc(te.matmul(ps[0:128, b * 512:(b + 1) * 512], wt, hl,
                                start=False, stop=(hc == HC - 1)),
                      "dn_mm" if hc == HC - 1 else "dn_mm0")
            xoff = 0 if fc % 2 == 0 else 1024
            c.wait(sy, "fin_o", max(0, c.val("fin_o") - 16))
            c.dinc(sy.dma_start(out=xsbuf[:, xoff:xoff + 512],
                                in_=x_in[:, fc * T + tb * 512: fc * T + (tb + 1) * 512]), "d_xs")
            c.dinc(sy.dma_start(out=xsbuf[:, xoff + 512:xoff + 1024],
                                in_=s_scr[fc, tb]), "d_xs")
            c.wait(ve, "dn_mm")
            c.wait(ve, "g_s", max(0, c.val("g_s") - 1))
            c.wait(ve, "fin_o", max(0, c.val("fin_o") - 16))
            c.inc(ve.tensor_mul(gbuf[:, b * 512:(b + 1) * 512],
                                ps[0:128, b * 512:(b + 1) * 512], rhb), "dn_ev")
            c.wait(sc, "dn_ev")
            c.wait(sc, "fin_m", max(0, c.val("fin_m") - 1))
            c.inc(sc.activation(gbuf[:, b * 512:(b + 1) * 512],
                                gbuf[:, b * 512:(b + 1) * 512], AF.Silu), "g_s")
            c.wait(ve, "g_s")
            c.wait(ve, "d_xs")
            c.inc(ve.tensor_mul(gbuf[:, b * 512:(b + 1) * 512],
                                gbuf[:, b * 512:(b + 1) * 512],
                                xsbuf[:, xoff + 512:xoff + 1024]), "fin_m")
            c.wait(ve, "fin_m")
            c.inc(ve.tensor_add(gbuf[:, b * 512:(b + 1) * 512],
                                gbuf[:, b * 512:(b + 1) * 512],
                                xsbuf[:, xoff:xoff + 512]), "fin_a")
            c.wait(gp, "fin_a")
            c.dinc(gp.dma_start(out=out_hbm[:, fc * T + tb * 512: fc * T + (tb + 1) * 512],
                                in_=gbuf[:, b * 512:(b + 1) * 512]), "fin_o")

    c.wait(gp, "fin_o")
    phD2.close()
    return nc


def _prep_core(xb, qkv_w, qkv_b, out_w, out_b, alpha, w_up, w_down,
               T=2048, D=1024, H=16, DFF=4096):
    DC, HC, FC, NP = D // 128, DFF // 128, D // 128, H // 2

    x_prep = np.ascontiguousarray(
        xb.T.reshape(DC, 128, T).transpose(1, 0, 2).reshape(128, DC * T))
    wq = np.empty((NP, 128, DC * 128), np.float32)
    wk = np.empty((NP, 128, DC * 128), np.float32)
    for p in range(NP):
        Wq = qkv_w[128 * p:128 * p + 128, :]
        Wk = qkv_w[D + 128 * p: D + 128 * p + 128, :]
        wq[p] = Wq.reshape(128, DC, 128).transpose(2, 1, 0).reshape(128, DC * 128)
        wk[p] = Wk.reshape(128, DC, 128).transpose(2, 1, 0).reshape(128, DC * 128)
    Wv = qkv_w[2 * D:3 * D, :]
    wv = Wv.reshape(D, DC, 128).transpose(2, 1, 0).reshape(128, DC * D)
    bq = np.ascontiguousarray(qkv_b[0:D].reshape(NP, 1, 128))
    bk = np.ascontiguousarray(qkv_b[D:2 * D].reshape(NP, 1, 128))
    bv = np.ascontiguousarray(qkv_b[2 * D:3 * D].reshape(1, D))
    wo = out_w.reshape(FC, 128, DC, 128).transpose(3, 2, 0, 1).reshape(128, DC * FC * 128)
    bo = np.ascontiguousarray(out_b.reshape(1, D))
    wup = w_up.reshape(HC, 128, DC, 128).transpose(3, 2, 0, 1).reshape(128, DC * HC * 128)
    wdn = w_down.reshape(FC, 128, HC, 128).transpose(3, 2, 0, 1).reshape(128, HC * FC * 128)
    al = np.ascontiguousarray(alpha.reshape(H, 1))
    return {
        "x_in": x_prep, "wq_in": np.ascontiguousarray(wq), "wk_in": np.ascontiguousarray(wk),
        "wv_in": np.ascontiguousarray(wv), "bq_in": bq, "bk_in": bk, "bv_in": bv,
        "wo_in": np.ascontiguousarray(wo), "bo_in": bo,
        "wup_in": np.ascontiguousarray(wup), "wdn_in": np.ascontiguousarray(wdn),
        "alpha_in": al,
    }


def kernel(L0_attn_x, L0_attn_qkv_w, L0_attn_qkv_b, L0_attn_out_w, L0_attn_out_b,
           L0_attn_decay_alpha, L0_ffn_w_up, L0_ffn_w_down):
    T, D, H, DFF = 2048, 1024, 16, 4096
    x = np.ascontiguousarray(np.asarray(L0_attn_x, dtype=np.float32))
    qkv_w = np.ascontiguousarray(np.asarray(L0_attn_qkv_w, dtype=np.float32))
    qkv_b = np.asarray(L0_attn_qkv_b, dtype=np.float32)
    out_w = np.ascontiguousarray(np.asarray(L0_attn_out_w, dtype=np.float32))
    out_b = np.asarray(L0_attn_out_b, dtype=np.float32)
    alpha = np.asarray(L0_attn_decay_alpha, dtype=np.float32)
    w_up = np.ascontiguousarray(np.asarray(L0_ffn_w_up, dtype=np.float32))
    w_down = np.ascontiguousarray(np.asarray(L0_ffn_w_down, dtype=np.float32))

    nc = build_program(T=T, D=D, H=H, DFF=DFF)
    in_maps = [
        _prep_core(x[b], qkv_w, qkv_b, out_w, out_b, alpha, w_up, w_down,
                   T=T, D=D, H=H, DFF=DFF)
        for b in range(N_CORES)
    ]
    res = run_bass_kernel_spmd(nc, in_maps, core_ids=list(range(N_CORES)))
    FC = D // 128
    out = np.empty((N_CORES, T, D), np.float32)
    for b in range(N_CORES):
        o = np.asarray(res.results[b]["out_hbm"])
        out[b] = o.reshape(128, FC, T).transpose(1, 0, 2).reshape(D, T).T
    return out


# version 18
# speedup vs baseline: 1.0416x; 1.0089x over previous
from contextlib import ExitStack

import numpy as np
import ml_dtypes

import concourse.bass as bass
import concourse.mybir as mybir
from concourse.bass_utils import run_bass_kernel_spmd

F32 = mybir.dt.float32
BF16 = mybir.dt.bfloat16
AF = mybir.ActivationFunctionType
ALU = mybir.AluOpType
AX = mybir.AxisListType

N_CORES = 8


_DED = {"pv_g", "vh_ev", "s_ev", "up_ev", "dn_ev", "g_h", "h_lo", "h2_rd",
        "d_h", "d_hB", "d_wto", "d_wtoB", "d_ws", "d_wsB", "d_wt", "d_wtB",
        "fin_o", "qn_rx", "kn_rx", "g_s", "fin_m", "con_g",
        "ws_abs", "ws_add", "wt_abs", "wt_g", "wt_m", "w_tcp",
        *[f"dx{l}" for l in "ABCDEFGH"], *[f"wtu{l}" for l in "ABCDEFGH"],
        *[f"wtd{l}" for l in "ABCDEFGH"], *[f"ha{l}" for l in "ABCDEFGH"],
        *[f"wv{l}" for l in "ABCDEFGH"]}

_GROUPS = {
    "g_te": ["tr_mm", "ln_sx", "ln_x2mm", "bc_m", "bc_r", "q_mm", "k_mm",
             "qn_mm", "kn_mm", "c_bc", "sc_mm", "vh_mm", "pv_mm", "s_mm",
             "up_mm", "dn_mm", "rfb_m", "rhb_m", "w_smm", "w_tbc",
             "nb_a", "nb_b"],
    "g_sc": ["al_e", "al_a", "al_h", "al_lb", "ab_h", "ab_lb", "mb_hi", "jr_h", "tr_cp",
             "q_h", "k_h", "c_s", "c_cp", "e2d", "exp",
             "nr_a", "nr_b", "wo_h", "wv_h", "c_h", "h_hi", "h_sq",
             "rfb_c", "rhb_c", "ws_abs", "wt_abs", "w_tcp", "ln_sq",
             "wq_h", "wk_h", "tr_1", "tr_2",
             "r_m2", "r_sq", "r_w", "r_rs", "r_t", "r_rf", "r_u",
             "xn_hic", "b_v", "b_o", "bqs", "bks",
             "rh_2"],
    "g_ve": ["con_v", "con_r", "al_n", "al_h2", "al_r", "al_l2", "mb_f",
             "mb_lo", "ab_z", "ab_h2", "ab_l", "ab_l2", "r6n", "r6z", "jr_1", "jr_l",
             "tr_3", "b_vv", "b_ov", "bqsv", "bksv",
             "r_m", "r_e2", "r_var", "r_tz", "r_u1", "r_v8",
             "xn_t1", "xn_f", "xn_lo", "wq_l", "wk_l", "q_l",
             "k_l", "q_sq", "k_sq", "r6a", "qn_m", "kn_m", "c_m", "c_n", "nc_a", "nc_b", "at_a",
             "at_b", "pt_fix", "vh_1", "wv_sp", "wo_sp", "c_sp", "wt_g",
             "wt_m", "wsz", "ws_add", "w_th", "rh_1", "rh_3",
             "rh_4", "fin_a"],
    "g_gp": ["d_l6", "d_r6", "d_at", "s_w", "rh_d", "r_rsd",
             "r_rfd", "ab_d"],
    "g_sy": ["d_x", "d_al", "d_al2", "d_bv", "d_bo", "d_wv", "d_wq", "d_wqs", "d_wk",
             "d_wo", "d_af", "d_wtu", "d_wtd", "d_ha", "d_xs"],
}
_TAG2SEM = {t: g for g, ts in _GROUPS.items() for t in ts}


class Ctx:

    def __init__(self, nc):
        self.nc = nc
        self.sems = {}
        self.cnt = {}

    def phys(self, name):
        if name in _DED:
            return name
        return _TAG2SEM[name]

    def S(self, name):
        ph = self.phys(name)
        if ph not in self.sems:
            self.sems[ph] = self.nc.alloc_semaphore(ph)
            self.cnt[ph] = 0
        return self.sems[ph]

    def inc(self, instr, name, n=1):
        if name.endswith("0"):
            return instr
        instr.then_inc(self.S(name), n)
        self.cnt[self.phys(name)] += n
        return instr

    def dinc(self, instr, name):
        self.inc(instr, name, 16)
        eng = self.engines[instr.ins.engine]
        eng.wait_ge(self.S(name), self.cnt[self.phys(name)])
        return instr

    def wait(self, eng, name, val=None):
        ph = self.phys(name)
        v = self.cnt.get(ph, 0) if val is None else val
        if v > 0:
            eng.wait_ge(self.S(name), v)

    def val(self, name):
        return self.cnt.get(self.phys(name), 0)


def build_program(T=2048, D=1024, H=16, DFF=4096, stop_after=None):
    HD = 64
    NP = H // 2
    DC = D // 128
    KC = T // 128
    QB = T // 512
    HC = DFF // 128
    FC = DC
    TB = QB
    QCOL = T // 512
    KCG = KC // 2

    nc = bass.Bass("TRN2", target_bir_lowering=False)
    c = Ctx(nc)
    sy, te, ve, sc, gp = nc.sync, nc.tensor, nc.vector, nc.scalar, nc.gpsimd
    c.engines = {e.engine: e for e in (sy, te, ve, sc, gp)}

    x_in = nc.dram_tensor("x_in", [128, DC * T], F32, kind="ExternalInput").ap()
    wq_in = nc.dram_tensor("wq_in", [NP, 128, DC * 128], F32, kind="ExternalInput").ap()
    wk_in = nc.dram_tensor("wk_in", [NP, 128, DC * 128], F32, kind="ExternalInput").ap()
    wv_in = nc.dram_tensor("wv_in", [128, DC * D], F32, kind="ExternalInput").ap()
    bq_in = nc.dram_tensor("bq_in", [NP, 1, 128], F32, kind="ExternalInput").ap()
    bk_in = nc.dram_tensor("bk_in", [NP, 1, 128], F32, kind="ExternalInput").ap()
    bv_in = nc.dram_tensor("bv_in", [1, D], F32, kind="ExternalInput").ap()
    wo_in = nc.dram_tensor("wo_in", [128, DC * FC * 128], F32, kind="ExternalInput").ap()
    bo_in = nc.dram_tensor("bo_in", [1, D], F32, kind="ExternalInput").ap()
    wup_in = nc.dram_tensor("wup_in", [128, DC * HC * 128], F32, kind="ExternalInput").ap()
    wdn_in = nc.dram_tensor("wdn_in", [128, HC * FC * 128], F32, kind="ExternalInput").ap()
    alpha_in = nc.dram_tensor("alpha_in", [H, 1], F32, kind="ExternalInput").ap()

    attn_scr = nc.dram_tensor("attn_scr", [H, QB, 64, 512], F32, kind="Internal").ap()
    s_scr = nc.dram_tensor("s_scr", [FC, TB, 128, 512], F32, kind="Internal").ap()
    h_scr_hi = nc.dram_tensor("h_scr_hi", [TB, 128, HC * 512], BF16, kind="Internal").ap()
    h_scr_lo = nc.dram_tensor("h_scr_lo", [TB, 128, HC * 512], BF16, kind="Internal").ap()
    wup_t_scr = nc.dram_tensor("wup_t_scr", [128, DC * HC * 128], BF16, kind="Internal").ap()
    wdn_t_scr = nc.dram_tensor("wdn_t_scr", [128, HC * FC * 128], BF16, kind="Internal").ap()

    out_hbm = nc.dram_tensor("out_hbm", [128, FC * T], F32, kind="ExternalOutput").ap()

    ps = nc.alloc_psum_tensor("ps", [128, 4096], F32).ap()

    ones_row = nc.alloc_sbuf_tensor("ones_row", [128, 128], F32).ap()
    ones_bf = nc.alloc_sbuf_tensor("ones_bf", [128, 512], BF16).ap()
    lhsT6 = nc.alloc_sbuf_tensor("lhsT6", [128, T], BF16).ap()
    Mb = nc.alloc_sbuf_tensor("Mb", [128, T], BF16).ap()
    R_relu = nc.alloc_sbuf_tensor("R_relu", [128, 128], F32).ap()
    alpha_c = nc.alloc_sbuf_tensor("alpha_c", [16, 8], F32).ap()
    tiny_bf = nc.alloc_sbuf_tensor("tiny_bf", [16, 8], BF16).ap()
    arow = nc.alloc_sbuf_tensor("arow", [128, 96], F32).ap()
    a4 = nc.alloc_sbuf_tensor("a4", [128, 16], F32).ap()
    neg2a = nc.alloc_sbuf_tensor("neg2a", [128, 16], F32).ap()
    rowbuf = nc.alloc_sbuf_tensor("rowbuf", [128, T], F32).ap()
    thr = nc.alloc_sbuf_tensor("thr", [128, 2], F32).ap()
    smalls = nc.alloc_sbuf_tensor("smalls", [128, 16], F32).ap()
    negc = nc.alloc_sbuf_tensor("negc", [128, 1], F32).ap()
    bq_bf = nc.alloc_sbuf_tensor("bq_bf", [128, 256], BF16).ap()
    bk_bf = nc.alloc_sbuf_tensor("bk_bf", [128, 256], BF16).ap()
    e2diag = nc.alloc_sbuf_tensor("e2diag", [128, 256], F32).ap()

    def salloc(stack, name, shape, dtype):
        return stack.enter_context(nc.sbuf_tensor(name, list(shape), dtype)).ap()


    def _early_finish():
        c.wait(gp, "sc_mm", c.val("sc_mm"))
        c.dinc(gp.dma_start(out=out_hbm[0:1, 0:4], in_=ones_row[0:1, 0:4]), "fin_o")
        c.wait(gp, "fin_o")
    xn_stack = ExitStack()
    xn_hi = salloc(xn_stack, "xn_hi", [128, DC * T], BF16)
    xn_lo = salloc(xn_stack, "xn_lo", [128, DC * T], BF16)

    def split_row(src_f32_row0, hi_row0, lo_row0, tag):
        c.inc(sc.activation(hi_row0, src_f32_row0, AF.Copy), tag)
        c.wait(ve, tag)
        c.inc(ve.tensor_sub(lo_row0, src_f32_row0, hi_row0), tag + "v")

    c.inc(ve.memset(ones_row, 1.0), "con_v")
    c.inc(ve.memset(ones_bf, 1.0), "con_v")
    c.inc(gp.iota(R_relu, pattern=[[-1, 128]], base=0, channel_multiplier=1,
                  allow_small_or_imprecise_dtypes=True), "con_g")
    c.wait(ve, "con_g", 1)
    c.inc(ve.tensor_scalar_max(R_relu, R_relu, 0.0), "con_r")

    phA = ExitStack()
    x_res = salloc(phA, "x_res", [128, DC * T], F32)
    scrA = salloc(phA, "scrA", [128, T], F32)
    scrA2 = salloc(phA, "scrA2", [128, T], F32)
    iota16 = salloc(phA, "iota16", [16, T], F32)
    jrows = salloc(phA, "jrows", [128, T], BF16)
    jrows2 = salloc(phA, "jrows2", [128, T], BF16)
    jrows3 = salloc(phA, "jrows3", [128, T], BF16)

    for ch, l in enumerate("ABCDEFGH"):
        w8 = (DC * T) // 8
        c.inc(sy.dma_start(out=x_res[:, ch * w8:(ch + 1) * w8],
                           in_=x_in[:, ch * w8:(ch + 1) * w8]), f"dx{l}", 16)
    c.dinc(sy.dma_start(out=alpha_c[0:H, 0:1], in_=alpha_in[0:H, 0:1]), "d_al")

    c.inc(gp.iota(iota16, pattern=[[1, T]], base=0, channel_multiplier=0,
                  allow_small_or_imprecise_dtypes=True), "con_g")

    c.wait(sc, "d_al")
    c.inc(sc.activation(alpha_c[0:H, 1:2], alpha_c[0:H, 0:1], AF.Exp), "al_e")
    c.wait(sc, "al_e")
    c.inc(sc.activation(alpha_c[0:H, 2:3], alpha_c[0:H, 1:2], AF.Ln, bias=1.0), "al_a")
    c.wait(ve, "al_a")
    c.inc(ve.tensor_scalar_mul(alpha_c[0:H, 3:4], alpha_c[0:H, 2:3], -1.0), "al_n")
    c.wait(sc, "al_n")
    c.inc(sc.activation(tiny_bf[0:H, 0:1], alpha_c[0:H, 2:3], AF.Copy), "al_h")
    c.wait(ve, "al_h")
    c.inc(ve.tensor_copy(alpha_c[0:H, 4:5], tiny_bf[0:H, 0:1]), "al_h2")
    c.wait(ve, "al_h2")
    c.inc(ve.tensor_sub(alpha_c[0:H, 5:6], alpha_c[0:H, 2:3], alpha_c[0:H, 4:5]), "al_r")
    c.wait(sc, "al_r")
    c.inc(sc.activation(tiny_bf[0:H, 1:2], alpha_c[0:H, 5:6], AF.Copy), "al_lb")
    c.wait(ve, "al_lb")
    c.inc(ve.tensor_copy(alpha_c[0:H, 6:7], tiny_bf[0:H, 1:2]), "al_l2")

    c.wait(ve, "con_g")
    c.wait(ve, "al_n")
    c.inc(ve.tensor_scalar_mul(scrA[0:H, 0:T], iota16[0:H, :], alpha_c[0:H, 3:4]), "mb_f")
    c.wait(sc, "mb_f")
    c.inc(sc.activation(Mb[0:H, :], scrA[0:H, 0:T], AF.Copy), "mb_hi")
    c.wait(ve, "mb_hi")
    c.inc(ve.tensor_sub(Mb[32:32 + H, :], scrA[0:H, 0:T], Mb[0:H, :]), "mb_lo")
    c.dinc(sy.dma_start(out=arow[0:1, 0:H],
                        in_=alpha_in.rearrange("h one -> one h")), "d_al2")
    c.wait(sc, "d_al2")
    c.inc(sc.activation(arow[0:1, 16:16 + H], arow[0:1, 0:H], AF.Exp), "tr_1")
    c.wait(sc, "tr_1")
    c.inc(sc.activation(arow[0:1, 32:32 + H], arow[0:1, 16:16 + H], AF.Ln, bias=1.0), "tr_2")
    c.wait(ve, "tr_2")
    c.inc(ve.tensor_scalar_mul(arow[0:1, 48:48 + H], arow[0:1, 32:32 + H], -2.0), "tr_3")
    c.wait(te, "tr_3")
    c.wait(te, "con_v")
    c.inc(te.matmul(ps[0:128, 0:H], ones_row[0:1, 0:128], arow[0:1, 48:48 + H],
                    start=True, stop=True), "tr_mm")
    c.wait(sc, "tr_mm")
    c.inc(sc.activation(neg2a[:, 0:H], ps[0:128, 0:H], AF.Copy), "tr_cp")


    c.inc(ve.memset(arow[0:1, 64:96], 0.0), "ab_z")
    c.wait(sc, "tr_2")
    c.wait(sc, "ab_z")
    c.inc(sc.activation(jrows[0:1, 0:H], arow[0:1, 32:32 + H], AF.Copy), "ab_h")
    c.wait(ve, "ab_h")
    c.inc(ve.tensor_copy(arow[0:1, 64:64 + H], jrows[0:1, 0:H]), "ab_h2")
    c.wait(ve, "ab_h2")
    c.inc(ve.tensor_sub(arow[0:1, 16:16 + H], arow[0:1, 32:32 + H], arow[0:1, 64:64 + H]), "ab_l")
    c.wait(sc, "ab_l")
    c.inc(sc.activation(jrows[0:1, 0:H], arow[0:1, 16:16 + H], AF.Copy), "ab_lb")
    c.wait(ve, "ab_lb")
    c.inc(ve.tensor_copy(arow[0:1, 80:80 + H], jrows[0:1, 0:H]), "ab_l2")
    c.wait(gp, "ab_l2")
    for r, src0 in ((0, 64), (1, 64), (2, 80), (3, 80)):
        c.dinc(gp.dma_start(out=a4[r:r + 1, 0:16], in_=arow[0:1, src0:src0 + 16]), "ab_d")
    c.dinc(gp.dma_start(out=a4[64:68, 0:16], in_=a4[0:4, 0:16]), "ab_d")

    c.wait(ve, "ab_l2")
    c.inc(ve.memset(jrows[0:1, :], 1.0), "jr_1")
    c.wait(sc, "con_g")
    c.wait(sc, "jr_1")
    c.inc(sc.activation(jrows2[0:1, :], iota16[0:1, :], AF.Copy), "jr_h")
    c.wait(ve, "jr_h")
    c.inc(ve.tensor_sub(jrows3[0:1, :], iota16[0:1, :], jrows2[0:1, :]), "jr_l")
    c.wait(gp, "jr_1")
    c.wait(gp, "jr_l")
    for r in (4, 5, 12, 13):
        c.dinc(gp.dma_start(out=lhsT6[r:r + 1, :], in_=jrows[0:1, :]), "d_l6")
    for r in (0, 2, 8, 10):
        c.dinc(gp.dma_start(out=lhsT6[r:r + 1, :], in_=jrows2[0:1, :]), "d_l6")
    for r in (1, 3, 9, 11):
        c.dinc(gp.dma_start(out=lhsT6[r:r + 1, :], in_=jrows3[0:1, :]), "d_l6")
    c.dinc(gp.dma_start(out=lhsT6[64:70, :], in_=lhsT6[8:14, :]), "d_l6")

    for l in "ABCDEFGH":
        c.wait(te, f"dx{l}", 16)
    c.wait(te, "tr_cp")
    for dc in range(DC):
        for j in range(QCOL):
            c.inc(te.matmul(ps[0:1, j * 512:(j + 1) * 512], ones_row[:, 0:1],
                            x_res[:, dc * T + j * 512: dc * T + (j + 1) * 512],
                            start=(dc == 0), stop=(dc == DC - 1)),
                  "ln_sx" if dc == DC - 1 else "ln_mm0")
    for dc in range(DC):
        buf = scrA if dc % 2 == 0 else scrA2
        for l in "ABCDEFGH":
            c.wait(sc, f"dx{l}", 16)
        c.wait(sc, "mb_lo")
        if dc >= 2:
            c.wait(sc, "ln_x2mm")
        c.inc(sc.activation(buf[:, 0:T], x_res[:, dc * T:(dc + 1) * T], AF.Square), "ln_sq")
        c.wait(te, "ln_sq")
        for j in range(QCOL):
            c.inc(te.matmul(ps[0:1, 2048 + j * 512: 2048 + (j + 1) * 512], ones_row[:, 0:1],
                            buf[:, j * 512:(j + 1) * 512],
                            start=(dc == 0), stop=(dc == DC - 1)), "ln_x2mm")
    c.wait(ve, "ln_sx")
    c.wait(ve, "ln_x2mm")
    c.inc(ve.tensor_scalar_mul(rowbuf[0:1, 0:T], ps[0:1, 0:T], 1.0 / D), "r_m")
    c.inc(ve.tensor_scalar_mul(scrA2[0:1, 0:T], ps[0:1, 2048:2048 + T], 1.0 / D), "r_e2")
    c.wait(sc, "r_m")
    c.wait(sc, "ln_sq")
    c.inc(sc.activation(scrA[0:1, 0:T], rowbuf[0:1, 0:T], AF.Square), "r_m2")
    c.wait(ve, "r_m2")
    c.wait(ve, "r_e2")
    c.inc(ve.tensor_sub(scrA2[0:1, 0:T], scrA2[0:1, 0:T], scrA[0:1, 0:T]), "r_var")
    c.wait(ve, "r_var")
    c.wait(ve, "jr_h")
    c.wait(ve, "ab_l2")
    c.inc(ve.tensor_scalar_add(iota16[0:1, 0:T], scrA2[0:1, 0:T], 1e-5), "r_tz")
    c.wait(sc, "r_tz")
    c.inc(sc.activation(scrA[0:1, 0:T], iota16[0:1, 0:T], AF.Ln), "r_sq")
    c.wait(sc, "r_sq")
    c.inc(sc.activation(iota16[0:1, 0:T], scrA[0:1, 0:T], AF.Exp, scale=-0.5), "r_rs")

    c.wait(te, "r_m2")
    c.wait(te, "r_e2")
    for j in range(QCOL):
        c.inc(te.matmul(ps[0:128, j * 512:(j + 1) * 512], ones_row[0:1, 0:128],
                        rowbuf[0:1, j * 512:(j + 1) * 512], start=True, stop=True), "bc_m")
    c.wait(te, "r_rs")
    for j in range(QCOL):
        c.inc(te.matmul(ps[0:128, 2048 + j * 512:2048 + (j + 1) * 512], ones_row[0:1, 0:128],
                        iota16[0:1, j * 512:(j + 1) * 512], start=True, stop=True), "bc_r")

    c.wait(sc, "bc_r", c.val("bc_r"))
    c.inc(sc.activation(iota16[0:1, 0:T], scrA2[0:1, 0:T], AF.Ln), "r_u")
    c.wait(ve, "r_u")
    c.inc(ve.tensor_sub(scrA2[0:1, 0:T], iota16[0:1, 0:T], scrA[0:1, 0:T]), "r_u1")
    c.wait(sc, "r_u1")
    c.inc(sc.activation(scrA[0:1, 0:T], scrA2[0:1, 0:T], AF.Exp, scale=0.5), "r_w")
    c.wait(ve, "r_w")
    c.inc(ve.tensor_scalar_add(scrA2[0:1, 0:T], scrA[0:1, 0:T], 1e-8), "r_v8")
    c.wait(sc, "r_v8")
    c.inc(sc.activation(scrA[0:1, 0:T], scrA2[0:1, 0:T], AF.Ln), "r_t")
    c.wait(sc, "r_t")
    c.wait(sc, "bc_m", c.val("bc_m"))
    c.inc(sc.activation(rowbuf[0:1, 0:T], scrA[0:1, 0:T], AF.Exp, scale=-1.0), "r_rf")

    c.wait(ve, "bc_m")
    c.wait(ve, "bc_r")
    c.wait(ve, "r_rf", c.val("r_rf"))
    for dc in range(DC):
        if dc >= 1:
            c.wait(ve, "xn_lo")
        c.inc(ve.tensor_sub(scrA[:, 0:T], x_res[:, dc * T:(dc + 1) * T], ps[0:128, 0:T]), "xn_t1")
        c.wait(ve, "xn_t1")
        c.inc(ve.tensor_mul(scrA2[:, 0:T], scrA[:, 0:T], ps[0:128, 2048:2048 + T]), "xn_f")
        c.wait(sc, "xn_f")
        c.inc(sc.activation(xn_hi[:, dc * T:(dc + 1) * T], scrA2[:, 0:T], AF.Copy), "xn_hic")
        c.wait(ve, "xn_hic")
        c.inc(ve.tensor_sub(xn_lo[:, dc * T:(dc + 1) * T], scrA2[:, 0:T],
                            xn_hi[:, dc * T:(dc + 1) * T]), "xn_lo")

    if stop_after == "ADBG":
        c.wait(gp, "r_rf", c.val("r_rf"))
        c.wait(gp, "xn_lo", c.val("xn_lo"))
        c.wait(gp, "sc_mm", c.val("sc_mm"))
        c.dinc(gp.dma_start(out=out_hbm[0:1, 0:T], in_=rowbuf[0:1, 0:T]), "fin_o")
        c.dinc(gp.dma_start(out=out_hbm[1:2, 0:T], in_=iota16[0:1, 0:T]), "fin_o")
        c.dinc(gp.dma_start(out=out_hbm[2:3, 0:2048], in_=xn_hi[5:6, 0:2048]), "fin_o")
        c.dinc(gp.dma_start(out=out_hbm[3:4, 0:2048], in_=xn_lo[5:6, 0:2048]), "fin_o")
        c.wait(gp, "fin_o")
        return nc
    c.wait(sy, "xn_lo")
    if stop_after == "A":
        _early_finish()
        return nc
    phA.close()

    phB = ExitStack()
    v_hat = salloc(phB, "v_hat", [128, KC * H * 65], F32)
    pt0 = salloc(phB, "pt0", [128, 2048], F32)
    phB0 = ExitStack()
    wv_hi = salloc(phB0, "wv_hi", [128, DC * D], BF16)
    wv_lo = salloc(phB0, "wv_lo", [128, DC * D], BF16)
    bv_bf = salloc(phB0, "bv_bf", [128, 2 * D], BF16)

    vrr = v_hat.rearrange("p (kc h e) -> p kc h e", kc=KC, h=H)

    c.wait(sy, "xn_lo", c.val("xn_lo"))
    c.dinc(sy.dma_start(out=pt0[0:1, 0:D], in_=bv_in), "d_bv")
    c.wait(sc, "d_bv")
    split_row(pt0[0:1, 0:D], bv_bf[0:1, 0:D], bv_bf[0:1, D:2 * D], "b_v")

    c.wait(sy, "xn_lo", c.val("xn_lo"))
    for dc in range(DC):
        l = "ABCDEFGH"[dc]
        c.inc(sy.dma_start(out=v_hat[:, dc * D:(dc + 1) * D],
                           in_=wv_in[:, dc * D:(dc + 1) * D]), f"wv{l}", 16)
    c.wait(sc, "xn_lo", c.val("xn_lo"))
    c.wait(sc, "bc_r", c.val("bc_r"))
    c.wait(sc, "d_l6", c.val("d_l6"))
    c.wait(ve, "d_l6", c.val("d_l6"))
    for dc in range(DC):
        l = "ABCDEFGH"[dc]
        c.wait(sc, f"wv{l}", 16)
        c.inc(sc.activation(wv_hi[:, dc * D:(dc + 1) * D],
                            v_hat[:, dc * D:(dc + 1) * D], AF.Copy), "wv_h")
        c.wait(ve, f"wv{l}", 16)
        c.wait(ve, "wv_h")
        c.inc(ve.tensor_sub(wv_lo[:, dc * D:(dc + 1) * D], v_hat[:, dc * D:(dc + 1) * D],
                            wv_hi[:, dc * D:(dc + 1) * D]), "wv_sp")

    if stop_after == "B0a":
        _early_finish()
        return nc
    c.wait(ve, "wv_h", c.val("wv_h"))
    c.wait(ve, "wv_sp", c.val("wv_sp"))
    c.inc(ve.memset(vrr[:, :, :, 64:65], 1.0), "vh_1")
    if stop_after == "B0b":
        _early_finish()
        return nc

    NVH = D // 512
    c.wait(te, "wv_sp")
    c.wait(te, "b_v")
    vh_bank = 0
    for kc in range(KC):
        for vh in range(NVH):
            b = vh_bank % 2
            vh_bank += 1
            c.wait(te, "vh_ev", max(0, c.val("vh_ev") - 1))
            c.inc(te.matmul(ps[0:128, b * 512:(b + 1) * 512], ones_bf[0:1, 0:128],
                            bv_bf[0:1, vh * 512:(vh + 1) * 512], start=True, stop=False), "vh_mm0")
            c.inc(te.matmul(ps[0:128, b * 512:(b + 1) * 512], ones_bf[0:1, 0:128],
                            bv_bf[0:1, D + vh * 512: D + (vh + 1) * 512], start=False, stop=False), "vh_mm0")
            for dc in range(DC):
                xh = xn_hi[:, dc * T + kc * 128: dc * T + kc * 128 + 128]
                xl = xn_lo[:, dc * T + kc * 128: dc * T + kc * 128 + 128]
                wh = wv_hi[:, dc * D + vh * 512: dc * D + (vh + 1) * 512]
                wl = wv_lo[:, dc * D + vh * 512: dc * D + (vh + 1) * 512]
                last = dc == DC - 1
                c.inc(te.matmul(ps[0:128, b * 512:(b + 1) * 512], xh, wh, start=False, stop=False), "vh_mm0")
                c.inc(te.matmul(ps[0:128, b * 512:(b + 1) * 512], xh, wl, start=False, stop=False), "vh_mm0")
                c.inc(te.matmul(ps[0:128, b * 512:(b + 1) * 512], xl, wh, start=False,
                                stop=last), "vh_mm" if last else "vh_mm0")
            if stop_after == "B0c":
                continue
            c.wait(ve, "vh_mm")
            c.wait(ve, "vh_1")
            c.wait(ve, "b_vv", c.val("b_vv"))
            nheads = 512 // HD
            dst = vrr[:, kc, vh * nheads:(vh + 1) * nheads, 0:64]
            src = ps[0:128, b * 512:(b + 1) * 512].rearrange("p (h e) -> p h e", e=64)
            c.inc(ve.tensor_copy(dst, src), "vh_ev")

    c.wait(te, "d_l6")
    phB0.close()


    pt1 = salloc(phB, "pt1", [128, 2048], F32)
    q_hi = salloc(phB, "q_hi", [128, T], BF16)
    q_lo = salloc(phB, "q_lo", [128, T], BF16)
    k_hi = salloc(phB, "k_hi", [128, T], BF16)
    k_lo = salloc(phB, "k_lo", [128, T], BF16)
    rhs6m = salloc(phB, "rhs6m", [128, T], BF16)
    rhs6p = salloc(phB, "rhs6p", [128, T], BF16)
    c.wait(ve, "vh_mm", c.val("vh_mm"))
    c.wait(ve, "vh_ev", c.val("vh_ev"))
    c.inc(ve.memset(rhs6m, 0.0), "r6z")
    c.inc(ve.memset(rhs6p, 0.0), "r6z")
    wqk_hi = salloc(phB, "wqk_hi", [128, DC * 128], BF16)
    wqk_lo = salloc(phB, "wqk_lo", [128, DC * 128], BF16)
    wstage = salloc(phB, "wstage", [128, DC * 128 + 256], F32)
    if stop_after in ("B0", "B0c"):
        _early_finish()
        return nc

    phW = ExitStack()
    WP = 512
    wstg = salloc(phW, "wstg", [128, 2 * WP], F32)
    wabs = salloc(phW, "wabs", [128, WP], F32)
    wsgn = salloc(phW, "wsgn", [128, WP], BF16)
    wmask = salloc(phW, "wmask", [128, WP], BF16)
    wtout = salloc(phW, "wtout", [128, WP], BF16)
    wsum = salloc(phW, "wsum", [128, 8], F32)
    NPC_UP = (DC * HC * 128) // WP
    NPC_DN = (HC * FC * 128) // WP
    w_state = {"stats_done": False, "thr_done": False}

    def w_units():
        c.wait(sy, "vh_mm", c.val("vh_mm"))
        c.wait(sy, "wv_sp", c.val("wv_sp"))
        c.wait(ve, "vh_mm", c.val("vh_mm"))
        c.inc(ve.memset(wsum[:, 0:2], 0.0), "wsz")
        c.wait(sc, "wsz")
        yield
        for (win, sumcol, npieces) in ((wup_in, 0, NPC_UP), (wdn_in, 1, NPC_DN)):
            for i in range(npieces):
                half = (i % 2) * WP
                dtag = "d_ws" if i % 2 == 0 else "d_wsB"
                c.wait(sy, "ws_abs", max(0, c.val("ws_abs") - 1))
                c.dinc(sy.dma_start(out=wstg[:, half:half + WP],
                                    in_=win[:, i * WP:(i + 1) * WP]), dtag)
                yield
                c.wait(sc, dtag)
                c.wait(sc, "ws_add", c.val("ws_add"))
                c.inc(sc.activation(wabs, wstg[:, half:half + WP], AF.Abs,
                                    accum_out=wsum[:, 4 + (i % 2):5 + (i % 2)]), "ws_abs")
                c.wait(ve, "ws_abs")
                c.inc(ve.tensor_add(wsum[:, sumcol:sumcol + 1], wsum[:, sumcol:sumcol + 1],
                                    wsum[:, 4 + (i % 2):5 + (i % 2)]), "ws_add")
                yield
        w_state["stats_done"] = True
        yield
        c.wait(sy, "ws_abs", c.val("ws_abs"))
        for (win, wtscr, thcol, npieces) in ((wup_in, wup_t_scr, 0, NPC_UP),
                                             (wdn_in, wdn_t_scr, 1, NPC_DN)):
            for i in range(npieces):
                half = (i % 2) * WP
                dtag = "d_wt" if i % 2 == 0 else "d_wtB"
                c.wait(sy, "wt_abs", max(0, c.val("wt_abs") - 2))
                c.dinc(sy.dma_start(out=wstg[:, half:half + WP],
                                    in_=win[:, i * WP:(i + 1) * WP]), dtag)
                yield
                c.wait(sc, dtag)
                c.wait(sc, "wt_g", c.val("wt_g"))
                c.wait(sc, "wt_m", c.val("wt_m"))
                c.inc(sc.activation(wabs, wstg[:, half:half + WP], AF.Abs), "wt_abs")
                c.inc(sc.activation(wsgn, wstg[:, half:half + WP], AF.Sign), "wt_abs")
                yield
                c.wait(ve, "wt_abs")
                c.wait(ve, "w_tcp", 1)
                c.inc(ve.tensor_single_scalar(wmask, wabs, thr[:, thcol:thcol + 1],
                                              ALU.is_gt), "wt_g")
                c.wait(ve, "wt_g")
                otag = "d_wto" if i % 2 == 0 else "d_wtoB"
                c.wait(ve, "d_wto", c.val("d_wto"))
                c.wait(ve, "d_wtoB", c.val("d_wtoB"))
                c.inc(ve.tensor_mul(wtout[:, 0:WP], wsgn, wmask), "wt_m")
                c.wait(gp, "wt_m")
                c.dinc(gp.dma_start(out=wtscr[:, i * WP:(i + 1) * WP],
                                    in_=wtout[:, 0:WP]), otag)
                yield

    wgen = w_units()
    _W_DONE = object()

    def drain_w(n):
        for _ in range(n):
            if w_state["stats_done"] and not w_state["thr_done"]:
                return
            if next(wgen, _W_DONE) is _W_DONE:
                return

    def qk_psum(bias_t, bias_tag, tag):
        c.wait(te, bias_tag)
        for j in range(QCOL):
            c.inc(te.matmul(ps[0:128, j * 512:(j + 1) * 512], bias_t[0:1, 0:128],
                            ones_bf[0:1, 0:512], start=True, stop=False), tag + "0")
            c.inc(te.matmul(ps[0:128, j * 512:(j + 1) * 512], bias_t[0:1, 128:256],
                            ones_bf[0:1, 0:512], start=False, stop=False), tag + "0")
            for dc in range(DC):
                xh = xn_hi[:, dc * T + j * 512: dc * T + (j + 1) * 512]
                xl = xn_lo[:, dc * T + j * 512: dc * T + (j + 1) * 512]
                wh = wqk_hi[:, dc * 128:(dc + 1) * 128]
                wl = wqk_lo[:, dc * 128:(dc + 1) * 128]
                last = dc == DC - 1
                c.inc(te.matmul(ps[0:128, j * 512:(j + 1) * 512], wh, xh,
                                start=False, stop=False), tag + "0")
                c.inc(te.matmul(ps[0:128, j * 512:(j + 1) * 512], wl, xh,
                                start=False, stop=False), tag + "0")
                c.inc(te.matmul(ps[0:128, j * 512:(j + 1) * 512], wh, xl,
                                start=False, stop=last), tag if last else tag + "0")

    c.dinc(sy.dma_start(out=wstage[:, 0:DC * 128], in_=wq_in[0]), "d_wqs")
    c.dinc(sy.dma_start(out=wstage[0:1, DC * 128:DC * 128 + 128], in_=bq_in[0]), "d_wqs")
    c.dinc(sy.dma_start(out=wstage[0:1, DC * 128 + 128:DC * 128 + 256], in_=bk_in[0]), "d_wqs")
    c.wait(sc, "d_wqs")
    c.wait(sc, "vh_mm", c.val("vh_mm"))
    c.inc(sc.activation(wqk_hi, wstage[:, 0:DC * 128], AF.Copy), "wq_h")
    c.wait(ve, "wq_h")
    c.wait(ve, "vh_mm", c.val("vh_mm"))
    c.inc(ve.tensor_sub(wqk_lo, wstage[:, 0:DC * 128], wqk_hi), "wq_l")
    split_row(wstage[0:1, DC * 128:DC * 128 + 128], bq_bf[0:1, 0:128], bq_bf[0:1, 128:256], "bqs")
    split_row(wstage[0:1, DC * 128 + 128:DC * 128 + 256], bk_bf[0:1, 0:128], bk_bf[0:1, 128:256], "bks")

    for p in range(NP):
        hA, hB = 2 * p, 2 * p + 1
        if p == NP // 2:
            while not w_state["stats_done"]:
                next(wgen)
            c.wait(te, "exp", c.val("exp"))
            c.wait(te, "at_b", c.val("at_b"))
            c.wait(te, "nc_b", c.val("nc_b"))
            c.wait(te, "ws_add")
            c.inc(te.matmul(ps[0:1, 0:2], ones_row[:, 0:1], wsum[:, 0:2],
                            start=True, stop=True), "w_smm")
            c.wait(ve, "w_smm")
            c.inc(ve.tensor_scalar_mul(wsum[0:1, 2:4], ps[0:1, 0:2], 0.5 / (D * DFF)), "w_th")
            c.wait(te, "w_th")
            c.inc(te.matmul(ps[0:128, 512:514], ones_row[0:1, 0:128], wsum[0:1, 2:4],
                            start=True, stop=True), "w_tbc")
            c.wait(sc, "w_tbc")
            c.inc(sc.activation(thr[:, 0:2], ps[0:128, 512:514], AF.Copy), "w_tcp")
            c.wait(te, "w_tcp")
            w_state["thr_done"] = True

        c.wait(gp, "sc_mm", c.val("sc_mm"))
        c.wait(gp, "vh_ev", c.val("vh_ev"))
        c.wait(gp, "mb_lo", c.val("mb_lo"))
        c.wait(gp, "ab_d", c.val("ab_d"))
        c.wait(gp, "r6n", c.val("r6n"))
        c.wait(gp, "r6z", c.val("r6z"))
        for (dstrow, srcrow) in ((4, 0 + hA), (5, 32 + hA), (12, 0 + hB), (13, 32 + hB)):
            c.dinc(gp.dma_start(out=rhs6m[dstrow:dstrow + 1, :],
                                in_=Mb[srcrow:srcrow + 1, :]), "d_r6")
        c.dinc(gp.dma_start(out=rhs6m[68:70, :], in_=rhs6m[12:14, :]), "d_r6")
        c.wait(ve, "d_r6", c.val("d_r6"))
        c.wait(ve, "sc_mm", c.val("sc_mm"))
        c.inc(ve.tensor_scalar(rhs6m[0:4, :], rhs6m[0:4, :], 0.0, a4[0:4, hA:hA + 1],
                               ALU.mult, ALU.add), "r6a")
        c.inc(ve.tensor_scalar(rhs6m[64:68, :], rhs6m[64:68, :], 0.0, a4[64:68, hB:hB + 1],
                               ALU.mult, ALU.add), "r6a")
        if stop_after == f"P{p}w3":
            break
        c.wait(ve, "r6a")
        c.inc(ve.tensor_scalar_mul(rhs6p[0:6, :], rhs6m[0:6, :], -1.0), "r6n")
        c.inc(ve.tensor_scalar_mul(rhs6p[64:70, :], rhs6m[64:70, :], -1.0), "r6n")

        if stop_after == f"P{p}w":
            break
        c.wait(sc, "pt_fix", c.val("pt_fix"))
        c.wait(sc, "tr_cp")
        c.wait(sc, "con_r")
        c.inc(sc.activation(e2diag[:, 0:128], R_relu, AF.Exp, scale=neg2a[:, hA:hA + 1]), "e2d")
        c.inc(sc.activation(e2diag[:, 128:256], R_relu, AF.Exp, scale=neg2a[:, hB:hB + 1]), "e2d")
        drain_w(4)

        c.wait(te, "vh_ev")
        c.wait(te, "exp", c.val("exp"))
        c.wait(te, "wq_l")
        qk_psum(bq_bf, "bqs", "q_mm")
        if stop_after == f"P{p}qm":
            break
        c.wait(sc, "q_mm")
        c.wait(sc, "sc_mm", c.val("sc_mm"))
        c.inc(sc.activation(q_hi, ps[0:128, 0:T], AF.Copy, scale=0.125), "q_h")
        c.wait(ve, "q_h")
        c.inc(ve.scalar_tensor_tensor(q_lo, ps[0:128, 0:T], 0.125, q_hi,
                                      ALU.mult, ALU.subtract), "q_l")
        c.wait(ve, "pv_g", c.val("pv_g"))
        c.wait(ve, "d_at", c.val("d_at"))
        if stop_after == f"P{p}qs":
            break
        c.inc(ve.tensor_mul(pt0[:, 0:T], q_hi, q_hi), "q_sq")
        if stop_after == f"P{p}n1":
            break

        c.wait(sy, "pv_g", c.val("pv_g"))
        c.wait(sy, "at_b", c.val("at_b"))
        c.wait(sy, "nc_b", c.val("nc_b"))
        c.wait(sy, "d_at", c.val("d_at"))
        if p == 0:
            c.wait(sy, "vh_mm", c.val("vh_mm"))
            c.wait(sy, "vh_ev", c.val("vh_ev"))
        c.dinc(sy.dma_start(out=pt1[:, 0:DC * 128], in_=wk_in[p]), "d_wk")
        c.wait(sc, "d_wk")
        c.wait(sc, "q_mm", c.val("q_mm"))
        c.inc(sc.activation(wqk_hi, pt1[:, 0:DC * 128], AF.Copy), "wk_h")
        c.wait(ve, "wk_h")
        c.inc(ve.tensor_sub(wqk_lo, pt1[:, 0:DC * 128], wqk_hi), "wk_l")
        c.wait(te, "q_l")
        c.wait(te, "wk_l")
        qk_psum(bk_bf, "bks", "k_mm")
        c.wait(sc, "k_mm")
        c.inc(sc.activation(k_hi, ps[0:128, 0:T], AF.Copy), "k_h")
        c.wait(ve, "k_h")
        c.inc(ve.tensor_sub(k_lo, ps[0:128, 0:T], k_hi), "k_l")

        if p + 1 < NP:
            c.wait(sy, "wq_l", c.val("wq_l"))
            c.wait(sy, "bqsv", c.val("bqsv"))
            c.wait(sy, "bksv", c.val("bksv"))
            c.dinc(sy.dma_start(out=wstage[:, 0:DC * 128], in_=wq_in[p + 1]), "d_wqs")
            c.dinc(sy.dma_start(out=wstage[0:1, DC * 128:DC * 128 + 128],
                                in_=bq_in[p + 1]), "d_wqs")
            c.dinc(sy.dma_start(out=wstage[0:1, DC * 128 + 128:DC * 128 + 256],
                                in_=bk_in[p + 1]), "d_wqs")
            c.wait(sc, "d_wqs")
            c.wait(sc, "k_mm")
            c.inc(sc.activation(wqk_hi, wstage[:, 0:DC * 128], AF.Copy), "wq_h")
            c.wait(ve, "wq_h")
            c.inc(ve.tensor_sub(wqk_lo, wstage[:, 0:DC * 128], wqk_hi), "wq_l")
            split_row(wstage[0:1, DC * 128:DC * 128 + 128],
                      bq_bf[0:1, 0:128], bq_bf[0:1, 128:256], "bqs")
            split_row(wstage[0:1, DC * 128 + 128:DC * 128 + 256],
                      bk_bf[0:1, 0:128], bk_bf[0:1, 128:256], "bks")
        if stop_after == f"P{p}w1":
            break

        for j in range(QCOL):
            c.wait(te, "q_sq")
            c.wait(te, "at_b", c.val("at_b"))
            c.wait(te, "qn_rx", max(0, c.val("qn_rx") - 1))
            c.inc(te.matmul(ps[0:1, 2048 + (j % 2) * 512: 2048 + (j % 2) * 512 + 512],
                            ones_row[:, 0:1], pt0[:, j * 512:(j + 1) * 512],
                            start=True, stop=True), "qn_mm")
            if stop_after == f"P{p}n2":
                break
            c.wait(ve, "qn_mm")
            c.inc(ve.reduce_max(smalls[0:1, j:j + 1],
                                ps[0:1, 2048 + (j % 2) * 512: 2048 + (j % 2) * 512 + 512],
                                axis=AX.X), "qn_rx")
        if stop_after in (f"P{p}n2", f"P{p}n3"):
            break
        c.wait(ve, "qn_rx")
        c.inc(ve.reduce_max(smalls[0:1, 8:9], smalls[0:1, 0:QCOL], axis=AX.X), "qn_m")
        if stop_after == f"P{p}qn":
            break

        c.wait(ve, "qn_mm", c.val("qn_mm"))
        c.inc(ve.tensor_mul(pt0[:, 0:T], k_hi, k_hi), "k_sq")
        c.wait(ve, "qn_m")
        for j in range(QCOL):
            c.wait(te, "k_sq")
            c.wait(te, "qn_rx", c.val("qn_rx"))
            c.wait(te, "kn_rx", max(0, c.val("kn_rx") - 1))
            c.inc(te.matmul(ps[0:1, 2048 + (j % 2) * 512: 2048 + (j % 2) * 512 + 512],
                            ones_row[:, 0:1], pt0[:, j * 512:(j + 1) * 512],
                            start=True, stop=True), "kn_mm")
            c.wait(ve, "kn_mm")
            c.inc(ve.reduce_max(smalls[0:1, j:j + 1],
                                ps[0:1, 2048 + (j % 2) * 512: 2048 + (j % 2) * 512 + 512],
                                axis=AX.X), "kn_rx")
        c.wait(ve, "kn_rx")
        c.inc(ve.reduce_max(smalls[0:1, 9:10], smalls[0:1, 0:QCOL], axis=AX.X), "kn_m")
        drain_w(4)
        if stop_after == f"P{p}k":
            break

        c.wait(sc, "qn_m")
        c.wait(sc, "kn_m")
        c.inc(sc.activation(smalls[0:1, 10:11], smalls[0:1, 8:9], AF.Sqrt), "c_s")
        c.inc(sc.activation(smalls[0:1, 11:12], smalls[0:1, 9:10], AF.Sqrt), "c_s")
        c.wait(ve, "c_s")
        c.inc(ve.tensor_mul(smalls[0:1, 12:13], smalls[0:1, 10:11], smalls[0:1, 11:12]), "c_m")
        c.wait(ve, "c_m")
        c.inc(ve.tensor_scalar_mul(smalls[0:1, 13:14], smalls[0:1, 12:13], -1.02), "c_n")
        c.wait(te, "c_n")
        c.wait(te, "kn_rx", c.val("kn_rx"))
        c.inc(te.matmul(ps[0:128, 2048:2049], ones_row[0:1, 0:128], smalls[0:1, 13:14],
                        start=True, stop=True), "c_bc")
        c.wait(sc, "c_bc")
        c.wait(sc, "exp", c.val("exp"))
        c.inc(sc.activation(negc, ps[0:128, 2048:2049], AF.Copy), "c_cp")

        if stop_after == f"P{p}qk":
            break
        c.wait(te, "k_l")
        c.wait(te, "k_sq", c.val("k_sq"))
        c.wait(te, "c_cp", c.val("c_cp"))
        c.wait(te, "d_r6")
        c.wait(te, "e2d", c.val("e2d"))

        def score_mms(h, qb, kc, b, out_list):
            base = 0 if h == hA else 64
            pb = ps[0:128, b * 512:(b + 1) * 512]
            kh = k_hi[base:base + 64, kc * 128:(kc + 1) * 128]
            kl = k_lo[base:base + 64, kc * 128:(kc + 1) * 128]
            qh = q_hi[base:base + 64, qb * 512:(qb + 1) * 512]
            ql = q_lo[base:base + 64, qb * 512:(qb + 1) * 512]
            out_list.append(lambda: c.inc(te.matmul(pb, kh, qh, start=True, stop=False), "sc_mm0"))
            out_list.append(lambda: c.inc(te.matmul(pb, kh, ql, start=False, stop=False), "sc_mm0"))
            out_list.append(lambda: c.inc(te.matmul(pb, kl, qh, start=False, stop=False), "sc_mm0"))
            lhs = lhsT6[base:base + 6, kc * 128:(kc + 1) * 128]
            r6m = rhs6m[base:base + 6, :]
            r6p = rhs6p[base:base + 6, :]
            q0, q1 = qb * 512, (qb + 1) * 512
            kstart, kend = kc * 128, kc * 128 + 128
            segs = []
            if q0 < min(kstart, q1):
                segs.append((q0, min(kstart, q1), r6p))
            dlo, dhi = max(q0, kstart), min(q1, kend)
            if dlo < dhi:
                segs.append((dlo, dhi, r6m))
            if max(q0, kend) < q1:
                segs.append((max(q0, kend), q1, r6m))
            for idx, (a0, a1, rsrc) in enumerate(segs):
                last = idx == len(segs) - 1
                out_list.append(lambda a0=a0, a1=a1, rsrc=rsrc, last=last:
                                c.inc(te.matmul(ps[0:128, b * 512 + (a0 - q0): b * 512 + (a1 - q0)],
                                                lhs, rsrc[:, a0:a1], start=False, stop=last),
                                      "sc_mm" if last else "sc_mm0"))

        def emit_pv(ptb, kc0, kc1):
            c.wait(te, "exp", c.val("exp"))
            c.wait(te, "pt_fix", c.val("pt_fix"))
            if kc0 == 0:
                c.wait(te, "at_b", c.val("at_b"))
            npv = 0
            for (h, kc, pcol) in ((hA, kc0, 0), (hA, kc1, 512), (hB, kc0, 1024), (hB, kc1, 1536)):
                b = 4 if h == hA else 5
                npv += 1
                c.inc(te.matmul(ps[0:65, b * 512:(b + 1) * 512],
                                vrr[:, kc, h, 0:65], ptb[:, pcol:pcol + 512],
                                start=(kc == 0), stop=(kc == KC - 1)),
                      "pv_g" if npv == 4 else "pv_mm0")

        for qb in range(QB):
            pending = None
            for kcg in range(KCG):
                kc0, kc1 = 2 * kcg, 2 * kcg + 1
                ptb = pt0 if kcg % 2 == 0 else pt1
                c.wait(te, "exp", c.val("exp"))
                mmsA, mmsB = [], []
                score_mms(hA, qb, kc0, 0, mmsA)
                score_mms(hA, qb, kc1, 1, mmsA)
                score_mms(hB, qb, kc0, 2, mmsB)
                score_mms(hB, qb, kc1, 3, mmsB)
                for i in range(max(len(mmsA), len(mmsB))):
                    if i < len(mmsA):
                        mmsA[i]()
                    if i < len(mmsB):
                        mmsB[i]()
                if pending is not None:
                    emit_pv(*pending)
                c.wait(sc, "sc_mm")
                c.wait(sc, "c_cp")
                c.wait(sc, "pv_g", max(0, c.val("pv_g") - 1))
                c.wait(sc, "at_b", c.val("at_b"))
                c.wait(sc, "nc_b", c.val("nc_b"))
                c.wait(sc, "d_at", c.val("d_at"))
                c.inc(sc.activation(ptb, ps[0:128, 0:2048], AF.Exp, bias=negc[:, 0:1]), "exp")
                for (h, col, pcol) in ((hA, 0, 0), (hA, 1, 512), (hB, 0, 1024), (hB, 1, 1536)):
                    kc = 2 * kcg + col
                    if kc * 128 >= qb * 512 and (kc + 1) * 128 <= (qb + 1) * 512:
                        off = pcol + (kc * 128 - qb * 512)
                        c.wait(ve, "exp")
                        c.inc(ve.tensor_mul(ptb[:, off:off + 128], ptb[:, off:off + 128],
                                            e2diag[:, (0 if h == hA else 128):(128 if h == hA else 256)]),
                              "pt_fix")
                pending = (ptb, kc0, kc1)
                drain_w(2 if p < NP // 2 else 3)
            emit_pv(*pending)

            c.wait(sc, "pv_g", c.val("pv_g"))
            c.wait(sc, "nb_a", c.val("nb_a"))
            c.wait(sc, "nb_b", c.val("nb_b"))
            c.inc(sc.activation(pt1[64:65, 0:512],
                                ps[64:65, 4 * 512:4 * 512 + 512], AF.Copy), "nr_a")
            c.inc(sc.activation(pt1[64:65, 512:1024],
                                ps[64:65, 5 * 512:5 * 512 + 512], AF.Copy), "nr_b")
            c.wait(te, "nr_a")
            c.wait(te, "nr_b")
            c.wait(te, "nc_b", c.val("nc_b"))
            c.inc(te.matmul(ps[0:64, 6 * 512:7 * 512], ones_row[64:65, 0:64],
                            pt1[64:65, 0:512], start=True, stop=True), "nb_a")
            c.inc(te.matmul(ps[0:64, 7 * 512:8 * 512], ones_row[64:65, 0:64],
                            pt1[64:65, 512:1024], start=True, stop=True), "nb_b")
            c.wait(ve, "nb_a")
            c.wait(ve, "nb_b")
            c.wait(ve, "at_b", c.val("at_b"))
            c.inc(ve.reciprocal(pt1[0:64, 0:512], ps[0:64, 6 * 512:7 * 512]), "nc_a")
            c.inc(ve.reciprocal(pt1[0:64, 512:1024], ps[0:64, 7 * 512:8 * 512]), "nc_b")
            c.wait(ve, "nc_a")
            c.wait(ve, "nc_b")
            c.inc(ve.tensor_mul(pt0[0:64, 0:512], ps[0:64, 4 * 512:5 * 512],
                                pt1[0:64, 0:512]), "at_a")
            c.inc(ve.tensor_mul(pt0[0:64, 512:1024], ps[0:64, 5 * 512:6 * 512],
                                pt1[0:64, 512:1024]), "at_b")
            c.wait(gp, "at_b")
            c.dinc(gp.dma_start(out=attn_scr[hA, qb], in_=pt0[0:64, 0:512]), "d_at")
            c.dinc(gp.dma_start(out=attn_scr[hB, qb], in_=pt0[0:64, 512:1024]), "d_at")
            drain_w(2)
            if stop_after == f"P{p}q{qb}":
                break

        if stop_after is not None and stop_after.startswith(f"P{p}"):
            break

    if stop_after == "B" or (stop_after or "").startswith("P"):
        _early_finish()
        return nc
    c.wait(sy, "d_at", c.val("d_at"))
    c.wait(sy, "pv_g", c.val("pv_g"))
    c.wait(sy, "at_b", c.val("at_b"))
    c.wait(sy, "nc_b", c.val("nc_b"))
    c.wait(sy, "exp", c.val("exp"))
    c.wait(te, "exp", c.val("exp"))
    c.wait(te, "at_b", c.val("at_b"))
    c.wait(te, "nc_b", c.val("nc_b"))
    while next(wgen, _W_DONE) is not _W_DONE:
        pass
    c.wait(sy, "d_wto")
    c.wait(sy, "d_wtoB")
    c.wait(sy, "wt_m", c.val("wt_m"))
    phW.close()
    phB.close()

    phC = ExitStack()
    wo_hi = salloc(phC, "wo_hi", [128, DC * FC * 128], BF16)
    wo_lo = salloc(phC, "wo_lo", [128, DC * FC * 128], BF16)
    wo_st = salloc(phC, "wo_st", [128, 2048], F32)
    bo_bf = salloc(phC, "bo_bf", [128, 2 * D], BF16)

    c.dinc(sy.dma_start(out=wo_st[0:1, 0:D], in_=bo_in), "d_bo")
    c.wait(sc, "d_bo")
    split_row(wo_st[0:1, 0:D], bo_bf[0:1, 0:D], bo_bf[0:1, D:2 * D], "b_o")
    attn_f = salloc(phC, "attn_f", [128, DC * 512], F32)
    attn_h2 = salloc(phC, "attn_h2", [128, DC * 512], BF16)
    attn_l2 = salloc(phC, "attn_l2", [128, DC * 512], BF16)
    s_qb = salloc(phC, "s_qb", [128, FC * 512], F32)

    NWO = (DC * FC * 128) // 2048
    for i in range(NWO):
        c.wait(sy, "wo_sp", c.val("wo_sp"))
        c.wait(sy, "b_ov", c.val("b_ov"))
        c.dinc(sy.dma_start(out=wo_st, in_=wo_in[:, i * 2048:(i + 1) * 2048]), "d_wo")
        c.wait(sc, "d_wo")
        c.inc(sc.activation(wo_hi[:, i * 2048:(i + 1) * 2048], wo_st, AF.Copy), "wo_h")
        c.wait(ve, "wo_h")
        c.inc(ve.tensor_sub(wo_lo[:, i * 2048:(i + 1) * 2048], wo_st,
                            wo_hi[:, i * 2048:(i + 1) * 2048]), "wo_sp")

    for qb in range(QB):
        c.wait(sy, "c_sp", c.val("c_sp"))
        for dc in range(DC):
            c.dinc(sy.dma_start(out=attn_f[0:64, dc * 512:(dc + 1) * 512],
                                in_=attn_scr[2 * dc, qb]), "d_af")
            c.dinc(sy.dma_start(out=attn_f[64:128, dc * 512:(dc + 1) * 512],
                                in_=attn_scr[2 * dc + 1, qb]), "d_af")
        c.wait(sc, "d_af")
        c.wait(sc, "s_mm", c.val("s_mm"))
        c.inc(sc.activation(attn_h2, attn_f, AF.Copy), "c_h")
        c.wait(ve, "c_h")
        c.inc(ve.tensor_sub(attn_l2, attn_f, attn_h2), "c_sp")
        c.wait(ve, "s_w", c.val("s_w"))
        c.wait(te, "c_sp")
        c.wait(te, "wo_sp")
        c.wait(te, "b_o")
        for fc in range(FC):
            b = fc % 2
            c.wait(te, "s_ev", max(0, c.val("s_ev") - 1))
            c.inc(te.matmul(ps[0:128, b * 512:(b + 1) * 512], bo_bf[0:1, fc * 128:(fc + 1) * 128],
                            ones_bf[0:1, 0:512], start=True, stop=False), "s_mm0")
            c.inc(te.matmul(ps[0:128, b * 512:(b + 1) * 512], bo_bf[0:1, D + fc * 128: D + (fc + 1) * 128],
                            ones_bf[0:1, 0:512], start=False, stop=False), "s_mm0")
            for dc in range(DC):
                w_hi = wo_hi[:, (dc * FC + fc) * 128:(dc * FC + fc) * 128 + 128]
                w_lo = wo_lo[:, (dc * FC + fc) * 128:(dc * FC + fc) * 128 + 128]
                ah = attn_h2[:, dc * 512:(dc + 1) * 512]
                al = attn_l2[:, dc * 512:(dc + 1) * 512]
                last = dc == DC - 1
                c.inc(te.matmul(ps[0:128, b * 512:(b + 1) * 512], w_hi, ah,
                                start=False, stop=False), "s_mm0")
                c.inc(te.matmul(ps[0:128, b * 512:(b + 1) * 512], w_lo, ah,
                                start=False, stop=False), "s_mm0")
                c.inc(te.matmul(ps[0:128, b * 512:(b + 1) * 512], w_hi, al,
                                start=False, stop=last), "s_mm" if last else "s_mm0")
            c.wait(ve, "s_mm")
            c.inc(ve.tensor_copy(s_qb[:, fc * 512:(fc + 1) * 512],
                                 ps[0:128, b * 512:(b + 1) * 512]), "s_ev")
        c.wait(gp, "s_ev")
        for fc in range(FC):
            c.dinc(gp.dma_start(out=s_scr[fc, qb], in_=s_qb[:, fc * 512:(fc + 1) * 512]), "s_w")

    c.wait(sy, "s_w")
    if stop_after == "C":
        _early_finish()
        return nc
    phC.close()

    phD1 = ExitStack()
    wt_up = salloc(phD1, "wt_up", [128, DC * HC * 128], BF16)
    hpre = salloc(phD1, "hpre", [128, 2 * 512], F32)
    hful = salloc(phD1, "hful", [128, 2 * 512], F32)
    h2b = salloc(phD1, "h2b", [128, 2 * 512], F32)
    hhi = salloc(phD1, "hhi", [128, 2 * 512], BF16)
    hlo = salloc(phD1, "hlo", [128, 2 * 512], BF16)
    rfb = salloc(phD1, "rfb", [128, 512], F32)
    rowd = salloc(phD1, "rowd", [128, 1024], F32)

    for ch, l in enumerate("ABCDEFGH"):
        w8 = (DC * HC * 128) // 8
        c.inc(sy.dma_start(out=wt_up[:, ch * w8:(ch + 1) * w8],
                           in_=wup_t_scr[:, ch * w8:(ch + 1) * w8]), f"wtu{l}", 16)
    c.wait(te, "s_ev", c.val("s_ev"))
    c.wait(sc, "s_w", c.val("s_w"))
    c.wait(ve, "s_w", c.val("s_w"))

    for tb in range(TB):
        c.wait(te, "rfb_c", c.val("rfb_c"))
        c.wait(te, "r_rf", c.val("r_rf"))
        c.inc(te.matmul(ps[0:128, 6 * 512:7 * 512], ones_row[0:1, 0:128],
                        rowbuf[0:1, tb * 512:(tb + 1) * 512], start=True, stop=True), "rfb_m")
        c.wait(sc, "rfb_m")
        c.wait(sc, "up_ev", c.val("up_ev"))
        c.inc(sc.activation(rfb, ps[0:128, 6 * 512:7 * 512], AF.Copy), "rfb_c")
        for l in "ABCDEFGH":
            c.wait(te, f"wtu{l}", 16)
        c.wait(te, "rfb_c")
        for hc in range(HC):
            b = hc % 2
            c.wait(te, "up_ev", max(0, c.val("up_ev") - 1))
            for dc in range(DC):
                wt = wt_up[:, (dc * HC + hc) * 128:(dc * HC + hc) * 128 + 128]
                xh = xn_hi[:, dc * T + tb * 512: dc * T + (tb + 1) * 512]
                xl = xn_lo[:, dc * T + tb * 512: dc * T + (tb + 1) * 512]
                c.inc(te.matmul(ps[0:128, b * 512:(b + 1) * 512], wt, xh,
                                start=(dc == 0), stop=False), "up_mm0")
                c.inc(te.matmul(ps[0:128, b * 512:(b + 1) * 512], wt, xl,
                                start=False, stop=(dc == DC - 1)),
                      "up_mm" if dc == DC - 1 else "up_mm0")
            c.wait(ve, "up_mm")
            c.wait(ve, "g_h", max(0, c.val("g_h") - 1))
            c.inc(ve.tensor_mul(hpre[:, b * 512:(b + 1) * 512],
                                ps[0:128, b * 512:(b + 1) * 512], rfb), "up_ev")
            c.wait(sc, "up_ev")
            c.wait(sc, "h_lo", max(0, c.val("h_lo") - 1))
            c.inc(sc.activation(hful[:, b * 512:(b + 1) * 512],
                                hpre[:, b * 512:(b + 1) * 512], AF.Gelu), "g_h")
            c.wait(sc, "g_h")
            c.wait(sc, "d_h" if b == 0 else "d_hB")
            c.inc(sc.activation(hhi[:, b * 512:(b + 1) * 512],
                                hful[:, b * 512:(b + 1) * 512], AF.Copy), "h_hi")
            c.wait(ve, "h_hi")
            c.inc(ve.tensor_sub(hlo[:, b * 512:(b + 1) * 512],
                                hful[:, b * 512:(b + 1) * 512],
                                hhi[:, b * 512:(b + 1) * 512]), "h_lo")
            c.wait(sc, "h2_rd", max(0, c.val("h2_rd") - 1))
            c.inc(sc.activation(h2b[:, b * 512:(b + 1) * 512],
                                hful[:, b * 512:(b + 1) * 512], AF.Square), "h_sq")
            c.wait(te, "h_sq")
            c.inc(te.matmul(ps[0:1, 7 * 512:7 * 512 + 512], ones_row[:, 0:1],
                            h2b[:, b * 512:(b + 1) * 512],
                            start=(hc == 0), stop=(hc == HC - 1)), "h2_rd")
            c.wait(gp, "h_lo")
            htag = "d_h" if b == 0 else "d_hB"
            c.dinc(gp.dma_start(out=h_scr_hi[tb][:, hc * 512:(hc + 1) * 512],
                                in_=hhi[:, b * 512:(b + 1) * 512]), htag)
            c.dinc(gp.dma_start(out=h_scr_lo[tb][:, hc * 512:(hc + 1) * 512],
                                in_=hlo[:, b * 512:(b + 1) * 512]), htag)
        c.wait(ve, "h2_rd")
        c.wait(ve, "rh_d", c.val("rh_d"))
        c.inc(ve.tensor_scalar_mul(rowd[0:1, 0:512], ps[0:1, 7 * 512:7 * 512 + 512],
                                   1.0 / DFF), "rh_1")
        c.wait(sc, "rh_1")
        c.inc(sc.activation(rowd[0:1, 512:1024], rowd[0:1, 0:512], AF.Sqrt), "rh_2")
        c.wait(ve, "rh_2")
        c.inc(ve.tensor_scalar_add(rowd[0:1, 0:512], rowd[0:1, 512:1024], 1e-8), "rh_3")
        c.wait(ve, "rh_3")
        c.inc(ve.reciprocal(rowd[0:1, 512:1024], rowd[0:1, 0:512]), "rh_4")
        c.wait(gp, "rh_4")
        c.dinc(gp.dma_start(out=rowbuf[32:33, tb * 512:(tb + 1) * 512],
                            in_=rowd[0:1, 512:1024]), "rh_d")

    c.wait(sy, "d_h")
    c.wait(sy, "d_hB")
    c.wait(sy, "rh_d")
    if stop_after == "D1":
        _early_finish()
        return nc
    phD1.close()
    xn_stack.close()

    phD2 = ExitStack()
    wt_dn = salloc(phD2, "wt_dn", [128, HC * FC * 128], BF16)
    h_ahi = salloc(phD2, "h_ahi", [128, HC * 512], BF16)
    h_alo = salloc(phD2, "h_alo", [128, HC * 512], BF16)
    gbuf = salloc(phD2, "gbuf", [128, 2 * 512], F32)
    xsbuf = salloc(phD2, "xsbuf", [128, 4 * 512], F32)
    rhb = salloc(phD2, "rhb", [128, 512], F32)

    for ch, l in enumerate("ABCDEFGH"):
        w8 = (HC * FC * 128) // 8
        c.inc(sy.dma_start(out=wt_dn[:, ch * w8:(ch + 1) * w8],
                           in_=wdn_t_scr[:, ch * w8:(ch + 1) * w8]), f"wtd{l}", 16)
    c.wait(te, "up_ev", c.val("up_ev"))
    c.wait(te, "rfb_c", c.val("rfb_c"))
    c.wait(te, "rh_1", c.val("rh_1"))

    for tb in range(TB):
        c.wait(sy, "dn_mm", c.val("dn_mm"))
        for l in "ABCDEFGH":
            c.wait(sy, f"ha{l}", 16 * tb)
        for ch in range(4):
            w4 = (HC * 512) // 4
            c.inc(sy.dma_start(out=h_ahi[:, ch * w4:(ch + 1) * w4],
                               in_=h_scr_hi[tb][:, ch * w4:(ch + 1) * w4]),
                  "ha" + "ABCDEFGH"[2 * ch], 16)
            c.inc(sy.dma_start(out=h_alo[:, ch * w4:(ch + 1) * w4],
                               in_=h_scr_lo[tb][:, ch * w4:(ch + 1) * w4]),
                  "ha" + "ABCDEFGH"[2 * ch + 1], 16)
        c.wait(te, "rhb_c", c.val("rhb_c"))
        c.wait(te, "rh_d")
        c.inc(te.matmul(ps[0:128, 6 * 512:7 * 512], ones_row[32:33, 0:128],
                        rowbuf[32:33, tb * 512:(tb + 1) * 512], start=True, stop=True), "rhb_m")
        c.wait(sc, "rhb_m")
        c.wait(sc, "dn_ev", c.val("dn_ev"))
        c.inc(sc.activation(rhb, ps[0:128, 6 * 512:7 * 512], AF.Copy), "rhb_c")
        for l in "ABCDEFGH":
            c.wait(te, f"wtd{l}", 16)
        for l in "ABCDEFGH":
            c.wait(te, f"ha{l}", 16 * (tb + 1))
        c.wait(te, "rhb_c")
        for fc in range(FC):
            b = fc % 2
            c.wait(te, "dn_ev", max(0, c.val("dn_ev") - 1))
            for hc in range(HC):
                wt = wt_dn[:, (hc * FC + fc) * 128:(hc * FC + fc) * 128 + 128]
                hh = h_ahi[:, hc * 512:(hc + 1) * 512]
                hl = h_alo[:, hc * 512:(hc + 1) * 512]
                c.inc(te.matmul(ps[0:128, b * 512:(b + 1) * 512], wt, hh,
                                start=(hc == 0), stop=False), "dn_mm0")
                c.inc(te.matmul(ps[0:128, b * 512:(b + 1) * 512], wt, hl,
                                start=False, stop=(hc == HC - 1)),
                      "dn_mm" if hc == HC - 1 else "dn_mm0")
            xoff = 0 if fc % 2 == 0 else 1024
            c.wait(sy, "fin_o", max(0, c.val("fin_o") - 16))
            c.dinc(sy.dma_start(out=xsbuf[:, xoff:xoff + 512],
                                in_=x_in[:, fc * T + tb * 512: fc * T + (tb + 1) * 512]), "d_xs")
            c.dinc(sy.dma_start(out=xsbuf[:, xoff + 512:xoff + 1024],
                                in_=s_scr[fc, tb]), "d_xs")
            c.wait(ve, "dn_mm")
            c.wait(ve, "g_s", max(0, c.val("g_s") - 1))
            c.wait(ve, "fin_o", max(0, c.val("fin_o") - 16))
            c.inc(ve.tensor_mul(gbuf[:, b * 512:(b + 1) * 512],
                                ps[0:128, b * 512:(b + 1) * 512], rhb), "dn_ev")
            c.wait(sc, "dn_ev")
            c.wait(sc, "fin_m", max(0, c.val("fin_m") - 1))
            c.inc(sc.activation(gbuf[:, b * 512:(b + 1) * 512],
                                gbuf[:, b * 512:(b + 1) * 512], AF.Silu), "g_s")
            c.wait(ve, "g_s")
            c.wait(ve, "d_xs")
            c.inc(ve.tensor_mul(gbuf[:, b * 512:(b + 1) * 512],
                                gbuf[:, b * 512:(b + 1) * 512],
                                xsbuf[:, xoff + 512:xoff + 1024]), "fin_m")
            c.wait(ve, "fin_m")
            c.inc(ve.tensor_add(gbuf[:, b * 512:(b + 1) * 512],
                                gbuf[:, b * 512:(b + 1) * 512],
                                xsbuf[:, xoff:xoff + 512]), "fin_a")
            c.wait(gp, "fin_a")
            c.dinc(gp.dma_start(out=out_hbm[:, fc * T + tb * 512: fc * T + (tb + 1) * 512],
                                in_=gbuf[:, b * 512:(b + 1) * 512]), "fin_o")

    c.wait(gp, "fin_o")
    phD2.close()
    return nc


def _prep_core(xb, qkv_w, qkv_b, out_w, out_b, alpha, w_up, w_down,
               T=2048, D=1024, H=16, DFF=4096):
    DC, HC, FC, NP = D // 128, DFF // 128, D // 128, H // 2

    x_prep = np.ascontiguousarray(
        xb.T.reshape(DC, 128, T).transpose(1, 0, 2).reshape(128, DC * T))
    wq = np.empty((NP, 128, DC * 128), np.float32)
    wk = np.empty((NP, 128, DC * 128), np.float32)
    for p in range(NP):
        Wq = qkv_w[128 * p:128 * p + 128, :]
        Wk = qkv_w[D + 128 * p: D + 128 * p + 128, :]
        wq[p] = Wq.reshape(128, DC, 128).transpose(2, 1, 0).reshape(128, DC * 128)
        wk[p] = Wk.reshape(128, DC, 128).transpose(2, 1, 0).reshape(128, DC * 128)
    Wv = qkv_w[2 * D:3 * D, :]
    wv = Wv.reshape(D, DC, 128).transpose(2, 1, 0).reshape(128, DC * D)
    bq = np.ascontiguousarray(qkv_b[0:D].reshape(NP, 1, 128))
    bk = np.ascontiguousarray(qkv_b[D:2 * D].reshape(NP, 1, 128))
    bv = np.ascontiguousarray(qkv_b[2 * D:3 * D].reshape(1, D))
    wo = out_w.reshape(FC, 128, DC, 128).transpose(3, 2, 0, 1).reshape(128, DC * FC * 128)
    bo = np.ascontiguousarray(out_b.reshape(1, D))
    wup = w_up.reshape(HC, 128, DC, 128).transpose(3, 2, 0, 1).reshape(128, DC * HC * 128)
    wdn = w_down.reshape(FC, 128, HC, 128).transpose(3, 2, 0, 1).reshape(128, HC * FC * 128)
    al = np.ascontiguousarray(alpha.reshape(H, 1))
    return {
        "x_in": x_prep, "wq_in": np.ascontiguousarray(wq), "wk_in": np.ascontiguousarray(wk),
        "wv_in": np.ascontiguousarray(wv), "bq_in": bq, "bk_in": bk, "bv_in": bv,
        "wo_in": np.ascontiguousarray(wo), "bo_in": bo,
        "wup_in": np.ascontiguousarray(wup), "wdn_in": np.ascontiguousarray(wdn),
        "alpha_in": al,
    }


def kernel(L0_attn_x, L0_attn_qkv_w, L0_attn_qkv_b, L0_attn_out_w, L0_attn_out_b,
           L0_attn_decay_alpha, L0_ffn_w_up, L0_ffn_w_down):
    T, D, H, DFF = 2048, 1024, 16, 4096
    x = np.ascontiguousarray(np.asarray(L0_attn_x, dtype=np.float32))
    qkv_w = np.ascontiguousarray(np.asarray(L0_attn_qkv_w, dtype=np.float32))
    qkv_b = np.asarray(L0_attn_qkv_b, dtype=np.float32)
    out_w = np.ascontiguousarray(np.asarray(L0_attn_out_w, dtype=np.float32))
    out_b = np.asarray(L0_attn_out_b, dtype=np.float32)
    alpha = np.asarray(L0_attn_decay_alpha, dtype=np.float32)
    w_up = np.ascontiguousarray(np.asarray(L0_ffn_w_up, dtype=np.float32))
    w_down = np.ascontiguousarray(np.asarray(L0_ffn_w_down, dtype=np.float32))

    nc = build_program(T=T, D=D, H=H, DFF=DFF)
    in_maps = [
        _prep_core(x[b], qkv_w, qkv_b, out_w, out_b, alpha, w_up, w_down,
                   T=T, D=D, H=H, DFF=DFF)
        for b in range(N_CORES)
    ]
    res = run_bass_kernel_spmd(nc, in_maps, core_ids=list(range(N_CORES)))
    FC = D // 128
    out = np.empty((N_CORES, T, D), np.float32)
    for b in range(N_CORES):
        o = np.asarray(res.results[b]["out_hbm"])
        out[b] = o.reshape(128, FC, T).transpose(1, 0, 2).reshape(D, T).T
    return out
